# version 19
# baseline (speedup 1.0000x reference)
import numpy as np
import ml_dtypes
from contextlib import ExitStack
import concourse.bass as bass
import concourse.mybir as mybir
from concourse.bass_utils import run_bass_kernel_spmd

F32 = mybir.dt.float32
BF16 = mybir.dt.bfloat16
I32 = mybir.dt.int32
AF = mybir.ActivationFunctionType
ALU = mybir.AluOpType
AX = mybir.AxisListType.X

B, S, D = 8, 4096, 2048
NCORES = 8
DC = D // 128
TC = S // 128
BIG_NEG = -1.0e30
MAX_CONSTS = 32

ENG = ("sync", "tensor", "vector", "scalar", "gpsimd")

WEIGHT_SPECS = {
    "L0_attn_qkv": ([2048, 6144], F32),
    "L0_attn_o_proj": ([2048, 2048], F32),
    "L0_attn_gamma": ([2048], F32),
    "L0_attn_beta": ([2048], F32),
    "L0_ffn_W_gate": ([16, 2048], F32),
    "L0_ffn_W_qkv": ([6144, 2048], F32),
    "L1_attn_a_real_param": ([1024], F32),
    "L1_attn_a_imag_param": ([1024], F32),
    "L1_attn_W_in": ([2048, 2048], F32),
    "L1_attn_b_in": ([2048], F32),
    "L1_attn_Wg_x": ([8, 256, 128], F32),
    "L1_attn_bg_x": ([1024], F32),
    "L1_attn_Wg_a": ([8, 256, 128], F32),
    "L1_attn_bg_a": ([1024], F32),
    "L1_attn_W_out": ([2048, 2048], F32),
    "L1_attn_b_out": ([2048], F32),
    "L1_ffn_w_up_q": ([8192, 2048], I32),
    "L1_ffn_w_up_scale": ([8192, 32], F32),
    "L1_ffn_b_up": ([8192], F32),
    "L1_ffn_w_up_lora_a": ([2048, 16], F32),
    "L1_ffn_w_up_lora_b": ([16, 8192], F32),
    "L1_ffn_w_down_q": ([2048, 8192], I32),
    "L1_ffn_w_down_scale": ([2048, 128], F32),
    "L1_ffn_b_down": ([2048], F32),
    "L1_ffn_w_down_lora_a": ([8192, 16], F32),
    "L1_ffn_w_down_lora_b": ([16, 2048], F32),
}

SCRATCH_SPECS = {
    "x_in": ([S, D], F32),
    "out": ([S, D], F32),
    "xres": ([D, S], F32),
    "xn32": ([D, S], F32),
    "xnbf": ([D, S], BF16),
    "xnorm": ([D, S], BF16),
    "q0": ([D, S], BF16),
    "k0": ([D, S], BF16),
    "v0": ([S, D], BF16),
    "att0": ([D, S], BF16),
    "s0": ([D, S], F32),
    "hot": ([S, 16], F32),
    "g1v": ([S, 16], F32),
    "kshF": ([128, S], BF16),
    "vshT": ([S, 128], BF16),
    "wqkT3": ([48, 128, 2048], BF16),
    "y0n": ([D, S], F32),
    "x1": ([D, S], F32),
    "xh": ([D, S], F32),
    "glx": ([1024, S], F32),
    "gla": ([1024, S], F32),
    "hsc": ([D, S], BF16),
    "wupT3": ([64, 128, 2048], BF16),
    "wdnT3": ([16, 128, 8192], BF16),
    "xa": ([16, S], F32),
    "xda": ([16, S], F32),
    "x2": ([8192, S], BF16),
    "st2": ([2, S], F32),
}


class Prog:
    def __init__(self, ext_in=(), ext_out=()):
        self.nc = bass.Bass("TRN2", target_bir_lowering=False)
        self.ext_in = set(ext_in) | {"x_in"} | set(WEIGHT_SPECS)
        self.ext_out = set(ext_out) | {"out"}
        self.drams = {}
        self.ops = {e: [] for e in ENG}
        self.cnt = {}
        self.consts = {}
        self.const_regd = set()
        self.bar = self.new_sem("bar")
        self.esem = {e: self.new_sem("es_" + e) for e in ENG}
        self._sem_pool = [self.new_sem(f"sp{i}") for i in range(24)]
        self._sem_next = 0
        self._since_bar = {e: False for e in ENG}
        self.cpool = self.nc.alloc_sbuf_tensor("cpool", [128, MAX_CONSTS], F32)[:]

    def new_sem(self, nm):
        h = self.nc.alloc_semaphore(nm)
        self.cnt[h] = 0
        return h

    def pool_sem(self):
        h = self._sem_pool[self._sem_next % len(self._sem_pool)]
        self._sem_next += 1
        return h

    def dram(self, nm):
        if nm in self.drams:
            return self.drams[nm]
        shape, dtype = WEIGHT_SPECS.get(nm) or SCRATCH_SPECS[nm]
        kind = ("ExternalInput" if nm in self.ext_in else
                "ExternalOutput" if nm in self.ext_out else "Internal")
        t = self.nc.dram_tensor(nm, list(shape), dtype, kind=kind).ap()
        self.drams[nm] = t
        return t

    def op(self, eng, method, *args, sem=None, **kw):
        is_dma = method.startswith("dma")
        if is_dma and sem is None:
            sem = self.pool_sem()
        incs = []
        ret = None
        if sem is not None:
            n = 16 if is_dma else 1
            self.cnt[sem] += n
            incs.append((sem, n))
            ret = self.cnt[sem]
        self.ops[eng].append(["op", method, args, kw, incs, is_dma])
        self._since_bar[eng] = True
        return ret

    def wait(self, eng, sem, tgt):
        self.ops[eng].append(["wait", sem, tgt])

    def _last_op(self, eng):
        for ent in reversed(self.ops[eng]):
            if ent[0] == "op":
                return ent
        return None

    def _milestone(self, eng, sem):
        ent = self._last_op(eng)
        assert ent is not None
        if not ent[4]:
            n = 16 if ent[5] else 1
            self.cnt[sem] += n
            ent[4].append((sem, n))
            return self.cnt[sem]
        if ent[5]:
            s0, n0 = ent[4][0]
            self.wait(eng, s0, self.cnt[s0])
            self.cnt[sem] += 1
            self.ops[eng].append(["op", "sem_inc", (sem, 1), {}, [], False])
            return self.cnt[sem]
        self.cnt[sem] += 1
        self.ops[eng].append(["op", "drain", (), {}, [(sem, 1)], False])
        return self.cnt[sem]

    def einc(self, eng):
        return self._milestone(eng, self.esem[eng])

    def selfbar(self, eng):
        tgt = self.einc(eng)
        self.wait(eng, self.esem[eng], tgt)

    def barrier(self):
        for e in ENG:
            if not self._since_bar[e]:
                continue
            self._milestone(e, self.bar)
        tgt = self.cnt[self.bar]
        for e in ENG:
            self.wait(e, self.bar, tgt)
        self._since_bar = {e: False for e in ENG}

    def const(self, v):
        v = float(v)
        if v not in self.consts:
            assert len(self.consts) < MAX_CONSTS
            self.consts[v] = len(self.consts)
        return self.cpool[:, self.consts[v]:self.consts[v] + 1]

    def constf(self, v):
        v = float(v)
        ap = self.const(v)
        if v not in self.const_regd:
            self.nc.const_aps.register(F32, v, ap)
            self.const_regd.add(v)
        return v

    def inline(self, name, arr):
        return self.nc.inline_tensor(np.ascontiguousarray(arr), name=name).ap()

    def build(self):
        for h in self._sem_pool:
            if self.cnt[h] > 0:
                self.wait("sync", h, self.cnt[h])
        self.ops["sync"].append(["op", "nop", (), {}, [], False])
        self._since_bar["sync"] = True
        self.barrier()
        vals = np.zeros(MAX_CONSTS, np.float32)
        for v, i in self.consts.items():
            vals[i] = v
        cp_d = self.inline("cpool_d", np.tile(vals, (128, 1)))
        csem = self.new_sem("csem")
        self.ops["sync"].insert(0, ["op", "dma_start", (), {"out": self.cpool, "in_": cp_d},
                                    [(csem, 16)], True])
        for e in ENG:
            if any(k[0] == "op" for k in self.ops[e]):
                self.ops[e].insert(1 if e == "sync" else 0, ["wait", csem, 16])
        with self.nc.Block() as block:
            for e in ENG:
                entries = self.ops[e]
                if not any(k[0] == "op" for k in entries):
                    continue
                def runner(eng, entries=entries):
                    for ent in entries:
                        if ent[0] == "wait":
                            eng.wait_ge(ent[1], ent[2])
                        else:
                            _, m, args, kw, incs, _ = ent
                            inst = getattr(eng, m)(*args, **kw)
                            for (s, n) in incs:
                                if inst is None:
                                    raise RuntimeError(f"op {m} returned None but has incs")
                                inst = inst.then_inc(s, n)
                getattr(block, e)(runner)
        return self.nc

    def n_ops(self):
        return {e: sum(1 for k in self.ops[e] if k[0] == "op") for e in ENG}


def dma_split3(P, out3, in3, sem, ns, eng="sync"):
    ns = 1
    C = out3.shape[1]
    ns = max(1, min(ns, C))
    step = C // ns
    t = None
    for k in range(ns):
        sl = slice(k * step, (k + 1) * step if k < ns - 1 else C)
        t = P.op(eng, "dma_start", out=out3[:, sl, :], in_=in3[:, sl, :], sem=sem)
    return t


def dma_split2(P, out2, in2, sem, ns, eng="sync"):
    ns = 1
    N = out2.shape[-1]
    ns = max(1, min(ns, N))
    step = N // ns
    t = None
    for k in range(ns):
        sl = slice(k * step, (k + 1) * step if k < ns - 1 else N)
        t = P.op(eng, "dma_start", out=out2[:, sl], in_=in2[:, sl], sem=sem)
    return t



def emit_prep(P):
    nc = P.nc
    ctx = {}
    idb = nc.alloc_sbuf_tensor("idb", [128, 128], BF16)[:]
    idf = nc.alloc_sbuf_tensor("idf", [128, 128], F32)[:]
    iota16 = nc.alloc_sbuf_tensor("iota16", [128, 16], F32)[:]
    ones_col = nc.alloc_sbuf_tensor("ones_col", [128, 1], BF16)[:]
    ones_row = nc.alloc_sbuf_tensor("ones_row", [1, 128], BF16)[:]
    ones_row_f = nc.alloc_sbuf_tensor("ones_row_f", [1, 128], F32)[:]
    vecs = nc.alloc_sbuf_tensor("vecs", [128, 192], F32)[:]
    ctx.update(idb=idb, idf=idf, iota16=iota16, ones_col=ones_col, ones_row=ones_row,
               ones_row_f=ones_row_f, vecs=vecs)

    VEC = {"gamma": 0, "beta": 16, "b_in": 32, "b_out": 48, "b_down": 64,
           "bg_x": 80, "bg_a": 88, "sp": 96, "aim": 104, "b_up": 112}
    ctx["VEC"] = VEC
    ctx["vec"] = lambda nm: vecs[:, VEC[nm]:VEC[nm] + 1]

    ld = P.pool_sem()
    P.op("sync", "dma_start", out=idb, in_=P.inline("idb_d", np.eye(128, dtype=np.float32).astype(ml_dtypes.bfloat16)), sem=ld)
    P.op("sync", "dma_start", out=idf, in_=P.inline("idf_d", np.eye(128, dtype=np.float32)), sem=ld)
    P.op("sync", "dma_start", out=iota16, in_=P.inline("iota_d", np.tile(np.arange(16, dtype=np.float32), (128, 1))), sem=ld)
    P.op("sync", "dma_start", out=ones_col, in_=P.inline("ones_col_d", np.ones((128, 1), np.float32).astype(ml_dtypes.bfloat16)), sem=ld)
    P.op("sync", "dma_start", out=ones_row, in_=P.inline("ones_row_d", np.ones((1, 128), np.float32).astype(ml_dtypes.bfloat16)), sem=ld)
    P.op("sync", "dma_start", out=ones_row_f, in_=P.inline("ones_row_f_d", np.ones((1, 128), np.float32)), sem=ld)

    vjobs = [
        ("L0_attn_gamma", 2048, "gamma"), ("L0_attn_beta", 2048, "beta"),
        ("L1_attn_b_in", 2048, "b_in"), ("L1_attn_b_out", 2048, "b_out"),
        ("L1_ffn_b_down", 2048, "b_down"), ("L1_attn_bg_x", 1024, "bg_x"),
        ("L1_attn_bg_a", 1024, "bg_a"), ("L1_attn_a_real_param", 1024, "sp"),
        ("L1_attn_a_imag_param", 1024, "aim"), ("L1_ffn_b_up", 8192, "b_up"),
    ]
    with ExitStack() as st:
        vload = st.enter_context(nc.sbuf_tensor("prep_vload", [64, 10 * 128], F32))[:]
        t1 = st.enter_context(nc.sbuf_tensor("prep_t1", [128, 24], F32))[:]
        ps = st.enter_context(nc.psum_tensor("prep_ps", [128, 176], F32))[:]
        off = 0
        for nm, n, dst in vjobs:
            r = n // 128
            src = P.dram(nm).rearrange("(a b) -> a b", b=128)
            tld = P.op("sync", "dma_start", out=vload[0:r, off:off + 128], in_=src, sem=ld)
            off += 128
        P.wait("tensor", ld, tld)
        off, pcol = 0, 0
        cols = {}
        for nm, n, dst in vjobs:
            r = n // 128
            P.op("tensor", "transpose", ps[:, pcol:pcol + r], vload[0:r, off:off + 128],
                 idf[0:r, 0:r])
            cols[dst] = (pcol, r)
            off += 128
            pcol += r
        ttgt = P.einc("tensor")
        P.wait("scalar", P.esem["tensor"], ttgt)
        for nm, n, dst in vjobs:
            pcol, r = cols[dst]
            P.op("scalar", "copy", out=vecs[:, VEC[dst]:VEC[dst] + r], in_=ps[:, pcol:pcol + r])
        sp = vecs[:, VEC["sp"]:VEC["sp"] + 8]
        P.selfbar("scalar")
        P.op("scalar", "activation", out=t1[:, 0:8], in_=sp, func=AF.Abs)
        P.selfbar("scalar")
        P.op("scalar", "activation", out=t1[:, 8:16], in_=t1[:, 0:8], func=AF.Exp, scale=-1.0)
        P.selfbar("scalar")
        stgt = P.op("scalar", "activation", out=t1[:, 0:8], in_=t1[:, 8:16], func=AF.Ln,
                    bias=1.0, sem=P.esem["scalar"])
        P.wait("vector", P.esem["scalar"], stgt)
        P.op("vector", "tensor_scalar_max", out=t1[:, 16:24], in0=sp, scalar1=0.0)
        P.selfbar("vector")
        P.op("vector", "tensor_tensor", out=sp, in0=t1[:, 0:8], in1=t1[:, 16:24], op=ALU.add)
        P.barrier()
    return ctx


def issue_p0_ld(P, i, xt, x_in, ld, rec, ldt0):
    if i >= 2:
        P.wait("sync", P.esem["vector"], rec[i - 2]["v_ev"])
        P.wait("sync", P.esem["scalar"], rec[i - 2]["s_evb"])
    ldt0[i] = dma_split2(P, xt[i % 2], x_in[i * 128:(i + 1) * 128, :], ld[i % 2], 4)



def emit_p0(P, ctx):
    nc = P.nc
    x_in = P.dram("x_in")
    xres = P.dram("xres").rearrange("(c p) t -> p c t", p=128)
    xn32 = P.dram("xn32").rearrange("(c p) t -> p c t", p=128)
    xnbf = P.dram("xnbf").rearrange("(c p) t -> p c t", p=128)
    st2 = P.dram("st2")
    idf = ctx["idf"]
    inv_d = P.const(1.0 / D)
    eps = P.const(1e-5)
    with ExitStack() as st:
        xt = [st.enter_context(nc.sbuf_tensor(f"p0_x{i}", [128, D], F32))[:] for i in range(2)]
        xn = [st.enter_context(nc.sbuf_tensor(f"p0_n{i}", [128, D], F32))[:] for i in range(2)]
        sq = st.enter_context(nc.sbuf_tensor("p0_sq", [128, D], F32))[:]
        stats = [st.enter_context(nc.sbuf_tensor(f"p0_st{i}", [128, 16], F32))[:] for i in range(2)]
        st2sb = [st.enter_context(nc.sbuf_tensor(f"p0_s2{i}", [2, 128], F32))[:] for i in range(2)]
        ev = [st.enter_context(nc.sbuf_tensor(f"p0_e{i}", [128, D], F32))[:] for i in range(2)]
        evn = [st.enter_context(nc.sbuf_tensor(f"p0_f{i}", [128, D], F32))[:] for i in range(2)]
        evb = [st.enter_context(nc.sbuf_tensor(f"p0_g{i}", [128, D], BF16))[:] for i in range(2)]
        psx = st.enter_context(nc.psum_tensor("p0_psx", [128, D], F32))[:]
        psn = st.enter_context(nc.psum_tensor("p0_psn", [128, D], F32))[:]
        ld = [P.pool_sem(), P.pool_sem()]
        stq = [P.pool_sem(), P.pool_sem()]
        rec = {}
        ldt0 = {}
        for i in range(TC):
            b = i % 2
            r = {}
            if i == 0:
                issue_p0_ld(P, 0, xt, x_in, ld, rec, ldt0)
            if i + 1 < TC:
                issue_p0_ld(P, i + 1, xt, x_in, ld, rec, ldt0)
            r["ld"] = ldt0[i]
            P.wait("vector", ld[b], r["ld"])
            P.op("vector", "reduce_sum", out=stats[b][:, 0:1], in_=xt[b], axis=AX)
            P.wait("scalar", ld[b], r["ld"])
            r["s_sq"] = P.op("scalar", "activation", out=sq, in_=xt[b], func=AF.Square,
                             accum_out=stats[b][:, 1:2], sem=P.esem["scalar"])
            P.wait("vector", P.esem["scalar"], r["s_sq"])
            P.selfbar("vector")
            P.op("vector", "tensor_single_scalar", out=stats[b][:, 2:3], in_=stats[b][:, 0:1],
                 scalar=inv_d, op=ALU.mult)
            P.op("vector", "tensor_single_scalar", out=stats[b][:, 3:4], in_=stats[b][:, 1:2],
                 scalar=inv_d, op=ALU.mult)
            P.selfbar("vector")
            P.op("vector", "tensor_tensor", out=stats[b][:, 4:5], in0=stats[b][:, 2:3],
                 in1=stats[b][:, 2:3], op=ALU.mult)
            P.selfbar("vector")
            r["v_var"] = P.op("vector", "tensor_tensor", out=stats[b][:, 5:6], in0=stats[b][:, 3:4],
                              in1=stats[b][:, 4:5], op=ALU.subtract, sem=P.esem["vector"])
            P.wait("scalar", P.esem["vector"], r["v_var"])
            r["s_std"] = P.op("scalar", "activation", out=stats[b][:, 6:7], in_=stats[b][:, 5:6],
                              func=AF.Sqrt, bias=eps, sem=P.esem["scalar"])
            P.wait("vector", P.esem["scalar"], r["s_std"])
            P.op("vector", "reciprocal", out=stats[b][:, 7:8], in_=stats[b][:, 6:7])
            P.selfbar("vector")
            if i >= 2:
                P.wait("vector", P.esem["tensor"], rec[i - 2]["t_all"])
            r["v_xn"] = P.op("vector", "tensor_scalar", out=xn[b], in0=xt[b],
                             scalar1=stats[b][:, 2:3], scalar2=stats[b][:, 7:8],
                             op0=ALU.subtract, op1=ALU.mult, sem=P.esem["vector"])
            P.op("vector", "reduce_sum", out=stats[b][:, 8:9], in_=xn[b], axis=AX)
            P.wait("scalar", P.esem["vector"], r["v_xn"])
            r["s_sq2"] = P.op("scalar", "activation", out=sq, in_=xn[b], func=AF.Square,
                              accum_out=stats[b][:, 9:10], sem=P.esem["scalar"])
            P.wait("vector", P.esem["scalar"], r["s_sq2"])
            P.selfbar("vector")
            P.op("vector", "tensor_single_scalar", out=stats[b][:, 14:15], in_=stats[b][:, 8:9],
                 scalar=inv_d, op=ALU.mult)
            P.op("vector", "tensor_single_scalar", out=stats[b][:, 10:11], in_=stats[b][:, 9:10],
                 scalar=inv_d, op=ALU.mult)
            P.selfbar("vector")
            P.op("vector", "tensor_tensor", out=stats[b][:, 11:12], in0=stats[b][:, 14:15],
                 in1=stats[b][:, 14:15], op=ALU.mult)
            P.selfbar("vector")
            r["v_var2"] = P.op("vector", "tensor_tensor", out=stats[b][:, 12:13], in0=stats[b][:, 10:11],
                               in1=stats[b][:, 11:12], op=ALU.subtract, sem=P.esem["vector"])
            P.wait("scalar", P.esem["vector"], r["v_var2"])
            r["s_std2"] = P.op("scalar", "activation", out=stats[b][:, 13:14], in_=stats[b][:, 12:13],
                               func=AF.Sqrt, bias=eps, sem=P.esem["scalar"])
            P.wait("vector", P.esem["scalar"], r["s_std2"])
            P.op("vector", "reciprocal", out=stats[b][:, 15:16], in_=stats[b][:, 13:14],
                 sem=P.esem["vector"])
            r["v_st2"] = P.cnt[P.esem["vector"]]
            P.wait("tensor", ld[b], r["ld"])
            if i >= 1:
                P.wait("tensor", P.esem["vector"], rec[i - 1]["v_ev"])
                P.wait("tensor", P.esem["scalar"], rec[i - 1]["s_st2"])
            for c in range(DC):
                P.op("tensor", "transpose", psx[:, c * 128:(c + 1) * 128],
                     xt[b][:, c * 128:(c + 1) * 128], idf)
            r["t_x"] = P.einc("tensor")
            P.wait("tensor", P.esem["vector"], r["v_xn"])
            for c in range(DC):
                P.op("tensor", "transpose", psn[:, c * 128:(c + 1) * 128],
                     xn[b][:, c * 128:(c + 1) * 128], idf)
            r["t_all"] = P.einc("tensor")
            P.wait("vector", P.esem["tensor"], r["t_x"])
            if i >= 2:
                P.wait("vector", stq[b], rec[i - 2]["st"])
            r["v_ev"] = P.op("vector", "tensor_copy", out=ev[b], in_=psx, sem=P.esem["vector"])
            P.wait("scalar", P.esem["tensor"], r["t_all"])
            if i >= 2:
                P.wait("scalar", stq[b], rec[i - 2]["st"])
            r["s_evn"] = P.op("scalar", "copy", out=evn[b], in_=psn, sem=P.esem["scalar"])
            P.selfbar("scalar")
            r["s_evb"] = P.op("scalar", "copy", out=evb[b], in_=evn[b], sem=P.esem["scalar"])
            P.wait("tensor", P.esem["scalar"], P.cnt[P.esem["scalar"]])
            P.wait("tensor", P.esem["vector"], r["v_st2"])
            r["t_st2"] = P.op("tensor", "transpose", psn[0:2, 0:128], stats[b][:, 14:16],
                              idf, sem=P.esem["tensor"])
            P.wait("scalar", P.esem["tensor"], r["t_st2"])
            if i >= 2:
                P.wait("scalar", stq[b], rec[i - 2]["st"])
            r["s_st2"] = P.op("scalar", "copy", out=st2sb[b], in_=psn[0:2, 0:128],
                              sem=P.esem["scalar"])
            P.wait("sync", P.esem["vector"], r["v_ev"])
            P.wait("sync", P.esem["scalar"], r["s_st2"])
            view = lambda t: t.rearrange("p (c n) -> p c n", c=DC)
            dma_split3(P, xres[:, :, i * 128:(i + 1) * 128], view(ev[b]), stq[b], 4)
            dma_split3(P, xn32[:, :, i * 128:(i + 1) * 128], view(evn[b]), stq[b], 4)
            dma_split3(P, xnbf[:, :, i * 128:(i + 1) * 128], view(evb[b]), stq[b], 2)
            r["st"] = P.op("sync", "dma_start", out=st2[:, i * 128:(i + 1) * 128], in_=st2sb[b], sem=stq[b])
            rec[i] = r
        P.barrier()



def emit_p1(P, ctx):
    nc = P.nc
    xn32 = P.dram("xn32").rearrange("(c p) t -> p c t", p=128)
    st2 = P.dram("st2")
    xnorm = P.dram("xnorm").rearrange("(c p) t -> p c t", p=128)
    VEC = ctx["VEC"]
    vecs = ctx["vecs"]
    onesf = ctx["ones_row_f"]
    TT = 512
    NT = S // TT
    with ExitStack() as st:
        xt = [st.enter_context(nc.sbuf_tensor(f"p1_x{i}", [128, DC * TT], F32))[:] for i in range(2)]
        mu = [st.enter_context(nc.sbuf_tensor(f"p1_m{i}", [1, TT], F32))[:] for i in range(2)]
        rs = [st.enter_context(nc.sbuf_tensor(f"p1_r{i}", [1, TT], F32))[:] for i in range(2)]
        bc = [st.enter_context(nc.sbuf_tensor(f"p1_b{i}", [128, 2 * TT], F32))[:] for i in range(2)]
        tmp8 = st.enter_context(nc.sbuf_tensor("p1_t8", [128, DC * TT], F32))[:]
        ob = [st.enter_context(nc.sbuf_tensor(f"p1_o{i}", [128, DC * TT], BF16))[:] for i in range(2)]
        psb = st.enter_context(nc.psum_tensor("p1_ps", [128, 2 * TT], F32))[:]
        ld = [P.pool_sem(), P.pool_sem()]
        stq = [P.pool_sem(), P.pool_sem()]
        rec = {}
        ldt = {}

        def issue_ld(i):
            if i >= 2:
                P.wait("sync", P.esem["vector"], rec[i - 2]["v_done"])
            dma_split3(P, xt[i % 2].rearrange("p (c n) -> p c n", c=DC),
                       xn32[:, :, i * TT:(i + 1) * TT], ld[i % 2], 8)
            P.op("sync", "dma_start", out=mu[i % 2], in_=st2[0:1, i * TT:(i + 1) * TT], sem=ld[i % 2])
            ldt[i] = P.op("sync", "dma_start", out=rs[i % 2], in_=st2[1:2, i * TT:(i + 1) * TT], sem=ld[i % 2])

        issue_ld(0)
        for i in range(NT):
            b = i % 2
            r = {}
            if i + 1 < NT:
                issue_ld(i + 1)
            r["ld"] = ldt[i]
            P.wait("tensor", ld[b], r["ld"])
            if i >= 1:
                P.wait("tensor", P.esem["vector"], rec[i - 1]["v_bc"])
            P.op("tensor", "matmul", psb[:, 0:TT], onesf, mu[b], start=True, stop=True)
            r["t_bc"] = P.op("tensor", "matmul", psb[:, TT:2 * TT], onesf, rs[b],
                             start=True, stop=True, sem=P.esem["tensor"])
            P.wait("vector", P.esem["tensor"], r["t_bc"])
            P.op("vector", "tensor_copy", out=bc[b][:, 0:TT], in_=psb[:, 0:TT])
            r["v_bc"] = P.op("vector", "tensor_copy", out=bc[b][:, TT:2 * TT], in_=psb[:, TT:2 * TT],
                             sem=P.esem["vector"])
            if i >= 2:
                P.wait("vector", stq[b], rec[i - 2]["st"])
            P.selfbar("vector")
            P.op("vector", "tensor_tensor", out=tmp8.rearrange("p (c n) -> p c n", c=DC),
                 in0=xt[b].rearrange("p (c n) -> p c n", c=DC),
                 in1=bc[b][:, 0:TT].unsqueeze(1).broadcast_to([128, DC, TT]),
                 op=ALU.subtract)
            P.selfbar("vector")
            P.op("vector", "tensor_tensor", out=tmp8.rearrange("p (c n) -> p c n", c=DC),
                 in0=tmp8.rearrange("p (c n) -> p c n", c=DC),
                 in1=bc[b][:, TT:2 * TT].unsqueeze(1).broadcast_to([128, DC, TT]),
                 op=ALU.mult)
            P.selfbar("vector")
            for c in range(DC):
                P.op("vector", "tensor_scalar", out=ob[b][:, c * TT:(c + 1) * TT],
                     in0=tmp8[:, c * TT:(c + 1) * TT],
                     scalar1=vecs[:, VEC["gamma"] + c:VEC["gamma"] + c + 1],
                     scalar2=vecs[:, VEC["beta"] + c:VEC["beta"] + c + 1],
                     op0=ALU.mult, op1=ALU.add)
            P.selfbar("vector")
            r["v_done"] = P.cnt[P.esem["vector"]]
            P.wait("sync", P.esem["vector"], r["v_done"])
            r["st"] = dma_split3(P, xnorm[:, :, i * TT:(i + 1) * TT],
                                 ob[b].rearrange("p (c n) -> p c n", c=DC), stq[b], 4)
            rec[i] = r
        P.barrier()



def emit_proj_fm(P, ctx, w_name, x_name, out_name, K, M, w_col0=0, w_row0=0,
                 scale=None, bias_base=None, resid_name=None, out_f32=False,
                 tag="pj", w_chunked=False):
    nc = P.nc
    KC_, MC = K // 128, M // 128
    x_d = P.dram(x_name).rearrange("(c p) t -> p c t", p=128)
    out_d = P.dram(out_name).rearrange("(c p) t -> p c t", p=128)
    resid_d = P.dram(resid_name).rearrange("(c p) t -> p c t", p=128) if resid_name else None
    vecs = ctx["vecs"]
    TT = 512
    NT = S // TT
    odt = F32 if (out_f32 or resid_name) else BF16
    NPS = 4
    with ExitStack() as st:
        wb = st.enter_context(nc.sbuf_tensor(f"{tag}_wb", [128, KC_ * M], BF16))[:]
        xt = [st.enter_context(nc.sbuf_tensor(f"{tag}_x{i}", [128, KC_ * TT], BF16))[:] for i in range(2)]
        ot = [st.enter_context(nc.sbuf_tensor(f"{tag}_o{i}", [128, MC * TT], odt))[:] for i in range(2)]
        rt = [st.enter_context(nc.sbuf_tensor(f"{tag}_r{i}", [128, TT], F32))[:] for i in range(NPS)] if resid_name else None
        ps = [st.enter_context(nc.psum_tensor(f"{tag}_p{i}", [128, TT], F32))[:] for i in range(NPS)]
        wld = P.pool_sem()
        ld = [P.pool_sem(), P.pool_sem()]
        stq = [P.pool_sem(), P.pool_sem()]
        rld = [P.pool_sem() for _ in range(NPS)]
        rld_t = {}
        if w_chunked:
            w_d = P.dram(w_name)
            K = KC_ * 128
            for mc in range(MC):
                dma_split2(P, wb[:, mc * K:(mc + 1) * K], w_d[w_col0 // 128 + mc], wld, 2)
            P.wait("tensor", wld, P.cnt[wld])
            wcol = lambda kc, mc: wb[:, mc * K + kc * 128: mc * K + (kc + 1) * 128]
        else:
            w_d = P.dram(w_name)
            wst = [st.enter_context(nc.sbuf_tensor(f"{tag}_ws{i}", [128, M], F32))[:] for i in range(2)]
            wld2 = [P.pool_sem(), P.pool_sem()]
            wcv = {}
            for kc in range(KC_):
                bk = kc % 2
                if kc >= 2:
                    P.wait("sync", P.esem["vector"], wcv[kc - 2])
                t = dma_split2(P, wst[bk],
                               w_d[w_row0 + kc * 128: w_row0 + (kc + 1) * 128, w_col0:w_col0 + M],
                               wld2[bk], 4)
                P.wait("vector", wld2[bk], t)
                wcv[kc] = P.op("vector", "tensor_copy", out=wb[:, kc * M:(kc + 1) * M],
                               in_=wst[bk], sem=P.esem["vector"])
            P.wait("tensor", P.esem["vector"], P.cnt[P.esem["vector"]])
            wcol = lambda kc, mc: wb[:, kc * M + mc * 128: kc * M + (mc + 1) * 128]
        rec = {}
        ev = {}
        ldt = {}

        def issue_ld(i):
            b = i % 2
            if i >= 2:
                P.wait("sync", P.esem["tensor"], rec[i - 2]["t_done"])
                if resid_name:
                    P.wait("sync", P.esem["vector"], rec[i - 2]["v_done"])
            ldt[i] = dma_split3(P, xt[b].rearrange("p (c n) -> p c n", c=KC_),
                                x_d[:, :, i * TT:(i + 1) * TT], ld[b], 4)

        issue_ld(0)
        for i in range(NT):
            b = i % 2
            r = {}
            if i + 1 < NT:
                issue_ld(i + 1)
            r["ld"] = ldt[i]
            P.wait("tensor", ld[b], r["ld"])
            first_ev = True
            for mc in range(MC):
                j = i * MC + mc
                pb = ps[j % NPS]
                if resid_name:
                    if j >= NPS:
                        P.wait("sync", P.esem["vector"], ev[j - NPS])
                    rld_t[j] = P.op("sync", "dma_start", out=rt[j % NPS],
                                    in_=resid_d[:, mc, i * TT:(i + 1) * TT], sem=rld[j % NPS])
                if j >= NPS:
                    P.wait("tensor", P.esem["vector"], ev[j - NPS])
                for kc in range(KC_):
                    P.op("tensor", "matmul", pb, wcol(kc, mc), xt[b][:, kc * TT:(kc + 1) * TT],
                         start=(kc == 0), stop=(kc == KC_ - 1))
                mtgt = P.einc("tensor")
                r["t_done"] = mtgt
                P.wait("vector", P.esem["tensor"], mtgt)
                if first_ev:
                    if i >= 2:
                        P.wait("vector", stq[b], rec[i - 2]["st"])
                    first_ev = False
                oc = ot[b][:, mc * TT:(mc + 1) * TT]
                if resid_name:
                    P.wait("vector", rld[j % NPS], rld_t[j])
                    P.op("vector", "tensor_tensor", out=oc, in0=pb,
                         in1=rt[j % NPS], op=ALU.add, sem=P.esem["vector"])
                elif bias_base is not None and scale is not None:
                    P.op("vector", "tensor_scalar", out=oc, in0=pb, scalar1=P.const(scale),
                         scalar2=vecs[:, bias_base + mc:bias_base + mc + 1],
                         op0=ALU.mult, op1=ALU.add, sem=P.esem["vector"])
                elif bias_base is not None:
                    P.op("vector", "tensor_single_scalar", out=oc, in_=pb,
                         scalar=vecs[:, bias_base + mc:bias_base + mc + 1], op=ALU.add,
                         sem=P.esem["vector"])
                elif scale is not None:
                    P.op("vector", "tensor_single_scalar", out=oc, in_=pb,
                         scalar=P.const(scale), op=ALU.mult, sem=P.esem["vector"])
                else:
                    P.op("vector", "tensor_copy", out=oc, in_=pb, sem=P.esem["vector"])
                ev[j] = P.cnt[P.esem["vector"]]
            r["v_done"] = P.cnt[P.esem["vector"]]
            P.wait("sync", P.esem["vector"], r["v_done"])
            r["st"] = dma_split3(P, out_d[:, :, i * TT:(i + 1) * TT],
                                 ot[b].rearrange("p (c n) -> p c n", c=MC), stq[b], 4)
            rec[i] = r
        P.barrier()


SC_Q0 = 1.0 / float(np.sqrt(2048.0))
SC_QMQ = 1.0 / float(np.sqrt(128.0))


def emit_p2q(P, ctx):
    emit_proj_fm(P, ctx, "L0_attn_qkv", "xnorm", "q0", 2048, 2048, w_col0=0,
                 scale=SC_Q0, tag="p2q")


def emit_p2k(P, ctx):
    emit_proj_fm(P, ctx, "L0_attn_qkv", "xnorm", "k0", 2048, 2048, w_col0=2048, tag="p2k")


def emit_proj_tm(P, ctx, w_name, x_name, out_name, K, M, w_col0=0, tag="pt"):
    nc = P.nc
    KC_, NC2 = K // 128, M // 512
    x_d = P.dram(x_name).rearrange("(c p) t -> p c t", p=128)
    out_d = P.dram(out_name)
    TT = 512
    NT = S // TT
    with ExitStack() as st:
        wb = st.enter_context(nc.sbuf_tensor(f"{tag}_wb", [128, KC_ * M], BF16))[:]
        wst = [st.enter_context(nc.sbuf_tensor(f"{tag}_ws{i}", [128, M], F32))[:] for i in range(2)]
        xt = [st.enter_context(nc.sbuf_tensor(f"{tag}_x{i}", [128, KC_ * TT], BF16))[:] for i in range(2)]
        ot = [st.enter_context(nc.sbuf_tensor(f"{tag}_o{i}", [128, 4 * M], BF16))[:] for i in range(2)]
        ps = [st.enter_context(nc.psum_tensor(f"{tag}_p{i}", [128, TT], F32))[:] for i in range(4)]
        wld2 = [P.pool_sem(), P.pool_sem()]
        ld = [P.pool_sem(), P.pool_sem()]
        stq = [P.pool_sem(), P.pool_sem()]
        w_d = P.dram(w_name)
        wcv = {}
        for kc in range(KC_):
            bk = kc % 2
            if kc >= 2:
                P.wait("sync", P.esem["vector"], wcv[kc - 2])
            t = dma_split2(P, wst[bk], w_d[kc * 128:(kc + 1) * 128, w_col0:w_col0 + M],
                           wld2[bk], 4)
            P.wait("vector", wld2[bk], t)
            wcv[kc] = P.op("vector", "tensor_copy", out=wb[:, kc * M:(kc + 1) * M],
                           in_=wst[bk], sem=P.esem["vector"])
        P.wait("tensor", P.esem["vector"], P.cnt[P.esem["vector"]])
        rec = {}
        ev = {}
        ldt = {}

        def issue_ld(i):
            if i >= 2:
                P.wait("sync", P.esem["tensor"], rec[i - 2]["t_done"])
            ldt[i] = dma_split3(P, xt[i % 2].rearrange("p (c n) -> p c n", c=KC_),
                                x_d[:, :, i * TT:(i + 1) * TT], ld[i % 2], 4)

        issue_ld(0)
        for i in range(NT):
            b = i % 2
            r = {}
            if i + 1 < NT:
                issue_ld(i + 1)
            r["ld"] = ldt[i]
            P.wait("tensor", ld[b], r["ld"])
            first_ev = True
            for tc in range(4):
                for nc2 in range(NC2):
                    j = (i * 4 + tc) * NC2 + nc2
                    pb = ps[j % 4]
                    if j >= 4:
                        P.wait("tensor", P.esem["vector"], ev[j - 4])
                    for kc in range(KC_):
                        P.op("tensor", "matmul", pb,
                             xt[b][:, kc * TT + tc * 128: kc * TT + (tc + 1) * 128],
                             wb[:, kc * M + nc2 * 512: kc * M + (nc2 + 1) * 512],
                             start=(kc == 0), stop=(kc == KC_ - 1))
                    mtgt = P.einc("tensor")
                    r["t_done"] = mtgt
                    P.wait("vector", P.esem["tensor"], mtgt)
                    if first_ev:
                        if i >= 2:
                            P.wait("vector", stq[b], rec[i - 2]["st"])
                        first_ev = False
                    P.op("vector", "tensor_copy",
                         out=ot[b][:, tc * M + nc2 * 512: tc * M + (nc2 + 1) * 512],
                         in_=pb, sem=P.esem["vector"])
                    ev[j] = P.cnt[P.esem["vector"]]
            r["v_done"] = P.cnt[P.esem["vector"]]
            P.wait("sync", P.esem["vector"], r["v_done"])
            r["st"] = dma_split3(P, out_d[i * TT:(i + 1) * TT, :].rearrange("(c p) n -> p c n", p=128),
                                 ot[b].rearrange("p (c n) -> p c n", c=4), stq[b], 4)
            rec[i] = r
        P.barrier()


def emit_p2v(P, ctx):
    emit_proj_tm(P, ctx, "L0_attn_qkv", "xnorm", "v0", 2048, 2048, w_col0=4096, tag="p2v")


def emit_p4(P, ctx):
    emit_proj_fm(P, ctx, "L0_attn_o_proj", "att0", "s0", 2048, 2048,
                 resid_name="xn32", tag="p4")



def emit_p3(P, ctx):
    nc = P.nc
    q_d = P.dram("q0").rearrange("(c p) t -> p c t", p=128)
    k_d = P.dram("k0").rearrange("(c p) t -> p c t", p=128)
    v_d = P.dram("v0")
    o_d = P.dram("att0").rearrange("(c p) t -> p c t", p=128)
    idb = ctx["idb"]
    ES_T, ES_V, ES_S = P.esem["tensor"], P.esem["vector"], P.esem["scalar"]
    NQ = S // 512
    with ExitStack() as st:
        maskA = st.enter_context(nc.sbuf_tensor("p3_maskA", [128, 4 * 512], F32))[:]
        qt_ = [st.enter_context(nc.sbuf_tensor(f"p3_q{i}", [128, DC * 512], BF16))[:] for i in range(2)]
        kt = [st.enter_context(nc.sbuf_tensor(f"p3_k{i}", [128, DC * 512], BF16))[:] for i in range(2)]
        vt = [st.enter_context(nc.sbuf_tensor(f"p3_v{i}", [128, 4 * 2048], BF16))[:] for i in range(2)]
        Osb = [st.enter_context(nc.sbuf_tensor(f"p3_O{i}", [128, 2048], F32))[:] for i in range(4)]
        Ssb = [st.enter_context(nc.sbuf_tensor(f"p3_S{i}", [128, 512], F32))[:] for i in range(2)]
        PTb = [st.enter_context(nc.sbuf_tensor(f"p3_P{i}", [128, 512], BF16))[:] for i in range(2)]
        PTs = [st.enter_context(nc.sbuf_tensor(f"p3_T{i}", [128, 512], BF16))[:] for i in range(2)]
        stat = st.enter_context(nc.sbuf_tensor("p3_st", [128, 16], F32))[:]
        onb = st.enter_context(nc.sbuf_tensor("p3_onb", [128, 2048], BF16))[:]
        osb = st.enter_context(nc.sbuf_tensor("p3_osb", [128, 2048], BF16))[:]
        psS = st.enter_context(nc.psum_tensor("p3_psS", [128, 1024], F32))[:]
        psT = st.enter_context(nc.psum_tensor("p3_psT", [128, 512], BF16))[:]
        psO = st.enter_context(nc.psum_tensor("p3_psO", [128, 2048], F32))[:]
        qld = [P.pool_sem(), P.pool_sem()]
        kld = [P.pool_sem(), P.pool_sem()]
        stq = P.pool_sem()
        P.op("gpsimd", "memset", maskA, 0.0)
        for i_ in range(4):
            P.op("gpsimd", "affine_select", out=maskA[:, i_ * 512:(i_ + 1) * 512],
                 in_=maskA[:, i_ * 512:(i_ + 1) * 512], compare_op=ALU.is_ge,
                 fill=BIG_NEG, base=128 * i_, pattern=[[-1, 512]],
                 channel_multiplier=1, sem=P.esem["gpsimd"])
        P.wait("vector", P.esem["gpsimd"], P.cnt[P.esem["gpsimd"]])
        units = []
        for qt in range(NQ):
            for kb in range(qt + 1):
                for qs in range(4):
                    units.append((qt, kb, qs, kb == qt, kb == 0))
        NU = len(units)
        s_mm, exp_done, mask_v, tr_done, cp_done, pv_done, vadd_done = {}, {}, {}, {}, {}, {}, {}
        den_v = {}
        qrec, q_loaded, kv_loaded = {}, {}, {}
        kvconsumed = {}

        def load_q(qt):
            if qt >= 2:
                P.wait("sync", ES_T, qrec[qt - 2])
            q_loaded[qt] = dma_split3(P, qt_[qt % 2].rearrange("p (c n) -> p c n", c=DC),
                                      q_d[:, :, qt * 512:(qt + 1) * 512], qld[qt % 2], 4)

        def load_kv(qt, kb):
            bk = kb % 2
            prev = kvconsumed.get(bk)
            if prev is not None:
                P.wait("sync", ES_T, prev)
            dma_split3(P, kt[bk].rearrange("p (c n) -> p c n", c=DC),
                       k_d[:, :, kb * 512:(kb + 1) * 512], kld[bk], 4)
            kv_loaded[kb] = dma_split3(P, vt[bk].rearrange("p (c n) -> p c n", c=4),
                                       v_d[kb * 512:(kb + 1) * 512, :]
                                       .rearrange("(c p) n -> p c n", p=128), kld[bk], 4)

        def emit_trcp(i):
            un = units[i]
            P.wait("tensor", ES_S, exp_done[i])
            if i >= 2:
                P.wait("tensor", ES_S, cp_done[i - 2])
            for c4 in range(4):
                P.op("tensor", "transpose", psT[:, c4 * 128:(c4 + 1) * 128],
                     PTb[i % 2][:, c4 * 128:(c4 + 1) * 128], idb)
            tr_done[i] = P.einc("tensor")
            P.wait("scalar", ES_T, tr_done[i])
            if i >= 2:
                P.wait("scalar", ES_T, pv_done.get(i - 2, 0))
            cp_done[i] = P.op("scalar", "copy", out=PTs[i % 2], in_=psT, sem=ES_S)

        def emit_pv(i):
            un = units[i]
            qt, kb, qs, diag, first = un
            bk = kb % 2
            P.wait("tensor", ES_S, cp_done[i])
            if i >= 1:
                P.wait("tensor", ES_V, vadd_done.get(i - 1, 0))
            for nc4 in range(4):
                for kc4 in range(4):
                    P.op("tensor", "matmul", psO[:, nc4 * 512:(nc4 + 1) * 512],
                         PTs[i % 2][:, kc4 * 128:(kc4 + 1) * 128],
                         vt[bk][:, kc4 * 2048 + nc4 * 512: kc4 * 2048 + (nc4 + 1) * 512],
                         start=(kc4 == 0), stop=(kc4 == 3))
            pv_done[i] = P.einc("tensor")
            if qs == 3:
                kvconsumed[bk] = pv_done[i]
            P.wait("vector", ES_T, pv_done[i])
            if first:
                P.op("vector", "tensor_copy", out=Osb[qs], in_=psO)
            else:
                P.op("vector", "tensor_tensor", out=Osb[qs], in0=Osb[qs], in1=psO,
                     op=ALU.add)
            P.wait("vector", ES_S, exp_done[i])
            if first:
                vadd_done[i] = P.op("vector", "tensor_copy", out=stat[:, 2 * qs:2 * qs + 1],
                                    in_=stat[:, 8 + i % 2:9 + i % 2], sem=ES_V)
            else:
                vadd_done[i] = P.op("vector", "tensor_tensor", out=stat[:, 2 * qs:2 * qs + 1],
                                    in0=stat[:, 2 * qs:2 * qs + 1],
                                    in1=stat[:, 8 + i % 2:9 + i % 2], op=ALU.add, sem=ES_V)
            den_v[i] = vadd_done[i]

        def finalize(qt):
            for qs in range(4):
                P.selfbar("vector")
                P.op("vector", "reciprocal", out=stat[:, 2 * qs + 1:2 * qs + 2],
                     in_=stat[:, 2 * qs:2 * qs + 1])
                P.selfbar("vector")
                nrm = P.op("vector", "tensor_single_scalar", out=onb, in_=Osb[qs],
                           scalar=stat[:, 2 * qs + 1:2 * qs + 2], op=ALU.mult, sem=ES_V)
                for r4 in range(4):
                    P.wait("tensor", ES_V, nrm)
                    P.wait("tensor", ES_S, P.cnt[ES_S])
                    for c4 in range(4):
                        P.op("tensor", "transpose", psT[:, c4 * 128:(c4 + 1) * 128],
                             onb[:, (r4 * 4 + c4) * 128:(r4 * 4 + c4 + 1) * 128], idb)
                    ttr = P.einc("tensor")
                    P.wait("scalar", ES_T, ttr)
                    P.op("scalar", "copy", out=osb[:, r4 * 512:(r4 + 1) * 512], in_=psT, sem=ES_S)
                sc = P.cnt[ES_S]
                P.wait("sync", ES_S, sc)
                std = P.op("sync", "dma_start",
                           out=o_d[:, :, qt * 512 + qs * 128: qt * 512 + (qs + 1) * 128],
                           in_=osb.rearrange("p (c n) -> p c n", c=DC), sem=stq)
                P.wait("vector", ES_S, sc)
                P.wait("scalar", stq, std)

        load_q(0)
        load_kv(0, 0)
        next_tr = 0
        next_pv = 0
        for i in range(NU):
            qt, kb, qs, diag, first = units[i]
            if qs == 0 and kb == 0 and qt + 1 < NQ:
                load_q(qt + 1)
            if i >= 2:
                P.wait("tensor", ES_S, exp_done[i - 2])
            if qs == 0 and kb == 0:
                P.wait("tensor", qld[qt % 2], q_loaded[qt])
            if qs == 0:
                P.wait("tensor", kld[kb % 2], kv_loaded[kb])
            pS = psS[:, (i % 2) * 512:(i % 2 + 1) * 512]
            for kc in range(DC):
                P.op("tensor", "matmul", pS,
                     qt_[qt % 2][:, kc * 512 + qs * 128: kc * 512 + (qs + 1) * 128],
                     kt[kb % 2][:, kc * 512:(kc + 1) * 512],
                     start=(kc == 0), stop=(kc == DC - 1))
            s_mm[i] = P.einc("tensor")
            if qs == 3 and kb == qt:
                qrec[qt] = s_mm[i]
            while next_tr <= i - 1:
                emit_trcp(next_tr)
                next_tr += 1
            while next_pv <= i - 2:
                emit_pv(next_pv)
                next_pv += 1
            if qs == 1 and kb < qt:
                load_kv(qt, kb + 1)
            if diag:
                P.wait("vector", ES_T, s_mm[i])
                if i >= 2:
                    P.wait("vector", ES_S, exp_done[i - 2])
                mask_v[i] = P.op("vector", "tensor_tensor", out=Ssb[i % 2], in0=pS,
                                 in1=maskA[:, qs * 512:(qs + 1) * 512], op=ALU.add,
                                 sem=ES_V)
                src = Ssb[i % 2]
                P.wait("scalar", ES_V, mask_v[i])
            else:
                P.wait("scalar", ES_T, s_mm[i])
                src = pS
            if i >= 2:
                P.wait("scalar", ES_T, tr_done.get(i - 2, 0))
                P.wait("scalar", ES_V, den_v.get(i - 2, 0))
            exp_done[i] = P.op("scalar", "activation", out=PTb[i % 2], in_=src, func=AF.Exp,
                               accum_out=stat[:, 8 + i % 2:9 + i % 2], sem=ES_S)
            if qs == 3 and kb == qt:
                while next_tr <= i:
                    emit_trcp(next_tr)
                    next_tr += 1
                while next_pv <= i:
                    emit_pv(next_pv)
                    next_pv += 1
                if qt + 1 < NQ:
                    load_kv(qt + 1, 0)
                finalize(qt)
        P.barrier()



def emit_p5a(P, ctx):
    nc = P.nc
    w_d = P.dram("L0_ffn_W_qkv")
    o_d = P.dram("wqkT3")
    idb = ctx["idb"]
    with ExitStack() as st:
        wst = [st.enter_context(nc.sbuf_tensor(f"p5a_w{i}", [128, 2048], F32))[:] for i in range(2)]
        wbf = [st.enter_context(nc.sbuf_tensor(f"p5a_b{i}", [128, 2048], BF16))[:] for i in range(2)]
        ev = [st.enter_context(nc.sbuf_tensor(f"p5a_e{i}", [128, 2048], BF16))[:] for i in range(2)]
        psT = [st.enter_context(nc.psum_tensor(f"p5a_p{i}", [128, 2048], BF16))[:] for i in range(2)]
        ld = [P.pool_sem(), P.pool_sem()]
        stq = [P.pool_sem(), P.pool_sem()]
        rec = {}
        ldt = {}

        def issue_ld(c):
            if c >= 2:
                P.wait("sync", P.esem["vector"], rec[c - 2]["cv"])
            ldt[c] = dma_split2(P, wst[c % 2], w_d[c * 128:(c + 1) * 128, :], ld[c % 2], 4)

        issue_ld(0)
        for c in range(48):
            b = c % 2
            r = {}
            if c + 1 < 48:
                issue_ld(c + 1)
            r["ld"] = ldt[c]
            P.wait("vector", ld[b], r["ld"])
            if c >= 2:
                P.wait("vector", P.esem["tensor"], rec[c - 2]["tr"])
            r["cv"] = P.op("vector", "tensor_copy", out=wbf[b], in_=wst[b], sem=P.esem["vector"])
            P.wait("tensor", P.esem["vector"], r["cv"])
            if c >= 2:
                P.wait("tensor", P.esem["scalar"], rec[c - 2]["ev"])
            for tc in range(DC):
                P.op("tensor", "transpose", psT[b][:, tc * 128:(tc + 1) * 128],
                     wbf[b][:, tc * 128:(tc + 1) * 128], idb)
            r["tr"] = P.einc("tensor")
            P.wait("scalar", P.esem["tensor"], r["tr"])
            if c >= 2:
                P.wait("scalar", stq[b], rec[c - 2]["st"])
            r["ev"] = P.op("scalar", "copy", out=ev[b], in_=psT[b], sem=P.esem["scalar"])
            P.wait("sync", P.esem["scalar"], r["ev"])
            r["st"] = P.op("sync", "dma_start", out=o_d[c], in_=ev[b], sem=stq[b])
            rec[c] = r
        P.barrier()



def emit_p5b(P, ctx):
    nc = P.nc
    wg_d = P.dram("L0_ffn_W_gate")
    x_d = P.dram("xn32").rearrange("(c p) t -> p c t", p=128)
    hot_d = P.dram("hot")
    g1_d = P.dram("g1v")
    idb, idf, iota16 = ctx["idb"], ctx["idf"], ctx["iota16"]
    U32 = mybir.dt.uint32
    with ExitStack() as st:
        wg = st.enter_context(nc.sbuf_tensor("p5b_wg", [16, 2048], F32))[:]
        wgT = st.enter_context(nc.sbuf_tensor("p5b_wgT", [128, DC * 16], F32))[:]
        xt = [st.enter_context(nc.sbuf_tensor(f"p5b_x{i}", [128, DC * 512], F32))[:] for i in range(2)]
        lg = st.enter_context(nc.sbuf_tensor("p5b_lg", [16, 512], F32))[:]
        lt = st.enter_context(nc.sbuf_tensor("p5b_lt", [128, 16], F32))[:]
        el = st.enter_context(nc.sbuf_tensor("p5b_el", [128, 16], F32))[:]
        mx = st.enter_context(nc.sbuf_tensor("p5b_mx", [128, 8], F32))[:]
        ix = st.enter_context(nc.sbuf_tensor("p5b_ix", [128, 8], U32))[:]
        sm = st.enter_context(nc.sbuf_tensor("p5b_sm", [128, 4], F32))[:]
        hsb = [st.enter_context(nc.sbuf_tensor(f"p5b_h{i}", [128, 32], F32))[:] for i in range(2)]
        psL = st.enter_context(nc.psum_tensor("p5b_pL", [16, 512], F32))[:]
        psX = st.enter_context(nc.psum_tensor("p5b_pX", [128, 16], F32))[:]
        ld = [P.pool_sem(), P.pool_sem()]
        stq = [P.pool_sem(), P.pool_sem()]
        ES_T, ES_V, ES_S = P.esem["tensor"], P.esem["vector"], P.esem["scalar"]
        P.op("sync", "dma_start", out=wg, in_=wg_d, sem=ld[0])
        P.wait("tensor", ld[0], P.cnt[ld[0]])
        for tc in range(DC):
            P.op("tensor", "transpose", psX, wg[0:16, tc * 128:(tc + 1) * 128], idf[0:16, 0:16])
            ttr = P.einc("tensor")
            P.wait("scalar", ES_T, ttr)
            P.op("scalar", "copy", out=wgT[:, tc * 16:(tc + 1) * 16], in_=psX, sem=ES_S)
            P.wait("tensor", ES_S, P.cnt[ES_S])
        rec = {}
        hst = {}
        ldt = {}

        def issue_ld(i):
            if i >= 2:
                P.wait("sync", ES_T, rec[i - 2]["mm"])
            ldt[i] = dma_split3(P, xt[i % 2].rearrange("p (c n) -> p c n", c=DC),
                                x_d[:, :, i * 512:(i + 1) * 512], ld[i % 2], 8)

        issue_ld(0)
        for i in range(8):
            b = i % 2
            r = {}
            if i + 1 < 8:
                issue_ld(i + 1)
            r["ld"] = ldt[i]
            P.wait("tensor", ld[b], r["ld"])
            if i >= 1:
                P.wait("tensor", ES_S, rec[i - 1]["lgc"])
            for kc in range(DC):
                P.op("tensor", "matmul", psL, wgT[:, kc * 16:(kc + 1) * 16],
                     xt[b][:, kc * 512:(kc + 1) * 512], start=(kc == 0), stop=(kc == DC - 1))
            r["mm"] = P.einc("tensor")
            P.wait("scalar", ES_T, r["mm"])
            r["lgc"] = P.op("scalar", "copy", out=lg, in_=psL, sem=ES_S)
            for s4 in range(4):
                t0 = i * 512 + s4 * 128
                P.wait("tensor", ES_S, r["lgc"])
                if i + s4 > 0:
                    P.wait("tensor", ES_V, P.cnt[ES_V])
                tt = P.op("tensor", "transpose", psX, lg[0:16, s4 * 128:(s4 + 1) * 128],
                          idf[0:16, 0:16], sem=ES_T)
                P.wait("vector", ES_T, tt)
                P.op("vector", "tensor_copy", out=lt, in_=psX)
                P.selfbar("vector")
                P.op("vector", "max", out=mx, in_=lt)
                P.selfbar("vector")
                P.op("vector", "max_index", out=ix, in_max=mx, in_values=lt)
                P.selfbar("vector")
                vsub = P.op("vector", "tensor_single_scalar", out=el, in_=lt, scalar=mx[:, 0:1],
                            op=ALU.subtract, sem=ES_V)
                P.wait("scalar", ES_V, vsub)
                ex = P.op("scalar", "activation", out=el, in_=el, func=AF.Exp, sem=ES_S)
                P.wait("vector", ES_S, ex)
                P.op("vector", "reduce_sum", out=sm[:, 0:1], in_=el, axis=AX)
                P.selfbar("vector")
                P.op("vector", "reciprocal", out=sm[:, 1:2], in_=sm[:, 0:1])
                P.op("vector", "tensor_copy", out=sm[:, 2:3], in_=ix[:, 0:1])
                P.selfbar("vector")
                bh = hsb[s4 % 2]
                uu = i * 4 + s4
                if uu >= 2:
                    P.wait("vector", stq[s4 % 2], hst[uu - 2])
                P.op("vector", "tensor_single_scalar", out=bh[:, 0:16], in_=iota16,
                     scalar=sm[:, 2:3], op=ALU.is_equal)
                P.selfbar("vector")
                vv = P.op("vector", "tensor_single_scalar", out=bh[:, 16:32], in_=bh[:, 0:16],
                          scalar=sm[:, 1:2], op=ALU.mult, sem=ES_V)
                P.wait("sync", ES_V, vv)
                P.op("sync", "dma_start", out=hot_d[t0:t0 + 128, :], in_=bh[:, 0:16], sem=stq[s4 % 2])
                hst[uu] = P.op("sync", "dma_start", out=g1_d[t0:t0 + 128, :], in_=bh[:, 16:32], sem=stq[s4 % 2])
            rec[i] = r
        P.barrier()



def emit_p5c(P, ctx):
    nc = P.nc
    w_d = P.dram("wqkT3")
    x_d = P.dram("xnbf").rearrange("(c p) t -> p c t", p=128)
    hot_d = P.dram("hot")
    g1_d = P.dram("g1v")
    kF_d = P.dram("kshF")
    vT_d = P.dram("vshT")
    idf = ctx["idf"]
    ES_T, ES_V, ES_S = P.esem["tensor"], P.esem["vector"], P.esem["scalar"]
    with ExitStack() as st:
        wk = st.enter_context(nc.sbuf_tensor("p5c_wk", [128, DC * 2048], BF16))[:]
        wv = st.enter_context(nc.sbuf_tensor("p5c_wv", [128, DC * 2048], BF16))[:]
        xt = [st.enter_context(nc.sbuf_tensor(f"p5c_x{i}", [128, DC * 512], BF16))[:] for i in range(2)]
        ktm = st.enter_context(nc.sbuf_tensor("p5c_ktm", [128, 2048], F32))[:]
        vtm = st.enter_context(nc.sbuf_tensor("p5c_vtm", [128, 2048], F32))[:]
        hsel = st.enter_context(nc.sbuf_tensor("p5c_h", [128, 32], F32))[:]
        acc = st.enter_context(nc.sbuf_tensor("p5c_acc", [128, 256], F32))[:]
        accb = st.enter_context(nc.sbuf_tensor("p5c_accb", [128, 256], BF16))[:]
        kTf = st.enter_context(nc.sbuf_tensor("p5c_kTf", [128, 128], BF16))[:]
        psK = [st.enter_context(nc.psum_tensor(f"p5c_pk{i}", [128, 512], F32))[:] for i in range(2)]
        psT = st.enter_context(nc.psum_tensor("p5c_pT", [128, 128], F32))[:]
        ld = [P.pool_sem(), P.pool_sem()]
        hld = P.pool_sem()
        stq = P.pool_sem()
        wld = P.pool_sem()
        for dst, base in ((wk, 16), (wv, 32)):
            for mc in range(16):
                P.op("sync", "dma_start",
                     out=dst.rearrange("p (kc m f) -> p kc m f", kc=DC, m=16)[:, :, mc:mc + 1, :],
                     in_=w_d[base + mc].rearrange("p (kc f) -> p kc f", kc=DC).unsqueeze(2),
                     sem=wld)
        P.wait("tensor", wld, P.cnt[wld])
        rec = {}
        ev = {}
        prev_tt = prev_sc = prev_st = None
        ldt = {}

        def issue_ld(i):
            if i >= 2:
                P.wait("sync", ES_T, rec[i - 2]["mm"])
            ldt[i] = dma_split3(P, xt[i % 2].rearrange("p (c n) -> p c n", c=DC),
                                x_d[:, :, i * 512:(i + 1) * 512], ld[i % 2], 4)

        issue_ld(0)
        u = 0
        for i in range(8):
            b = i % 2
            r = {}
            if i + 1 < 8:
                issue_ld(i + 1)
            r["ld"] = ldt[i]
            P.wait("tensor", ld[b], r["ld"])
            for tc in range(4):
                t0 = i * 512 + tc * 128
                P.wait("sync", ES_V, P.cnt[ES_V])
                hl = P.op("sync", "dma_start", out=hsel[:, 0:16], in_=hot_d[t0:t0 + 128, :], sem=hld)
                hl = P.op("sync", "dma_start", out=hsel[:, 16:32], in_=g1_d[t0:t0 + 128, :], sem=hld)
                for which, wt, dtm in ((0, wk, ktm), (1, wv, vtm)):
                    for nc4 in range(4):
                        pb = psK[u % 2]
                        if u >= 2:
                            P.wait("tensor", ES_V, ev[u - 2])
                        for kc in range(DC):
                            P.op("tensor", "matmul", pb,
                                 xt[b][:, kc * 512 + tc * 128: kc * 512 + (tc + 1) * 128],
                                 wt[:, kc * 2048 + nc4 * 512: kc * 2048 + (nc4 + 1) * 512],
                                 start=(kc == 0), stop=(kc == DC - 1))
                        mm = P.einc("tensor")
                        r["mm"] = mm
                        P.wait("vector", ES_T, mm)
                        ev[u] = P.op("vector", "tensor_copy",
                                     out=dtm[:, nc4 * 512:(nc4 + 1) * 512], in_=pb, sem=ES_V)
                        u += 1
                P.wait("vector", hld, P.cnt[hld])
                if prev_tt is not None:
                    P.wait("vector", ES_T, prev_tt)
                    P.wait("vector", stq, prev_st)
                for which, dtm, col0, a0 in ((0, ktm, 0, 0), (1, vtm, 16, 32 * 4)):
                    for h in range(16):
                        if h == 0:
                            P.op("vector", "tensor_single_scalar", out=acc[:, a0:a0 + 128],
                                 in_=dtm[:, 0:128], scalar=hsel[:, col0:col0 + 1], op=ALU.mult)
                        else:
                            P.op("vector", "scalar_tensor_tensor", out=acc[:, a0:a0 + 128],
                                 in0=dtm[:, h * 128:(h + 1) * 128],
                                 scalar=hsel[:, col0 + h:col0 + h + 1],
                                 in1=acc[:, a0:a0 + 128], op0=ALU.mult, op1=ALU.add)
                P.selfbar("vector")
                cv = P.op("vector", "tensor_copy", out=accb, in_=acc, sem=ES_V)
                P.wait("tensor", ES_V, cv)
                if prev_sc is not None:
                    P.wait("tensor", ES_S, prev_sc)
                tt = P.op("tensor", "transpose", psT, acc[:, 0:128], idf, sem=ES_T)
                prev_tt = tt
                P.wait("scalar", ES_T, tt)
                if prev_st is not None:
                    P.wait("scalar", stq, prev_st)
                sc = P.op("scalar", "copy", out=kTf, in_=psT, sem=ES_S)
                prev_sc = sc
                P.wait("sync", ES_S, sc)
                P.wait("sync", ES_V, cv)
                P.op("sync", "dma_start", out=kF_d[:, t0:t0 + 128], in_=kTf, sem=stq)
                prev_st = P.op("sync", "dma_start", out=vT_d[t0:t0 + 128, :],
                               in_=accb[:, 128:256], sem=stq)
            rec[i] = r
        P.barrier()



def emit_p5d(P, ctx):
    emit_proj_fm(P, ctx, "wqkT3", "xnbf", "q0", 2048, 2048, w_col0=0,
                 scale=SC_QMQ, tag="p5d", w_chunked=True)



def emit_p5e(P, ctx):
    nc = P.nc
    q_d = P.dram("q0").rearrange("(c p) t -> p c t", p=128)
    kF_d = P.dram("kshF")
    vT_d = P.dram("vshT").rearrange("(c p) n -> p c n", p=128)
    y_d = P.dram("y0n").rearrange("(c p) t -> p c t", p=128)
    ones_col, ones_row_f = ctx["ones_col"], ctx["ones_row_f"]
    ES_T, ES_V, ES_S = P.esem["tensor"], P.esem["vector"], P.esem["scalar"]
    with ExitStack() as st:
        maskM = st.enter_context(nc.sbuf_tensor("p5e_maskM", [128, 4 * 512], BF16))[:]
        ksh = st.enter_context(nc.sbuf_tensor("p5e_k", [128, S], BF16))[:]
        vsh = st.enter_context(nc.sbuf_tensor("p5e_v", [128, 32 * 128], BF16))[:]
        qt_ = [st.enter_context(nc.sbuf_tensor(f"p5e_q{i}", [128, DC * 512], BF16))[:] for i in range(2)]
        PT = [st.enter_context(nc.sbuf_tensor(f"p5e_P{i}", [128, 1024], BF16))[:] for i in range(2)]
        dsb = [st.enter_context(nc.sbuf_tensor(f"p5e_d{i}", [1, 512], F32))[:] for i in range(2)]
        rsb = [st.enter_context(nc.sbuf_tensor(f"p5e_r{i}", [1, 512], F32))[:] for i in range(2)]
        bcb = [st.enter_context(nc.sbuf_tensor(f"p5e_bc{i}", [128, 512], F32))[:] for i in range(2)]
        ysb = [st.enter_context(nc.sbuf_tensor(f"p5e_y{i}", [128, 512], F32))[:] for i in range(2)]
        psS = [st.enter_context(nc.psum_tensor(f"p5e_pS{i}", [128, 1024], F32))[:] for i in range(2)]
        psO = st.enter_context(nc.psum_tensor("p5e_pO", [128, 512], F32))[:]
        psD = [st.enter_context(nc.psum_tensor(f"p5e_pD{i}", [1, 512], F32))[:] for i in range(2)]
        psB = st.enter_context(nc.psum_tensor("p5e_pB", [128, 512], F32))[:]
        qld = [P.pool_sem(), P.pool_sem()]
        stq = [P.pool_sem(), P.pool_sem()]
        wld = P.pool_sem()
        P.op("gpsimd", "memset", maskM, 1.0)
        for i_ in range(4):
            P.op("gpsimd", "affine_select", out=maskM[:, i_ * 512:(i_ + 1) * 512],
                 in_=maskM[:, i_ * 512:(i_ + 1) * 512], compare_op=ALU.is_ge,
                 fill=0.0, base=-128 * i_, pattern=[[1, 512]],
                 channel_multiplier=-1, sem=P.esem["gpsimd"])
        P.wait("vector", P.esem["gpsimd"], P.cnt[P.esem["gpsimd"]])
        P.op("sync", "dma_start", out=ksh, in_=kF_d, sem=wld)
        P.op("sync", "dma_start", out=vsh.rearrange("p (c n) -> p c n", c=32), in_=vT_d, sem=wld)
        P.wait("tensor", wld, P.cnt[wld])
        units = []
        hu_ = 0
        for qt in range(8):
            for h in range(16):
                nh = 2 * (qt + 1)
                for kb in range(qt + 1):
                    for half in range(2):
                        iu = kb * 2 + half
                        units.append((qt, h, kb, half, hu_, iu == 0, iu == nh - 1,
                                      kb == qt))
                hu_ += 1
        NU = len(units)
        s_mm, exp_done, mask_done, pvden_done = {}, {}, {}, {}
        dv_done, rv_done, bc_done, bcb_done, yev, yst = {}, {}, {}, {}, {}, {}
        qrec, q_loaded = {}, {}

        def load_q(qt):
            bq = qt % 2
            if qt >= 2:
                P.wait("sync", ES_T, qrec[qt - 2])
            q_loaded[qt] = dma_split3(P, qt_[bq].rearrange("p (c n) -> p c n", c=DC),
                                      q_d[:, :, qt * 512:(qt + 1) * 512], qld[bq], 4)

        def emit_head_end(un, last_pv):
            qt, h, kb, half, hu, first, last, diag = un
            P.wait("scalar", ES_T, last_pv)
            if hu >= 2:
                P.wait("scalar", ES_V, rv_done[hu - 2])
            dv = P.op("scalar", "copy", out=dsb[hu % 2], in_=psD[hu % 2], sem=ES_S)
            dv_done[hu] = dv
            P.wait("vector", ES_S, dv)
            if hu >= 2:
                P.wait("vector", ES_T, bc_done[hu - 2])
            rv = P.op("vector", "reciprocal", out=rsb[hu % 2], in_=dsb[hu % 2], sem=ES_V)
            rv_done[hu] = rv
            P.wait("tensor", ES_V, rv)
            if hu >= 1:
                P.wait("tensor", ES_V, bcb_done[hu - 1])
            bc = P.op("tensor", "matmul", psB, ones_row_f, rsb[hu % 2], start=True,
                      stop=True, sem=ES_T)
            bc_done[hu] = bc
            P.wait("vector", ES_T, bc)
            if hu >= 2:
                P.wait("vector", ES_V, yev[hu - 2])
            cb = P.op("vector", "tensor_copy", out=bcb[hu % 2], in_=psB, sem=ES_V)
            bcb_done[hu] = cb
            P.wait("vector", ES_V, cb)
            P.wait("vector", ES_T, last_pv)
            if hu >= 2:
                P.wait("vector", stq[hu % 2], yst[hu - 2])
            yv = P.op("vector", "tensor_tensor", out=ysb[hu % 2], in0=psO,
                      in1=bcb[hu % 2], op=ALU.mult, sem=ES_V)
            yev[hu] = yv
            P.wait("sync", ES_V, yv)
            yst[hu] = P.op("sync", "dma_start", out=y_d[:, h, qt * 512:(qt + 1) * 512],
                           in_=ysb[hu % 2], sem=stq[hu % 2])

        def emit_pvden(i):
            un = units[i]
            qt, h, kb, half, hu, first, last, diag = un
            if diag:
                P.wait("tensor", ES_V, mask_done[i])
            else:
                P.wait("tensor", ES_S, exp_done[i])
            if first:
                if hu >= 1:
                    P.wait("tensor", ES_V, yev[hu - 1])
                if hu >= 2:
                    P.wait("tensor", ES_S, dv_done[hu - 2])
            for j in range(2):
                kc = half * 2 + j
                P.op("tensor", "matmul", psO,
                     vsh[:, (kb * 4 + kc) * 128:(kb * 4 + kc + 1) * 128],
                     PT[i % 2][:, j * 512:(j + 1) * 512],
                     start=(first and j == 0), stop=(last and j == 1))
            for j in range(2):
                P.op("tensor", "matmul", psD[hu % 2], ones_col,
                     PT[i % 2][:, j * 512:(j + 1) * 512],
                     start=(first and j == 0), stop=(last and j == 1))
            pvden_done[i] = P.einc("tensor")
            if last:
                emit_head_end(un, pvden_done[i])

        load_q(0)
        for i in range(NU):
            un = units[i]
            qt, h, kb, half, hu, first, last, diag = un
            if h == 0 and first:
                if qt + 1 < 8:
                    load_q(qt + 1)
            if i >= 2:
                P.wait("tensor", ES_S, exp_done[i - 2])
            if h == 0 and first:
                P.wait("tensor", qld[qt % 2], q_loaded[qt])
            for j in range(2):
                kc = half * 2 + j
                P.op("tensor", "matmul", psS[i % 2][:, j * 512:(j + 1) * 512],
                     ksh[:, (kb * 4 + kc) * 128:(kb * 4 + kc + 1) * 128],
                     qt_[qt % 2][:, h * 512:(h + 1) * 512], start=True, stop=True)
            s_mm[i] = P.einc("tensor")
            if h == 15 and last:
                qrec[qt] = s_mm[i]
            if i >= 2:
                emit_pvden(i - 2)
            P.wait("scalar", ES_T, s_mm[i])
            if i >= 2:
                P.wait("scalar", ES_T, pvden_done[i - 2])
            ex = P.op("scalar", "activation", out=PT[i % 2], in_=psS[i % 2], func=AF.Exp,
                      sem=ES_S)
            exp_done[i] = ex
            if diag:
                P.wait("vector", ES_S, ex)
                mask_done[i] = P.op("vector", "tensor_tensor", out=PT[i % 2],
                                    in0=PT[i % 2],
                                    in1=maskM[:, half * 1024:(half + 1) * 1024],
                                    op=ALU.mult, sem=ES_V)
        emit_pvden(NU - 2)
        emit_pvden(NU - 1)
        P.barrier()



def emit_p6(P, ctx):
    nc = P.nc
    x_d = P.dram("xres").rearrange("(c p) t -> p c t", p=128)
    s_d = P.dram("s0").rearrange("(c p) t -> p c t", p=128)
    y_d = P.dram("y0n").rearrange("(c p) t -> p c t", p=128)
    x1_d = P.dram("x1").rearrange("(c p) t -> p c t", p=128)
    xn_d = P.dram("xnbf").rearrange("(c p) t -> p c t", p=128)
    ones_col, ones_row_f = ctx["ones_col"], ctx["ones_row_f"]
    inv_d = P.const(1.0 / D)
    eps = P.const(1e-5)
    ES_T, ES_V, ES_S = P.esem["tensor"], P.esem["vector"], P.esem["scalar"]
    TT = 256
    NT = S // TT
    with ExitStack() as st:
        xt = [st.enter_context(nc.sbuf_tensor(f"p6_x{i}", [128, DC * TT], F32))[:] for i in range(2)]
        stt_ = [st.enter_context(nc.sbuf_tensor(f"p6_s{i}", [128, DC * TT], F32))[:] for i in range(2)]
        yt = [st.enter_context(nc.sbuf_tensor(f"p6_y{i}", [128, DC * TT], F32))[:] for i in range(2)]
        gt8 = st.enter_context(nc.sbuf_tensor("p6_g8", [128, DC * TT], F32))[:]
        nt8 = st.enter_context(nc.sbuf_tensor("p6_n8", [128, DC * TT], F32))[:]
        x1t = [st.enter_context(nc.sbuf_tensor(f"p6_o{i}", [128, DC * TT], F32))[:] for i in range(2)]
        x1b = st.enter_context(nc.sbuf_tensor("p6_b", [128, DC * TT], BF16))[:]
        sqb = st.enter_context(nc.sbuf_tensor("p6_q", [128, DC * TT], BF16))[:]
        row = st.enter_context(nc.sbuf_tensor("p6_r", [1, 8 * TT], F32))[:]
        bcb = [st.enter_context(nc.sbuf_tensor(f"p6_c{i}", [128, 2 * TT], F32))[:] for i in range(2)]
        onb = [st.enter_context(nc.sbuf_tensor(f"p6_n{i}", [128, DC * TT], BF16))[:] for i in range(2)]
        psS = st.enter_context(nc.psum_tensor("p6_pS", [1, 2 * TT], F32))[:]
        psB = st.enter_context(nc.psum_tensor("p6_pB", [128, 2 * TT], F32))[:]
        ld = [P.pool_sem(), P.pool_sem()]
        stq = [P.pool_sem(), P.pool_sem()]
        rec = {}
        ldt = {}

        def issue_ld(i):
            if i >= 2:
                P.wait("sync", ES_V, rec[i - 2]["v_main"])
                P.wait("sync", ES_S, rec[i - 2]["s_done"])
            dma_split3(P, xt[i % 2].rearrange("p (c n) -> p c n", c=DC),
                       x_d[:, :, i * TT:(i + 1) * TT], ld[i % 2], 4)
            dma_split3(P, stt_[i % 2].rearrange("p (c n) -> p c n", c=DC),
                       s_d[:, :, i * TT:(i + 1) * TT], ld[i % 2], 4)
            ldt[i] = dma_split3(P, yt[i % 2].rearrange("p (c n) -> p c n", c=DC),
                                y_d[:, :, i * TT:(i + 1) * TT], ld[i % 2], 4)

        def norm_tail(i):
            r = rec[i]
            P.wait("vector", ES_T, r["t_bc"])
            P.op("vector", "tensor_copy", out=bcb[i % 2][:, 0:TT], in_=psB[:, 0:TT])
            P.op("vector", "tensor_copy", out=bcb[i % 2][:, TT:2 * TT], in_=psB[:, TT:2 * TT])
            P.selfbar("vector")
            if i >= 2:
                P.wait("vector", stq[i % 2], rec[i - 2]["st"])
            P.op("vector", "tensor_tensor", out=nt8.rearrange("p (c n) -> p c n", c=DC),
                 in0=x1t[i % 2].rearrange("p (c n) -> p c n", c=DC),
                 in1=bcb[i % 2][:, 0:TT].unsqueeze(1).broadcast_to([128, DC, TT]),
                 op=ALU.subtract)
            P.selfbar("vector")
            P.op("vector", "tensor_tensor", out=onb[i % 2].rearrange("p (c n) -> p c n", c=DC),
                 in0=nt8.rearrange("p (c n) -> p c n", c=DC),
                 in1=bcb[i % 2][:, TT:2 * TT].unsqueeze(1).broadcast_to([128, DC, TT]),
                 op=ALU.mult)
            P.selfbar("vector")
            r["v_done"] = P.cnt[ES_V]
            P.wait("sync", ES_V, r["v_done"])
            dma_split3(P, x1_d[:, :, i * TT:(i + 1) * TT],
                       x1t[i % 2].rearrange("p (c n) -> p c n", c=DC), stq[i % 2], 4)
            r["st"] = dma_split3(P, xn_d[:, :, i * TT:(i + 1) * TT],
                                 onb[i % 2].rearrange("p (c n) -> p c n", c=DC), stq[i % 2], 2)

        issue_ld(0)
        for i in range(NT):
            b = i % 2
            r = {}
            r["ld"] = ldt[i]
            P.wait("scalar", ld[b], r["ld"])
            P.wait("vector", ld[b], r["ld"])
            if i >= 1:
                P.wait("vector", ES_T, rec[i - 1]["t_bc"])
                P.wait("vector", ES_S, rec[i - 1]["s_done"])
                P.wait("scalar", ES_T, rec[i - 1]["t_bc"])
                P.wait("scalar", ES_V, rec[i - 1]["v_main"])
            sg = P.op("scalar", "activation", out=gt8, in_=yt[b], func=AF.Silu, sem=ES_S)
            P.wait("vector", ES_S, sg)
            P.op("vector", "tensor_tensor", out=gt8, in0=stt_[b], in1=gt8, op=ALU.mult)
            P.selfbar("vector")
            if i >= 2:
                P.wait("vector", stq[b], rec[i - 2]["st"])
            P.op("vector", "tensor_tensor", out=x1t[b], in0=xt[b], in1=gt8, op=ALU.add)
            P.selfbar("vector")
            P.op("vector", "tensor_copy", out=x1b, in_=x1t[b], sem=ES_V)
            cv = P.cnt[ES_V]
            r["v_main"] = cv
            rec[i] = r
            if i >= 1:
                norm_tail(i - 1)
            if i + 1 < NT:
                issue_ld(i + 1)
            P.wait("scalar", ES_V, cv)
            sq = P.op("scalar", "activation", out=sqb, in_=x1b, func=AF.Square, sem=ES_S)
            r["s_done"] = sq
            P.wait("tensor", ES_V, cv)
            P.wait("tensor", ES_S, sq)
            if i >= 1:
                P.wait("tensor", ES_V, rec[i - 1]["v_row"])
            for c in range(DC):
                P.op("tensor", "matmul", psS[:, 0:TT], ones_col, x1b[:, c * TT:(c + 1) * TT],
                     start=(c == 0), stop=(c == DC - 1))
            for c in range(DC):
                P.op("tensor", "matmul", psS[:, TT:2 * TT], ones_col, sqb[:, c * TT:(c + 1) * TT],
                     start=(c == 0), stop=(c == DC - 1))
            tst = P.einc("tensor")
            P.wait("vector", ES_T, tst)
            P.op("vector", "tensor_single_scalar", out=row[:, 0:TT], in_=psS[:, 0:TT],
                 scalar=inv_d[0:1], op=ALU.mult)
            P.op("vector", "tensor_single_scalar", out=row[:, TT:2 * TT], in_=psS[:, TT:2 * TT],
                 scalar=inv_d[0:1], op=ALU.mult)
            P.selfbar("vector")
            P.op("vector", "tensor_tensor", out=row[:, 2 * TT:3 * TT], in0=row[:, 0:TT],
                 in1=row[:, 0:TT], op=ALU.mult)
            P.selfbar("vector")
            vv = P.op("vector", "tensor_tensor", out=row[:, 3 * TT:4 * TT], in0=row[:, TT:2 * TT],
                      in1=row[:, 2 * TT:3 * TT], op=ALU.subtract, sem=ES_V)
            P.wait("scalar", ES_V, vv)
            sv = P.op("scalar", "activation", out=row[:, 4 * TT:5 * TT], in_=row[:, 3 * TT:4 * TT],
                      func=AF.Sqrt, bias=eps[0:1], sem=ES_S)
            P.wait("vector", ES_S, sv)
            rv = P.op("vector", "reciprocal", out=row[:, 5 * TT:6 * TT], in_=row[:, 4 * TT:5 * TT],
                      sem=ES_V)
            r["v_row"] = rv
            P.wait("tensor", ES_V, rv)
            P.op("tensor", "matmul", psB[:, 0:TT], ones_row_f, row[:, 0:TT], start=True, stop=True)
            r["t_bc"] = P.op("tensor", "matmul", psB[:, TT:2 * TT], ones_row_f,
                             row[:, 5 * TT:6 * TT], start=True, stop=True, sem=ES_T)
            rec[i] = r
        norm_tail(NT - 1)
        P.barrier()


def emit_p7a(P, ctx):
    emit_proj_fm(P, ctx, "L1_attn_W_in", "xnbf", "xh", 2048, 2048,
                 bias_base=ctx["VEC"]["b_in"], out_f32=True, tag="p7a")



def emit_p7g(P, ctx):
    nc = P.nc
    xh_d = P.dram("xh").rearrange("(c p) t -> p c t", p=128)
    gx_d = P.dram("glx").rearrange("(c p) t -> p c t", p=128)
    ga_d = P.dram("gla").rearrange("(c p) t -> p c t", p=128)
    VEC = ctx["VEC"]
    vecs = ctx["vecs"]
    ES_T, ES_V, ES_S = P.esem["tensor"], P.esem["vector"], P.esem["scalar"]
    TT = 512
    with ExitStack() as st:
        wg = [st.enter_context(nc.sbuf_tensor(f"p7g_w{i}", [128, 16 * 128], BF16))[:] for i in range(2)]
        wst = st.enter_context(nc.sbuf_tensor("p7g_ws", [128, 16 * 128], F32))[:]
        xt = [st.enter_context(nc.sbuf_tensor(f"p7g_x{i}", [128, DC * TT], F32))[:] for i in range(2)]
        xb = [st.enter_context(nc.sbuf_tensor(f"p7g_b{i}", [128, DC * TT], BF16))[:] for i in range(2)]
        ot = [st.enter_context(nc.sbuf_tensor(f"p7g_o{i}", [128, 16 * TT], F32))[:] for i in range(2)]
        ps = [st.enter_context(nc.psum_tensor(f"p7g_p{i}", [128, TT], F32))[:] for i in range(2)]
        wld = P.pool_sem()
        ld = [P.pool_sem(), P.pool_sem()]
        stq = [P.pool_sem(), P.pool_sem()]
        for gi, nm in enumerate(("L1_attn_Wg_x", "L1_attn_Wg_a")):
            w_d = P.dram(nm)
            if gi == 1:
                P.wait("sync", ES_V, P.cnt[ES_V])
            t = P.op("sync", "dma_start", out=wst.rearrange("p (a b) -> p a b", a=16),
                     in_=w_d.rearrange("i (kc p) f -> p (i kc) f", p=128), sem=wld)
            P.wait("vector", wld, t)
            P.op("vector", "tensor_copy", out=wg[gi], in_=wst, sem=ES_V)
        P.wait("tensor", ES_V, P.cnt[ES_V])
        rec = {}
        ev = {}
        for i in range(8):
            b = i % 2
            r = {}
            if i >= 2:
                P.wait("sync", ES_T, rec[i - 2]["mm"])
            r["ld"] = dma_split3(P, xt[b].rearrange("p (c n) -> p c n", c=DC),
                                 xh_d[:, :, i * TT:(i + 1) * TT], ld[b], 8)
            P.wait("vector", ld[b], r["ld"])
            if i >= 2:
                P.wait("vector", ES_T, rec[i - 2]["mm"])
            cv = P.op("vector", "tensor_copy", out=xb[b], in_=xt[b], sem=ES_V)
            P.wait("tensor", ES_V, cv)
            for gi in range(2):
                for blk in range(8):
                    j = (i * 2 + gi) * 8 + blk
                    pb = ps[j % 2]
                    if j >= 2:
                        P.wait("tensor", ES_V, ev[j - 2])
                    for kc in range(2):
                        P.op("tensor", "matmul", pb, wg[gi][:, (blk * 2 + kc) * 128:(blk * 2 + kc + 1) * 128],
                             xb[b][:, (blk * 2 + kc) * TT:(blk * 2 + kc + 1) * TT],
                             start=(kc == 0), stop=(kc == 1))
                    mm = P.einc("tensor")
                    r["mm"] = mm
                    P.wait("vector", ES_T, mm)
                    if gi == 0 and blk == 0 and i >= 2:
                        P.wait("vector", stq[b], rec[i - 2]["st"])
                    bb = VEC["bg_x"] if gi == 0 else VEC["bg_a"]
                    ev[j] = P.op("vector", "tensor_single_scalar",
                                 out=ot[b][:, (gi * 8 + blk) * TT:(gi * 8 + blk + 1) * TT],
                                 in_=pb, scalar=vecs[:, bb + blk:bb + blk + 1], op=ALU.add,
                                 sem=ES_V)
            P.wait("sync", ES_V, ev[(i * 2 + 1) * 8 + 7])
            dma_split3(P, gx_d[:, :, i * TT:(i + 1) * TT],
                       ot[b][:, 0:8 * TT].rearrange("p (c n) -> p c n", c=8), stq[b], 4)
            r["st"] = dma_split3(P, ga_d[:, :, i * TT:(i + 1) * TT],
                                 ot[b][:, 8 * TT:16 * TT].rearrange("p (c n) -> p c n", c=8), stq[b], 4)
            rec[i] = r
        P.barrier()



def emit_p7b(P, ctx):
    nc = P.nc
    xh_d = P.dram("xh").rearrange("(c p) t -> p c t", p=128)
    gx_d = P.dram("glx").rearrange("(c p) t -> p c t", p=128)
    ga_d = P.dram("gla").rearrange("(c p) t -> p c t", p=128)
    h_d = P.dram("hsc").rearrange("(c p) t -> p c t", p=128)
    VEC = ctx["VEC"]
    vecs = ctx["vecs"]
    ES_T, ES_V, ES_S = P.esem["tensor"], P.esem["vector"], P.esem["scalar"]
    TT = 256
    NT = S // TT
    TWO_PI = float(2 * np.pi)
    INV_2PI = P.constf(1.0 / (2 * np.pi))
    NEG_2PI = P.constf(-TWO_PI)
    HALF_PI = P.const(np.pi / 2)
    with ExitStack() as st:
        xt = [st.enter_context(nc.sbuf_tensor(f"p7b_x{i}", [128, DC * TT], F32))[:] for i in range(2)]
        gxt = [st.enter_context(nc.sbuf_tensor(f"p7b_g{i}", [128, 8 * TT], F32))[:] for i in range(2)]
        gat = [st.enter_context(nc.sbuf_tensor(f"p7b_a{i}", [128, 8 * TT], F32))[:] for i in range(2)]
        hb = [st.enter_context(nc.sbuf_tensor(f"p7b_h{i}", [128, DC * TT], BF16))[:] for i in range(2)]
        la8 = st.enter_context(nc.sbuf_tensor("p7b_la8", [128, 8 * TT], F32))[:]
        phi8 = st.enter_context(nc.sbuf_tensor("p7b_ph8", [128, 8 * TT], F32))[:]
        nrm8 = st.enter_context(nc.sbuf_tensor("p7b_nr8", [128, 8 * TT], F32))[:]
        psis8 = st.enter_context(nc.sbuf_tensor("p7b_ps8", [128, 8 * TT], F32))[:]
        cosp8 = st.enter_context(nc.sbuf_tensor("p7b_co8", [128, 8 * TT], F32))[:]
        sinp8 = st.enter_context(nc.sbuf_tensor("p7b_si8", [128, 8 * TT], F32))[:]
        m8 = st.enter_context(nc.sbuf_tensor("p7b_m8", [128, 8 * TT], F32))[:]
        tw8 = st.enter_context(nc.sbuf_tensor("p7b_tw8", [128, 8 * TT], F32))[:]
        xr8 = st.enter_context(nc.sbuf_tensor("p7b_xr8", [128, 8 * TT], F32))[:]
        xi8 = st.enter_context(nc.sbuf_tensor("p7b_xi8", [128, 8 * TT], F32))[:]
        ur8 = st.enter_context(nc.sbuf_tensor("p7b_ur8", [128, 8 * TT], F32))[:]
        mi8 = st.enter_context(nc.sbuf_tensor("p7b_mi8", [128, 8 * TT], I32))[:]
        onet = st.enter_context(nc.sbuf_tensor("p7b_one", [128, TT], F32))[:]
        carry = st.enter_context(nc.sbuf_tensor("p7b_cr", [128, 24], F32))[:]
        ld = [P.pool_sem(), P.pool_sem()]
        stq = [P.pool_sem(), P.pool_sem()]
        P.op("gpsimd", "memset", onet, 1.0)
        P.op("gpsimd", "memset", carry, 0.0, sem=P.esem["gpsimd"])
        P.wait("vector", P.esem["gpsimd"], P.cnt[P.esem["gpsimd"]])
        rec = {}
        sC_prev = sE_prev = None
        ldt = {}

        def issue_ld(i):
            if i >= 2:
                P.wait("sync", ES_V, rec[i - 2]["v_done"])
                P.wait("sync", ES_S, rec[i - 2]["s_done"])
            dma_split3(P, xt[i % 2].rearrange("p (c n) -> p c n", c=DC),
                       xh_d[:, :, i * TT:(i + 1) * TT], ld[i % 2], 4)
            dma_split3(P, gxt[i % 2].rearrange("p (c n) -> p c n", c=8),
                       gx_d[:, :, i * TT:(i + 1) * TT], ld[i % 2], 2)
            ldt[i] = dma_split3(P, gat[i % 2].rearrange("p (c n) -> p c n", c=8),
                                ga_d[:, :, i * TT:(i + 1) * TT], ld[i % 2], 2)

        issue_ld(0)
        for i in range(NT):
            b = i % 2
            r = {}
            if i + 1 < NT:
                issue_ld(i + 1)
            r["ld"] = ldt[i]
            P.wait("scalar", ld[b], r["ld"])
            P.op("scalar", "activation", out=gat[b], in_=gat[b], func=AF.Sigmoid)
            sA = P.op("scalar", "activation", out=gxt[b], in_=gxt[b], func=AF.Sigmoid,
                      sem=ES_S)
            P.wait("vector", ES_S, sA)
            if sC_prev is not None:
                P.wait("vector", ES_S, sC_prev)
            for cc in range(8):
                slg = slice(cc * TT, (cc + 1) * TT)
                P.op("vector", "tensor_scalar", out=la8[:, slg], in0=gat[b][:, slg],
                     scalar1=vecs[:, VEC["sp"] + cc:VEC["sp"] + cc + 1],
                     scalar2=-8.0, op0=ALU.mult, op1=ALU.mult)
                P.op("vector", "tensor_single_scalar", out=phi8[:, slg], in_=gat[b][:, slg],
                     scalar=vecs[:, VEC["aim"] + cc:VEC["aim"] + cc + 1], op=ALU.mult)
            vB = P.einc("vector")
            P.wait("scalar", ES_V, vB)
            P.op("scalar", "activation", out=nrm8, in_=la8, func=AF.Exp, scale=2.0)
            P.selfbar("scalar")
            P.op("scalar", "activation", out=la8, in_=la8, func=AF.Exp)
            sC = P.op("scalar", "activation", out=nrm8, in_=nrm8, func=AF.Sqrt,
                      scale=-1.0, bias=1.0, sem=ES_S)
            sC_prev = sC
            if sE_prev is not None:
                P.wait("vector", ES_S, sE_prev)
            for cc in range(8):
                slg = slice(cc * TT, (cc + 1) * TT)
                P.op("vector", "tensor_tensor_scan", out=phi8[:, slg], data0=onet,
                     data1=phi8[:, slg], initial=carry[:, cc:cc + 1],
                     op0=ALU.mult, op1=ALU.add)
            P.selfbar("vector")
            for cc in range(8):
                P.op("vector", "tensor_copy", out=carry[:, cc:cc + 1],
                     in_=phi8[:, cc * TT + TT - 1:cc * TT + TT])
            P.op("vector", "tensor_single_scalar", out=m8, in_=phi8,
                 scalar=INV_2PI, op=ALU.mult)
            P.selfbar("vector")
            P.op("vector", "tensor_copy", out=mi8, in_=m8)
            P.selfbar("vector")
            P.op("vector", "tensor_copy", out=m8, in_=mi8)
            P.selfbar("vector")
            P.op("vector", "scalar_tensor_tensor", out=psis8, in0=m8,
                 scalar=NEG_2PI, in1=phi8, op0=ALU.mult, op1=ALU.add)
            P.op("vector", "tensor_scalar", out=m8, in0=phi8, scalar1=INV_2PI,
                 scalar2=0.25, op0=ALU.mult, op1=ALU.add)
            P.selfbar("vector")
            P.op("vector", "tensor_copy", out=mi8, in_=m8)
            P.selfbar("vector")
            P.op("vector", "tensor_copy", out=m8, in_=mi8)
            P.selfbar("vector")
            P.op("vector", "scalar_tensor_tensor", out=cosp8, in0=m8,
                 scalar=NEG_2PI, in1=phi8, op0=ALU.mult, op1=ALU.add)
            vD = P.einc("vector")
            P.wait("scalar", ES_V, vD)
            P.op("scalar", "activation", out=sinp8, in_=psis8, func=AF.Sin)
            sE = P.op("scalar", "activation", out=cosp8, in_=cosp8, func=AF.Sin,
                      bias=HALF_PI, sem=ES_S)
            sE_prev = sE
            P.wait("vector", ES_S, sE)
            if i >= 2:
                P.wait("vector", stq[b], rec[i - 2]["st"])
            P.op("vector", "tensor_tensor", out=tw8, in0=gxt[b], in1=nrm8, op=ALU.mult)
            P.selfbar("vector")
            P.op("vector", "tensor_tensor", out=xr8, in0=xt[b][:, 0:8 * TT],
                 in1=tw8, op=ALU.mult)
            P.op("vector", "tensor_tensor", out=xi8, in0=xt[b][:, 8 * TT:16 * TT],
                 in1=tw8, op=ALU.mult)
            P.selfbar("vector")
            P.op("vector", "tensor_tensor", out=ur8, in0=xr8, in1=cosp8, op=ALU.mult)
            P.op("vector", "tensor_tensor", out=tw8, in0=xi8, in1=sinp8, op=ALU.mult)
            P.selfbar("vector")
            P.op("vector", "tensor_tensor", out=ur8, in0=ur8, in1=tw8, op=ALU.add)
            P.op("vector", "tensor_tensor", out=tw8, in0=xi8, in1=cosp8, op=ALU.mult)
            P.selfbar("vector")
            P.op("vector", "tensor_tensor", out=xi8, in0=xr8, in1=sinp8, op=ALU.mult)
            P.selfbar("vector")
            P.op("vector", "tensor_tensor", out=xi8, in0=tw8, in1=xi8, op=ALU.subtract)
            P.selfbar("vector")
            for cc in range(8):
                slg = slice(cc * TT, (cc + 1) * TT)
                P.op("vector", "tensor_tensor_scan", out=ur8[:, slg], data0=la8[:, slg],
                     data1=ur8[:, slg], initial=carry[:, 8 + cc:9 + cc],
                     op0=ALU.mult, op1=ALU.add)
                P.op("vector", "tensor_tensor_scan", out=xi8[:, slg], data0=la8[:, slg],
                     data1=xi8[:, slg], initial=carry[:, 16 + cc:17 + cc],
                     op0=ALU.mult, op1=ALU.add)
            P.selfbar("vector")
            for cc in range(8):
                P.op("vector", "tensor_copy", out=carry[:, 8 + cc:9 + cc],
                     in_=ur8[:, cc * TT + TT - 1:cc * TT + TT])
                P.op("vector", "tensor_copy", out=carry[:, 16 + cc:17 + cc],
                     in_=xi8[:, cc * TT + TT - 1:cc * TT + TT])
            P.op("vector", "tensor_tensor", out=tw8, in0=ur8, in1=cosp8, op=ALU.mult)
            P.op("vector", "tensor_tensor", out=xr8, in0=xi8, in1=sinp8, op=ALU.mult)
            P.selfbar("vector")
            P.op("vector", "tensor_tensor", out=hb[b][:, 0:8 * TT], in0=tw8, in1=xr8,
                 op=ALU.subtract)
            P.selfbar("vector")
            P.op("vector", "tensor_tensor", out=tw8, in0=xi8, in1=cosp8, op=ALU.mult)
            P.op("vector", "tensor_tensor", out=xr8, in0=ur8, in1=sinp8, op=ALU.mult)
            P.selfbar("vector")
            P.op("vector", "tensor_tensor", out=hb[b][:, 8 * TT:16 * TT], in0=tw8, in1=xr8,
                 op=ALU.add)
            P.selfbar("vector")
            r["v_done"] = P.cnt[ES_V]
            r["s_done"] = P.cnt[ES_S]
            P.wait("sync", ES_V, r["v_done"])
            r["st"] = dma_split3(P, h_d[:, :, i * TT:(i + 1) * TT],
                                 hb[b].rearrange("p (c n) -> p c n", c=DC), stq[b], 2)
            rec[i] = r
        P.barrier()


def emit_p7c(P, ctx):
    emit_proj_fm(P, ctx, "L1_attn_W_out", "hsc", "s0", 2048, 2048,
                 bias_base=ctx["VEC"]["b_out"], out_f32=True, tag="p7c")



def emit_dequant(P, ctx, q_name, s_name, out_name, R, C, tag):
    nc = P.nc
    q_d = P.dram(q_name)
    s_d = P.dram(s_name)
    o_d = P.dram(out_name)
    idb = ctx["idb"]
    ES_T, ES_V, ES_S = P.esem["tensor"], P.esem["vector"], P.esem["scalar"]
    NCH = R // 128
    SC = C // 64
    with ExitStack() as st:
        qi = [st.enter_context(nc.sbuf_tensor(f"{tag}_q{i}", [128, C], I32))[:] for i in range(2)]
        sc_ = [st.enter_context(nc.sbuf_tensor(f"{tag}_s{i}", [128, SC], F32))[:] for i in range(2)]
        dq = [st.enter_context(nc.sbuf_tensor(f"{tag}_d{i}", [128, C], BF16))[:] for i in range(2)]
        ev = [st.enter_context(nc.sbuf_tensor(f"{tag}_e{i}", [128, C], BF16))[:] for i in range(2)]
        psT = [st.enter_context(nc.psum_tensor(f"{tag}_p{i}", [128, 2048], BF16))[:] for i in range(2)]
        ld = [P.pool_sem(), P.pool_sem()]
        stq = [P.pool_sem(), P.pool_sem()]
        rec = {}
        pT_copy = [None, None]
        ldt = {}

        def issue_ld(c):
            if c >= 2:
                P.wait("sync", ES_V, rec[c - 2]["dq"])
            dma_split2(P, qi[c % 2], q_d[c * 128:(c + 1) * 128, :], ld[c % 2], max(2, C // 1024))
            ldt[c] = P.op("sync", "dma_start", out=sc_[c % 2], in_=s_d[c * 128:(c + 1) * 128, :],
                          sem=ld[c % 2])

        issue_ld(0)
        for c in range(NCH):
            b = c % 2
            r = {}
            if c + 1 < NCH:
                issue_ld(c + 1)
            r["ld"] = ldt[c]
            P.wait("vector", ld[b], r["ld"])
            if c >= 2:
                P.wait("vector", ES_T, rec[c - 2]["tr"])
            r["dq"] = P.op("vector", "scalar_tensor_tensor",
                           out=dq[b].rearrange("p (a z) -> p a z", z=64),
                           in0=qi[b].rearrange("p (a z) -> p a z", z=64),
                           scalar=P.constf(-7.5),
                           in1=sc_[b].unsqueeze(2).broadcast_to([128, SC, 64]),
                           op0=ALU.add, op1=ALU.mult, sem=ES_V)
            P.wait("tensor", ES_V, r["dq"])
            if c >= 2:
                P.wait("tensor", ES_S, rec[c - 2]["ev"])
            nb = C // 2048
            for bb_ in range(nb):
                par = (c * nb + bb_) % 2
                pT = psT[par]
                if pT_copy[par] is not None:
                    P.wait("tensor", ES_S, pT_copy[par])
                for tc in range(16):
                    P.op("tensor", "transpose", pT[:, tc * 128:(tc + 1) * 128],
                         dq[b][:, (bb_ * 16 + tc) * 128:(bb_ * 16 + tc + 1) * 128], idb)
                r["tr"] = P.einc("tensor")
                P.wait("scalar", ES_T, r["tr"])
                if c >= 2 and bb_ == 0:
                    P.wait("scalar", stq[b], rec[c - 2]["st"])
                r["ev"] = P.op("scalar", "copy", out=ev[b][:, bb_ * 2048:(bb_ + 1) * 2048],
                               in_=pT, sem=ES_S)
                pT_copy[par] = r["ev"]
            P.wait("sync", ES_S, r["ev"])
            r["st"] = dma_split2(P, o_d[c], ev[b], stq[b], max(2, C // 2048))
            rec[c] = r
        P.barrier()


def emit_p8a(P, ctx):
    emit_dequant(P, ctx, "L1_ffn_w_up_q", "L1_ffn_w_up_scale", "wupT3", 8192, 2048, "p8a")


def emit_p8b(P, ctx):
    emit_dequant(P, ctx, "L1_ffn_w_down_q", "L1_ffn_w_down_scale", "wdnT3", 2048, 8192, "p8b")



def emit_p8c(P, ctx):
    nc = P.nc
    a_d = P.dram("L1_ffn_w_up_lora_a")
    x_d = P.dram("xnbf").rearrange("(c p) t -> p c t", p=128)
    xa_d = P.dram("xa")
    ES_T, ES_V, ES_S = P.esem["tensor"], P.esem["vector"], P.esem["scalar"]
    with ExitStack() as st:
        wa = st.enter_context(nc.sbuf_tensor("p8c_w", [128, DC * 16], BF16))[:]
        ws = st.enter_context(nc.sbuf_tensor("p8c_s", [128, DC * 16], F32))[:]
        xt = [st.enter_context(nc.sbuf_tensor(f"p8c_x{i}", [128, DC * 512], BF16))[:] for i in range(2)]
        ot = [st.enter_context(nc.sbuf_tensor(f"p8c_o{i}", [16, 512], F32))[:] for i in range(2)]
        ps = [st.enter_context(nc.psum_tensor(f"p8c_p{i}", [16, 512], F32))[:] for i in range(2)]
        wld = P.pool_sem()
        ld = [P.pool_sem(), P.pool_sem()]
        stq = [P.pool_sem(), P.pool_sem()]
        t = P.op("sync", "dma_start", out=ws.rearrange("p (c n) -> p c n", c=DC),
                 in_=a_d.rearrange("(c p) r -> p c r", p=128), sem=wld)
        P.wait("vector", wld, t)
        cv = P.op("vector", "tensor_copy", out=wa, in_=ws, sem=ES_V)
        P.wait("tensor", ES_V, cv)
        rec = {}
        ldt = {}

        def issue_ld(i):
            if i >= 2:
                P.wait("sync", ES_T, rec[i - 2]["mm"])
                P.wait("sync", stq[i % 2], rec[i - 2]["st"])
            ldt[i] = dma_split3(P, xt[i % 2].rearrange("p (c n) -> p c n", c=DC),
                                x_d[:, :, i * 512:(i + 1) * 512], ld[i % 2], 4)

        issue_ld(0)
        for i in range(8):
            b = i % 2
            r = {}
            if i + 1 < 8:
                issue_ld(i + 1)
            r["ld"] = ldt[i]
            P.wait("tensor", ld[b], r["ld"])
            if i >= 2:
                P.wait("tensor", ES_V, rec[i - 2]["ev"])
            for kc in range(DC):
                P.op("tensor", "matmul", ps[b], wa[:, kc * 16:(kc + 1) * 16],
                     xt[b][:, kc * 512:(kc + 1) * 512], start=(kc == 0), stop=(kc == DC - 1))
            r["mm"] = P.einc("tensor")
            P.wait("vector", ES_T, r["mm"])
            if i >= 2:
                P.wait("vector", stq[b], rec[i - 2]["st"])
            r["ev"] = P.op("vector", "tensor_copy", out=ot[b], in_=ps[b], sem=ES_V)
            P.wait("sync", ES_V, r["ev"])
            r["st"] = P.op("sync", "dma_start", out=xa_d[:, i * 512:(i + 1) * 512], in_=ot[b],
                           sem=stq[b])
            rec[i] = r
        P.barrier()



def emit_p9(P, ctx):
    nc = P.nc
    w_d = P.dram("wupT3")
    x_d = P.dram("xnbf").rearrange("(c p) t -> p c t", p=128)
    x2_d = P.dram("x2").rearrange("(c p) t -> p c t", p=128)
    xda_d = P.dram("xda")
    a_d = P.dram("L1_ffn_w_up_lora_a")
    lb_d = P.dram("L1_ffn_w_up_lora_b")
    wda_d = P.dram("L1_ffn_w_down_lora_a")
    VEC = ctx["VEC"]
    vecs = ctx["vecs"]
    ES_T, ES_V, ES_S = P.esem["tensor"], P.esem["vector"], P.esem["scalar"]
    with ExitStack() as st:
        xn = st.enter_context(nc.sbuf_tensor("p9_xn", [128, DC * S], BF16))[:]
        lb = st.enter_context(nc.sbuf_tensor("p9_lb", [16, 8192], BF16))[:]
        xab = st.enter_context(nc.sbuf_tensor("p9_xa", [16, S], BF16))[:]
        wda = st.enter_context(nc.sbuf_tensor("p9_wda", [128, 64 * 16], BF16))[:]
        xda = st.enter_context(nc.sbuf_tensor("p9_xda", [16, S], F32))[:]
        wda_a = st.enter_context(nc.sbuf_tensor("p9_waa", [128, DC * 16], BF16))[:]
        xab_ev = {}
        wub = [st.enter_context(nc.sbuf_tensor(f"p9_w{i}", [128, DC * 128], BF16))[:] for i in range(2)]
        x2t = [st.enter_context(nc.sbuf_tensor(f"p9_o{i}", [128, 512], BF16))[:] for i in range(4)]
        ps = [st.enter_context(nc.psum_tensor(f"p9_p{i}", [128, 512], F32))[:] for i in range(4)]
        psA = [st.enter_context(nc.psum_tensor(f"p9_a{i}", [16, 512], F32))[:] for i in range(2)]
        wld = [P.pool_sem(), P.pool_sem()]
        ld = P.pool_sem()
        stq = [P.pool_sem() for _ in range(4)]
        with ExitStack() as st2:
            stg = st2.enter_context(nc.sbuf_tensor("p9_stg", [128, 1024], F32))[:]
            for c8 in range(8):
                dma_split3(P, xn.rearrange("p (c n) -> p c n", c=DC)[:, :, c8 * 512:(c8 + 1) * 512],
                           x_d[:, :, c8 * 512:(c8 + 1) * 512], ld, 4)
            jobs = []
            for sl in range(8):
                jobs.append((lb[:, sl * 1024:(sl + 1) * 1024], lb_d[:, sl * 1024:(sl + 1) * 1024], 16))
            for dst, srcap, pp in jobs:
                t = P.op("sync", "dma_start", out=stg[0:pp, 0:dst.shape[1]], in_=srcap, sem=ld)
                P.wait("vector", ld, t)
                cv = P.op("vector", "tensor_copy", out=dst, in_=stg[0:pp, 0:dst.shape[1]], sem=ES_V)
                P.wait("sync", ES_V, cv)
            t = P.op("sync", "dma_start", out=stg.rearrange("p (c r) -> p c r", c=64),
                     in_=wda_d.rearrange("(c p) r -> p c r", p=128), sem=ld)
            P.wait("vector", ld, t)
            cv = P.op("vector", "tensor_copy", out=wda, in_=stg[:, 0:64 * 16], sem=ES_V)
            P.wait("sync", ES_V, cv)
            t = P.op("sync", "dma_start",
                     out=stg[:, 0:DC * 16].rearrange("p (c n) -> p c n", c=DC),
                     in_=a_d.rearrange("(c p) r -> p c r", p=128), sem=ld)
            P.wait("vector", ld, t)
            cv = P.op("vector", "tensor_copy", out=wda_a, in_=stg[:, 0:DC * 16], sem=ES_V)
            P.op("vector", "memset", xda, 0.0, sem=ES_V)
            vinit = P.cnt[ES_V]
            P.wait("tensor", ES_V, vinit)
            for i8 in range(8):
                pa = psA[i8 % 2]
                if i8 >= 2:
                    P.wait("tensor", ES_V, xab_ev[i8 - 2])
                for kc in range(DC):
                    P.op("tensor", "matmul", pa, wda_a[:, kc * 16:(kc + 1) * 16],
                         xn[:, kc * S + i8 * 512: kc * S + (i8 + 1) * 512],
                         start=(kc == 0), stop=(kc == DC - 1))
                tmm = P.einc("tensor")
                P.wait("vector", ES_T, tmm)
                xab_ev[i8] = P.op("vector", "tensor_copy",
                                  out=xab[:, i8 * 512:(i8 + 1) * 512], in_=pa, sem=ES_V)
            P.wait("tensor", ES_V, P.cnt[ES_V])
        rec = {}
        xda_add = {}
        xda_mm = {}
        ev_s = {}
        st_s = {}
        uinfo = {}
        u = 0
        wldt = {}

        def issue_w9(h2):
            if h2 >= 2:
                P.wait("sync", ES_T, rec[h2 - 2])
            wldt[h2] = dma_split2(P, wub[h2 % 2], w_d[h2], wld[h2 % 2], 2)

        issue_w9(0)
        for hc in range(64):
            bw = hc % 2
            if hc + 1 < 64:
                issue_w9(hc + 1)
            P.wait("tensor", wld[bw], wldt[hc])
            for tt in range(8):
                bu = u % 4
                ba = u % 2
                pb = ps[bu]
                if u >= 4:
                    P.wait("tensor", ES_S, ev_s.get(u - 4, 0))
                for kc in range(DC):
                    P.op("tensor", "matmul", pb, wub[bw][:, kc * 128:(kc + 1) * 128],
                         xn[:, kc * S + tt * 512: kc * S + tt * 512 + 512],
                         start=(kc == 0), stop=False)
                P.op("tensor", "matmul", pb, lb[:, hc * 128:(hc + 1) * 128],
                     xab[:, tt * 512:(tt + 1) * 512], start=False, stop=True)
                mm1 = P.einc("tensor")
                rec[hc] = mm1
                P.wait("scalar", ES_T, mm1)
                if u >= 4:
                    P.wait("scalar", stq[bu], st_s.get(u - 4, 0))
                    P.wait("scalar", ES_T, xda_mm.get(u - 4, 0))
                ev = P.op("scalar", "activation", out=x2t[bu], in_=pb, func=AF.Relu,
                          bias=vecs[:, VEC["b_up"] + hc:VEC["b_up"] + hc + 1], sem=ES_S)
                ev_s[u] = ev
                uinfo[u] = (hc, tt)
                if u >= 1:
                    emit_p9_xda(P, u - 1, uinfo, ev_s, xda_mm, xda_add, psA, wda, x2t, xda)
                P.wait("sync", ES_S, ev)
                st_s[u] = P.op("sync", "dma_start",
                               out=x2_d[:, hc, tt * 512:(tt + 1) * 512], in_=x2t[bu],
                               sem=stq[bu])
                u += 1
        emit_p9_xda(P, u - 1, uinfo, ev_s, xda_mm, xda_add, psA, wda, x2t, xda)
        P.wait("sync", ES_V, P.cnt[ES_V])
        P.op("sync", "dma_start", out=xda_d, in_=xda, sem=stq[0])
        P.barrier()


def emit_p9_xda(P, w, uinfo, ev_s, xda_mm, xda_add, psA, wda, x2t, xda):
    ES_T, ES_V, ES_S = P.esem["tensor"], P.esem["vector"], P.esem["scalar"]
    hc, tt = uinfo[w]
    P.wait("tensor", ES_S, ev_s[w])
    if w >= 2:
        P.wait("tensor", ES_V, xda_add.get(w - 2, 0))
    xmm = P.op("tensor", "matmul", psA[w % 2], wda[:, hc * 16:(hc + 1) * 16],
               x2t[w % 4], start=True, stop=True, sem=ES_T)
    xda_mm[w] = xmm
    P.wait("vector", ES_T, xmm)
    xda_add[w] = P.op("vector", "tensor_tensor", out=xda[:, tt * 512:(tt + 1) * 512],
                      in0=xda[:, tt * 512:(tt + 1) * 512], in1=psA[w % 2], op=ALU.add,
                      sem=ES_V)



def emit_p10(P, ctx):
    nc = P.nc
    w_d = P.dram("wdnT3")
    x2_d = P.dram("x2").rearrange("(c p) t -> p c t", p=128)
    x1_d = P.dram("x1").rearrange("(c p) t -> p c t", p=128)
    s2_d = P.dram("s0").rearrange("(c p) t -> p c t", p=128)
    xda_d = P.dram("xda")
    lb_d = P.dram("L1_ffn_w_down_lora_b")
    out_d = P.dram("out")
    idf = ctx["idf"]
    VEC = ctx["VEC"]
    vecs = ctx["vecs"]
    ES_T, ES_V, ES_S = P.esem["tensor"], P.esem["vector"], P.esem["scalar"]
    HC = 64
    with ExitStack() as st:
        x2t = [st.enter_context(nc.sbuf_tensor(f"p10_x{i}", [128, HC * 512], BF16))[:] for i in range(2)]
        wdb = [st.enter_context(nc.sbuf_tensor(f"p10_w{i}", [128, HC * 128], BF16))[:] for i in range(2)]
        lb = st.enter_context(nc.sbuf_tensor("p10_lb", [16, 2048], BF16))[:]
        xdab = st.enter_context(nc.sbuf_tensor("p10_xd", [16, S], BF16))[:]
        rt = [st.enter_context(nc.sbuf_tensor(f"p10_r{i}", [128, 512], F32))[:] for i in range(2)]
        qt2 = [st.enter_context(nc.sbuf_tensor(f"p10_q{i}", [128, 512], F32))[:] for i in range(2)]
        y2 = st.enter_context(nc.sbuf_tensor("p10_y2", [128, 512], F32))[:]
        g2 = st.enter_context(nc.sbuf_tensor("p10_g2", [128, 512], F32))[:]
        xo = [st.enter_context(nc.sbuf_tensor(f"p10_o{i}", [128, 512], F32))[:] for i in range(2)]
        ev = [st.enter_context(nc.sbuf_tensor(f"p10_e{i}", [128, 512], F32))[:] for i in range(2)]
        ps = [st.enter_context(nc.psum_tensor(f"p10_p{i}", [128, 512], F32))[:] for i in range(4)]
        psT = [st.enter_context(nc.psum_tensor(f"p10_t{i}", [128, 512], F32))[:] for i in range(2)]
        ld = [P.pool_sem(), P.pool_sem()]
        wld = [P.pool_sem(), P.pool_sem()]
        rld = [P.pool_sem(), P.pool_sem()]
        stq = [P.pool_sem(), P.pool_sem()]
        sld = P.pool_sem()
        with ExitStack() as st2:
            stg = st2.enter_context(nc.sbuf_tensor("p10_stg", [16, 2048], F32))[:]
            jobs = [(lb, lb_d)]
            for sl in range(2):
                jobs.append((xdab[:, sl * 2048:(sl + 1) * 2048], xda_d[:, sl * 2048:(sl + 1) * 2048]))
            for dst, srcap in jobs:
                t = P.op("sync", "dma_start", out=stg[:, 0:dst.shape[1]], in_=srcap, sem=sld)
                P.wait("vector", sld, t)
                cv = P.op("vector", "tensor_copy", out=dst, in_=stg[:, 0:dst.shape[1]], sem=ES_V)
                P.wait("sync", ES_V, cv)
            vinit = P.cnt[ES_V]
            P.wait("tensor", ES_V, vinit)
        rec = {}
        ev_v = {}
        tr_d = {}
        cp_d = {}
        st_d = {}
        wcons = {}
        y2ev = {}
        sil = {}
        uinfo = {}
        x2ld = {}
        wldt = {}

        def issue_x2(t2):
            if t2 >= 2:
                P.wait("sync", ES_T, rec[t2 - 2])
            x2ld[t2] = dma_split3(P, x2t[t2 % 2].rearrange("p (c n) -> p c n", c=HC),
                                  x2_d[:, :, t2 * 512:(t2 + 1) * 512], ld[t2 % 2], 8)

        def issue_w(u2):
            if u2 >= 2:
                P.wait("sync", ES_T, wcons.get(u2 - 2, 0))
            wldt[u2] = dma_split2(P, wdb[u2 % 2], w_d[u2 % 16], wld[u2 % 2], 8)

        issue_x2(0)
        issue_w(0)
        u = 0
        for tt in range(8):
            b = tt % 2
            if tt + 1 < 8:
                issue_x2(tt + 1)
            P.wait("tensor", ld[b], x2ld[tt])
            for dc in range(16):
                uinfo[u] = (tt, dc)
                bw = u % 2
                if u + 1 < 128:
                    issue_w(u + 1)
                if u >= 2:
                    P.wait("sync", ES_V, ev_v.get(u - 2, 0))
                rtg = P.op("sync", "dma_start", out=rt[u % 2],
                           in_=s2_d[:, dc, tt * 512:(tt + 1) * 512], sem=rld[u % 2])
                rtg = P.op("sync", "dma_start", out=qt2[u % 2],
                           in_=x1_d[:, dc, tt * 512:(tt + 1) * 512], sem=rld[u % 2])
                P.wait("tensor", wld[bw], wldt[u])
                pb = ps[u % 4]
                if u >= 4:
                    P.wait("tensor", ES_V, y2ev.get(u - 4, 0))
                for kc in range(HC):
                    P.op("tensor", "matmul", pb, wdb[bw][:, kc * 128:(kc + 1) * 128],
                         x2t[b][:, kc * 512:(kc + 1) * 512], start=(kc == 0), stop=False)
                P.op("tensor", "matmul", pb, lb[:, dc * 128:(dc + 1) * 128],
                     xdab[:, tt * 512:(tt + 1) * 512], start=False, stop=True)
                mm = P.einc("tensor")
                rec[tt] = mm
                wcons[u] = mm
                P.wait("vector", ES_T, mm)
                if u >= 1:
                    P.wait("vector", ES_S, sil.get(u - 1, 0))
                y2ev[u] = P.op("vector", "tensor_single_scalar", out=y2, in_=pb,
                               scalar=vecs[:, VEC["b_down"] + dc:VEC["b_down"] + dc + 1],
                               op=ALU.add, sem=ES_V)
                P.wait("scalar", ES_V, y2ev[u])
                sil[u] = P.op("scalar", "activation", out=g2, in_=y2, func=AF.Silu, sem=ES_S)
                P.wait("vector", ES_S, sil[u])
                P.wait("vector", rld[u % 2], P.cnt[rld[u % 2]])
                P.op("vector", "tensor_tensor", out=g2, in0=rt[u % 2], in1=g2, op=ALU.mult)
                P.selfbar("vector")
                if u >= 2:
                    P.wait("vector", ES_T, tr_d.get(u - 2, 0))
                ev_v[u] = P.op("vector", "tensor_tensor", out=xo[u % 2], in0=qt2[u % 2],
                               in1=g2, op=ALU.add, sem=ES_V)
                u += 1
                if u >= 2:
                    emit_p10_tail(P, ctx, u - 2, uinfo[u - 2], psT, xo, ev, out_d, stq,
                                  tr_d, cp_d, st_d, ev_v)
        emit_p10_tail(P, ctx, u - 1, uinfo[u - 1], psT, xo, ev, out_d, stq,
                      tr_d, cp_d, st_d, ev_v)
        P.barrier()


def emit_p10_tail(P, ctx, u, info, psT, xo, ev, out_d, stq, tr_d, cp_d, st_d, ev_v):
    idf = ctx["idf"]
    ES_T, ES_V, ES_S = P.esem["tensor"], P.esem["vector"], P.esem["scalar"]
    tt, dc = info
    P.wait("tensor", ES_V, ev_v[u])
    if u >= 2:
        P.wait("tensor", ES_S, cp_d.get(u - 2, 0))
    for c4 in range(4):
        P.op("tensor", "transpose", psT[u % 2][:, c4 * 128:(c4 + 1) * 128],
             xo[u % 2][:, c4 * 128:(c4 + 1) * 128], idf)
    tr_d[u] = P.einc("tensor")
    P.wait("scalar", ES_T, tr_d[u])
    if u >= 2:
        P.wait("scalar", stq[u % 2], st_d.get(u - 2, 0))
    cp_d[u] = P.op("scalar", "copy", out=ev[u % 2], in_=psT[u % 2], sem=ES_S)
    P.wait("sync", ES_S, cp_d[u])
    st_d[u] = P.op("sync", "dma_start",
                   out=out_d[tt * 512:(tt + 1) * 512, dc * 128:(dc + 1) * 128]
                   .rearrange("(c p) f -> p c f", p=128),
                   in_=ev[u % 2].rearrange("p (c f) -> p c f", c=4), sem=stq[u % 2])



PHASES = ["p0", "p1", "p2q", "p2k", "p2v", "p3", "p4",
          "p5a", "p5b", "p5c", "p5d", "p5e", "p6",
          "p7a", "p7g", "p7b", "p7c", "p8a", "p8b", "p9", "p10"]

_cache = {}


def _build():
    if "nc" not in _cache:
        P = Prog()
        ctx = emit_prep(P)
        for ph in PHASES:
            globals()["emit_" + ph](P, ctx)
        _cache["nc"] = P.build()
        _cache["P"] = P
    return _cache["P"], _cache["nc"]


def kernel(**inputs):
    P, nc = _build()
    x = np.ascontiguousarray(np.asarray(inputs["L0_attn_x"], np.float32))
    weights = {nm: np.ascontiguousarray(np.asarray(inputs[nm])) for nm in WEIGHT_SPECS
               if nm in P.drams}
    maps = []
    for c in range(NCORES):
        m = dict(weights)
        m["x_in"] = np.ascontiguousarray(x[c])
        maps.append(m)
    res = run_bass_kernel_spmd(nc, maps, core_ids=list(range(NCORES)))
    return np.stack([np.asarray(r["out"], np.float32) for r in res.results], axis=0)


# version 20
# speedup vs baseline: 1.0344x; 1.0344x over previous
import numpy as np
import ml_dtypes
from contextlib import ExitStack
import concourse.bass as bass
import concourse.mybir as mybir
from concourse.bass_utils import run_bass_kernel_spmd

F32 = mybir.dt.float32
BF16 = mybir.dt.bfloat16
I32 = mybir.dt.int32
AF = mybir.ActivationFunctionType
ALU = mybir.AluOpType
AX = mybir.AxisListType.X

B, S, D = 8, 4096, 2048
NCORES = 8
DC = D // 128
TC = S // 128
BIG_NEG = -1.0e30
MAX_CONSTS = 32

ENG = ("sync", "tensor", "vector", "scalar", "gpsimd")

WEIGHT_SPECS = {
    "L0_attn_qkv": ([2048, 6144], F32),
    "L0_attn_o_proj": ([2048, 2048], F32),
    "L0_attn_gamma": ([2048], F32),
    "L0_attn_beta": ([2048], F32),
    "L0_ffn_W_gate": ([16, 2048], F32),
    "L0_ffn_W_qkv": ([6144, 2048], F32),
    "L1_attn_a_real_param": ([1024], F32),
    "L1_attn_a_imag_param": ([1024], F32),
    "L1_attn_W_in": ([2048, 2048], F32),
    "L1_attn_b_in": ([2048], F32),
    "L1_attn_Wg_x": ([8, 256, 128], F32),
    "L1_attn_bg_x": ([1024], F32),
    "L1_attn_Wg_a": ([8, 256, 128], F32),
    "L1_attn_bg_a": ([1024], F32),
    "L1_attn_W_out": ([2048, 2048], F32),
    "L1_attn_b_out": ([2048], F32),
    "L1_ffn_w_up_q": ([8192, 2048], I32),
    "L1_ffn_w_up_scale": ([8192, 32], F32),
    "L1_ffn_b_up": ([8192], F32),
    "L1_ffn_w_up_lora_a": ([2048, 16], F32),
    "L1_ffn_w_up_lora_b": ([16, 8192], F32),
    "L1_ffn_w_down_q": ([2048, 8192], I32),
    "L1_ffn_w_down_scale": ([2048, 128], F32),
    "L1_ffn_b_down": ([2048], F32),
    "L1_ffn_w_down_lora_a": ([8192, 16], F32),
    "L1_ffn_w_down_lora_b": ([16, 2048], F32),
}

SCRATCH_SPECS = {
    "x_in": ([S, D], F32),
    "out": ([S, D], F32),
    "xres": ([D, S], F32),
    "xn32": ([D, S], F32),
    "xnbf": ([D, S], BF16),
    "xnorm": ([D, S], BF16),
    "q0": ([D, S], BF16),
    "k0": ([D, S], BF16),
    "v0": ([S, D], BF16),
    "att0": ([D, S], BF16),
    "s0": ([D, S], F32),
    "hot": ([S, 16], F32),
    "g1v": ([S, 16], F32),
    "kshF": ([128, S], BF16),
    "vshT": ([S, 128], BF16),
    "wqkT3": ([48, 128, 2048], BF16),
    "y0n": ([D, S], F32),
    "x1": ([D, S], F32),
    "xh": ([D, S], F32),
    "glx": ([1024, S], F32),
    "gla": ([1024, S], F32),
    "hsc": ([D, S], BF16),
    "wupT3": ([64, 128, 2048], BF16),
    "wdnT3": ([16, 128, 8192], BF16),
    "xa": ([16, S], F32),
    "xda": ([16, S], F32),
    "x2": ([8192, S], BF16),
    "st2": ([2, S], F32),
}


class Prog:
    def __init__(self, ext_in=(), ext_out=()):
        self.nc = bass.Bass("TRN2", target_bir_lowering=False)
        self.ext_in = set(ext_in) | {"x_in"} | set(WEIGHT_SPECS)
        self.ext_out = set(ext_out) | {"out"}
        self.drams = {}
        self.ops = {e: [] for e in ENG}
        self.cnt = {}
        self.consts = {}
        self.const_regd = set()
        self.bar = self.new_sem("bar")
        self.esem = {e: self.new_sem("es_" + e) for e in ENG}
        self._sem_pool = [self.new_sem(f"sp{i}") for i in range(24)]
        self._sem_next = 0
        self._since_bar = {e: False for e in ENG}
        self.cpool = self.nc.alloc_sbuf_tensor("cpool", [128, MAX_CONSTS], F32)[:]

    def new_sem(self, nm):
        h = self.nc.alloc_semaphore(nm)
        self.cnt[h] = 0
        return h

    def pool_sem(self):
        h = self._sem_pool[self._sem_next % len(self._sem_pool)]
        self._sem_next += 1
        return h

    def dram(self, nm):
        if nm in self.drams:
            return self.drams[nm]
        shape, dtype = WEIGHT_SPECS.get(nm) or SCRATCH_SPECS[nm]
        kind = ("ExternalInput" if nm in self.ext_in else
                "ExternalOutput" if nm in self.ext_out else "Internal")
        t = self.nc.dram_tensor(nm, list(shape), dtype, kind=kind).ap()
        self.drams[nm] = t
        return t

    def op(self, eng, method, *args, sem=None, **kw):
        is_dma = method.startswith("dma")
        if is_dma and sem is None:
            sem = self.pool_sem()
        incs = []
        ret = None
        if sem is not None:
            n = 16 if is_dma else 1
            self.cnt[sem] += n
            incs.append((sem, n))
            ret = self.cnt[sem]
        self.ops[eng].append(["op", method, args, kw, incs, is_dma])
        self._since_bar[eng] = True
        return ret

    def wait(self, eng, sem, tgt):
        self.ops[eng].append(["wait", sem, tgt])

    def _last_op(self, eng):
        for ent in reversed(self.ops[eng]):
            if ent[0] == "op":
                return ent
        return None

    def _milestone(self, eng, sem):
        ent = self._last_op(eng)
        assert ent is not None
        if not ent[4]:
            n = 16 if ent[5] else 1
            self.cnt[sem] += n
            ent[4].append((sem, n))
            return self.cnt[sem]
        if ent[5]:
            s0, n0 = ent[4][0]
            self.wait(eng, s0, self.cnt[s0])
            self.cnt[sem] += 1
            self.ops[eng].append(["op", "sem_inc", (sem, 1), {}, [], False])
            return self.cnt[sem]
        self.cnt[sem] += 1
        self.ops[eng].append(["op", "drain", (), {}, [(sem, 1)], False])
        return self.cnt[sem]

    def einc(self, eng):
        return self._milestone(eng, self.esem[eng])

    def selfbar(self, eng):
        tgt = self.einc(eng)
        self.wait(eng, self.esem[eng], tgt)

    def barrier(self):
        for e in ENG:
            if not self._since_bar[e]:
                continue
            self._milestone(e, self.bar)
        tgt = self.cnt[self.bar]
        for e in ENG:
            self.wait(e, self.bar, tgt)
        self._since_bar = {e: False for e in ENG}

    def const(self, v):
        v = float(v)
        if v not in self.consts:
            assert len(self.consts) < MAX_CONSTS
            self.consts[v] = len(self.consts)
        return self.cpool[:, self.consts[v]:self.consts[v] + 1]

    def constf(self, v):
        v = float(v)
        ap = self.const(v)
        if v not in self.const_regd:
            self.nc.const_aps.register(F32, v, ap)
            self.const_regd.add(v)
        return v

    def inline(self, name, arr):
        return self.nc.inline_tensor(np.ascontiguousarray(arr), name=name).ap()

    def build(self):
        for h in self._sem_pool:
            if self.cnt[h] > 0:
                self.wait("sync", h, self.cnt[h])
        self.ops["sync"].append(["op", "nop", (), {}, [], False])
        self._since_bar["sync"] = True
        self.barrier()
        vals = np.zeros(MAX_CONSTS, np.float32)
        for v, i in self.consts.items():
            vals[i] = v
        cp_d = self.inline("cpool_d", np.tile(vals, (128, 1)))
        csem = self.new_sem("csem")
        self.ops["sync"].insert(0, ["op", "dma_start", (), {"out": self.cpool, "in_": cp_d},
                                    [(csem, 16)], True])
        for e in ENG:
            if any(k[0] == "op" for k in self.ops[e]):
                self.ops[e].insert(1 if e == "sync" else 0, ["wait", csem, 16])
        with self.nc.Block() as block:
            for e in ENG:
                entries = self.ops[e]
                if not any(k[0] == "op" for k in entries):
                    continue
                def runner(eng, entries=entries):
                    for ent in entries:
                        if ent[0] == "wait":
                            eng.wait_ge(ent[1], ent[2])
                        else:
                            _, m, args, kw, incs, _ = ent
                            inst = getattr(eng, m)(*args, **kw)
                            for (s, n) in incs:
                                if inst is None:
                                    raise RuntimeError(f"op {m} returned None but has incs")
                                inst = inst.then_inc(s, n)
                getattr(block, e)(runner)
        return self.nc

    def n_ops(self):
        return {e: sum(1 for k in self.ops[e] if k[0] == "op") for e in ENG}


def dma_split3(P, out3, in3, sem, ns, eng="sync"):
    ns = 1
    C = out3.shape[1]
    ns = max(1, min(ns, C))
    step = C // ns
    t = None
    for k in range(ns):
        sl = slice(k * step, (k + 1) * step if k < ns - 1 else C)
        t = P.op(eng, "dma_start", out=out3[:, sl, :], in_=in3[:, sl, :], sem=sem)
    return t


def dma_split2(P, out2, in2, sem, ns, eng="sync"):
    ns = 1
    N = out2.shape[-1]
    ns = max(1, min(ns, N))
    step = N // ns
    t = None
    for k in range(ns):
        sl = slice(k * step, (k + 1) * step if k < ns - 1 else N)
        t = P.op(eng, "dma_start", out=out2[:, sl], in_=in2[:, sl], sem=sem)
    return t



def emit_prep(P):
    nc = P.nc
    ctx = {}
    idb = nc.alloc_sbuf_tensor("idb", [128, 128], BF16)[:]
    idf = nc.alloc_sbuf_tensor("idf", [128, 128], F32)[:]
    iota16 = nc.alloc_sbuf_tensor("iota16", [128, 16], F32)[:]
    ones_col = nc.alloc_sbuf_tensor("ones_col", [128, 1], BF16)[:]
    ones_row = nc.alloc_sbuf_tensor("ones_row", [1, 128], BF16)[:]
    ones_row_f = nc.alloc_sbuf_tensor("ones_row_f", [1, 128], F32)[:]
    vecs = nc.alloc_sbuf_tensor("vecs", [128, 192], F32)[:]
    ctx.update(idb=idb, idf=idf, iota16=iota16, ones_col=ones_col, ones_row=ones_row,
               ones_row_f=ones_row_f, vecs=vecs)

    VEC = {"gamma": 0, "beta": 16, "b_in": 32, "b_out": 48, "b_down": 64,
           "bg_x": 80, "bg_a": 88, "sp": 96, "aim": 104, "b_up": 112}
    ctx["VEC"] = VEC
    ctx["vec"] = lambda nm: vecs[:, VEC[nm]:VEC[nm] + 1]

    ld = P.pool_sem()
    P.op("sync", "dma_start", out=idb, in_=P.inline("idb_d", np.eye(128, dtype=np.float32).astype(ml_dtypes.bfloat16)), sem=ld)
    P.op("sync", "dma_start", out=idf, in_=P.inline("idf_d", np.eye(128, dtype=np.float32)), sem=ld)
    P.op("sync", "dma_start", out=iota16, in_=P.inline("iota_d", np.tile(np.arange(16, dtype=np.float32), (128, 1))), sem=ld)
    P.op("sync", "dma_start", out=ones_col, in_=P.inline("ones_col_d", np.ones((128, 1), np.float32).astype(ml_dtypes.bfloat16)), sem=ld)
    P.op("sync", "dma_start", out=ones_row, in_=P.inline("ones_row_d", np.ones((1, 128), np.float32).astype(ml_dtypes.bfloat16)), sem=ld)
    P.op("sync", "dma_start", out=ones_row_f, in_=P.inline("ones_row_f_d", np.ones((1, 128), np.float32)), sem=ld)

    vjobs = [
        ("L0_attn_gamma", 2048, "gamma"), ("L0_attn_beta", 2048, "beta"),
        ("L1_attn_b_in", 2048, "b_in"), ("L1_attn_b_out", 2048, "b_out"),
        ("L1_ffn_b_down", 2048, "b_down"), ("L1_attn_bg_x", 1024, "bg_x"),
        ("L1_attn_bg_a", 1024, "bg_a"), ("L1_attn_a_real_param", 1024, "sp"),
        ("L1_attn_a_imag_param", 1024, "aim"), ("L1_ffn_b_up", 8192, "b_up"),
    ]
    with ExitStack() as st:
        vload = st.enter_context(nc.sbuf_tensor("prep_vload", [64, 10 * 128], F32))[:]
        t1 = st.enter_context(nc.sbuf_tensor("prep_t1", [128, 24], F32))[:]
        ps = st.enter_context(nc.psum_tensor("prep_ps", [128, 176], F32))[:]
        off = 0
        for nm, n, dst in vjobs:
            r = n // 128
            src = P.dram(nm).rearrange("(a b) -> a b", b=128)
            tld = P.op("sync", "dma_start", out=vload[0:r, off:off + 128], in_=src, sem=ld)
            off += 128
        P.wait("tensor", ld, tld)
        off, pcol = 0, 0
        cols = {}
        for nm, n, dst in vjobs:
            r = n // 128
            P.op("tensor", "transpose", ps[:, pcol:pcol + r], vload[0:r, off:off + 128],
                 idf[0:r, 0:r])
            cols[dst] = (pcol, r)
            off += 128
            pcol += r
        ttgt = P.einc("tensor")
        P.wait("scalar", P.esem["tensor"], ttgt)
        for nm, n, dst in vjobs:
            pcol, r = cols[dst]
            P.op("scalar", "copy", out=vecs[:, VEC[dst]:VEC[dst] + r], in_=ps[:, pcol:pcol + r])
        sp = vecs[:, VEC["sp"]:VEC["sp"] + 8]
        P.selfbar("scalar")
        P.op("scalar", "activation", out=t1[:, 0:8], in_=sp, func=AF.Abs)
        P.selfbar("scalar")
        P.op("scalar", "activation", out=t1[:, 8:16], in_=t1[:, 0:8], func=AF.Exp, scale=-1.0)
        P.selfbar("scalar")
        stgt = P.op("scalar", "activation", out=t1[:, 0:8], in_=t1[:, 8:16], func=AF.Ln,
                    bias=1.0, sem=P.esem["scalar"])
        P.wait("vector", P.esem["scalar"], stgt)
        P.op("vector", "tensor_scalar_max", out=t1[:, 16:24], in0=sp, scalar1=0.0)
        P.selfbar("vector")
        P.op("vector", "tensor_tensor", out=sp, in0=t1[:, 0:8], in1=t1[:, 16:24], op=ALU.add)
        P.barrier()
    return ctx


def issue_p0_ld(P, i, xt, x_in, ld, rec, ldt0):
    if i >= 2:
        P.wait("sync", P.esem["vector"], rec[i - 2]["v_ev"])
        P.wait("sync", P.esem["scalar"], rec[i - 2]["s_evb"])
    ldt0[i] = dma_split2(P, xt[i % 2], x_in[i * 128:(i + 1) * 128, :], ld[i % 2], 4)



def emit_p0(P, ctx):
    nc = P.nc
    x_in = P.dram("x_in")
    xres = P.dram("xres").rearrange("(c p) t -> p c t", p=128)
    xn32 = P.dram("xn32").rearrange("(c p) t -> p c t", p=128)
    xnbf = P.dram("xnbf").rearrange("(c p) t -> p c t", p=128)
    st2 = P.dram("st2")
    idf = ctx["idf"]
    inv_d = P.const(1.0 / D)
    eps = P.const(1e-5)
    with ExitStack() as st:
        xt = [st.enter_context(nc.sbuf_tensor(f"p0_x{i}", [128, D], F32))[:] for i in range(2)]
        xn = [st.enter_context(nc.sbuf_tensor(f"p0_n{i}", [128, D], F32))[:] for i in range(2)]
        sq = st.enter_context(nc.sbuf_tensor("p0_sq", [128, D], F32))[:]
        stats = [st.enter_context(nc.sbuf_tensor(f"p0_st{i}", [128, 16], F32))[:] for i in range(2)]
        st2sb = [st.enter_context(nc.sbuf_tensor(f"p0_s2{i}", [2, 128], F32))[:] for i in range(2)]
        ev = [st.enter_context(nc.sbuf_tensor(f"p0_e{i}", [128, D], F32))[:] for i in range(2)]
        evn = [st.enter_context(nc.sbuf_tensor(f"p0_f{i}", [128, D], F32))[:] for i in range(2)]
        evb = [st.enter_context(nc.sbuf_tensor(f"p0_g{i}", [128, D], BF16))[:] for i in range(2)]
        psx = st.enter_context(nc.psum_tensor("p0_psx", [128, D], F32))[:]
        psn = st.enter_context(nc.psum_tensor("p0_psn", [128, D], F32))[:]
        ld = [P.pool_sem(), P.pool_sem()]
        stq = [P.pool_sem(), P.pool_sem()]
        rec = {}
        ldt0 = {}
        for i in range(TC):
            b = i % 2
            r = {}
            if i == 0:
                issue_p0_ld(P, 0, xt, x_in, ld, rec, ldt0)
            if i + 1 < TC:
                issue_p0_ld(P, i + 1, xt, x_in, ld, rec, ldt0)
            r["ld"] = ldt0[i]
            P.wait("vector", ld[b], r["ld"])
            P.op("vector", "reduce_sum", out=stats[b][:, 0:1], in_=xt[b], axis=AX)
            P.wait("scalar", ld[b], r["ld"])
            r["s_sq"] = P.op("scalar", "activation", out=sq, in_=xt[b], func=AF.Square,
                             accum_out=stats[b][:, 1:2], sem=P.esem["scalar"])
            P.wait("vector", P.esem["scalar"], r["s_sq"])
            P.selfbar("vector")
            P.op("vector", "tensor_single_scalar", out=stats[b][:, 2:3], in_=stats[b][:, 0:1],
                 scalar=inv_d, op=ALU.mult)
            P.op("vector", "tensor_single_scalar", out=stats[b][:, 3:4], in_=stats[b][:, 1:2],
                 scalar=inv_d, op=ALU.mult)
            P.selfbar("vector")
            P.op("vector", "tensor_tensor", out=stats[b][:, 4:5], in0=stats[b][:, 2:3],
                 in1=stats[b][:, 2:3], op=ALU.mult)
            P.selfbar("vector")
            r["v_var"] = P.op("vector", "tensor_tensor", out=stats[b][:, 5:6], in0=stats[b][:, 3:4],
                              in1=stats[b][:, 4:5], op=ALU.subtract, sem=P.esem["vector"])
            P.wait("scalar", P.esem["vector"], r["v_var"])
            r["s_std"] = P.op("scalar", "activation", out=stats[b][:, 6:7], in_=stats[b][:, 5:6],
                              func=AF.Sqrt, bias=eps, sem=P.esem["scalar"])
            P.wait("vector", P.esem["scalar"], r["s_std"])
            P.op("vector", "reciprocal", out=stats[b][:, 7:8], in_=stats[b][:, 6:7])
            P.selfbar("vector")
            if i >= 2:
                P.wait("vector", P.esem["tensor"], rec[i - 2]["t_all"])
            r["v_xn"] = P.op("vector", "tensor_scalar", out=xn[b], in0=xt[b],
                             scalar1=stats[b][:, 2:3], scalar2=stats[b][:, 7:8],
                             op0=ALU.subtract, op1=ALU.mult, sem=P.esem["vector"])
            P.op("vector", "reduce_sum", out=stats[b][:, 8:9], in_=xn[b], axis=AX)
            P.wait("scalar", P.esem["vector"], r["v_xn"])
            r["s_sq2"] = P.op("scalar", "activation", out=sq, in_=xn[b], func=AF.Square,
                              accum_out=stats[b][:, 9:10], sem=P.esem["scalar"])
            P.wait("vector", P.esem["scalar"], r["s_sq2"])
            P.selfbar("vector")
            P.op("vector", "tensor_single_scalar", out=stats[b][:, 14:15], in_=stats[b][:, 8:9],
                 scalar=inv_d, op=ALU.mult)
            P.op("vector", "tensor_single_scalar", out=stats[b][:, 10:11], in_=stats[b][:, 9:10],
                 scalar=inv_d, op=ALU.mult)
            P.selfbar("vector")
            P.op("vector", "tensor_tensor", out=stats[b][:, 11:12], in0=stats[b][:, 14:15],
                 in1=stats[b][:, 14:15], op=ALU.mult)
            P.selfbar("vector")
            r["v_var2"] = P.op("vector", "tensor_tensor", out=stats[b][:, 12:13], in0=stats[b][:, 10:11],
                               in1=stats[b][:, 11:12], op=ALU.subtract, sem=P.esem["vector"])
            P.wait("scalar", P.esem["vector"], r["v_var2"])
            r["s_std2"] = P.op("scalar", "activation", out=stats[b][:, 13:14], in_=stats[b][:, 12:13],
                               func=AF.Sqrt, bias=eps, sem=P.esem["scalar"])
            P.wait("vector", P.esem["scalar"], r["s_std2"])
            P.op("vector", "reciprocal", out=stats[b][:, 15:16], in_=stats[b][:, 13:14],
                 sem=P.esem["vector"])
            r["v_st2"] = P.cnt[P.esem["vector"]]
            P.wait("tensor", ld[b], r["ld"])
            if i >= 1:
                P.wait("tensor", P.esem["vector"], rec[i - 1]["v_ev"])
                P.wait("tensor", P.esem["scalar"], rec[i - 1]["s_st2"])
            for c in range(DC):
                P.op("tensor", "transpose", psx[:, c * 128:(c + 1) * 128],
                     xt[b][:, c * 128:(c + 1) * 128], idf)
            r["t_x"] = P.einc("tensor")
            P.wait("tensor", P.esem["vector"], r["v_xn"])
            for c in range(DC):
                P.op("tensor", "transpose", psn[:, c * 128:(c + 1) * 128],
                     xn[b][:, c * 128:(c + 1) * 128], idf)
            r["t_all"] = P.einc("tensor")
            P.wait("vector", P.esem["tensor"], r["t_x"])
            if i >= 2:
                P.wait("vector", stq[b], rec[i - 2]["st"])
            r["v_ev"] = P.op("vector", "tensor_copy", out=ev[b], in_=psx, sem=P.esem["vector"])
            P.wait("scalar", P.esem["tensor"], r["t_all"])
            if i >= 2:
                P.wait("scalar", stq[b], rec[i - 2]["st"])
            r["s_evn"] = P.op("scalar", "copy", out=evn[b], in_=psn, sem=P.esem["scalar"])
            P.selfbar("scalar")
            r["s_evb"] = P.op("scalar", "copy", out=evb[b], in_=evn[b], sem=P.esem["scalar"])
            P.wait("tensor", P.esem["scalar"], P.cnt[P.esem["scalar"]])
            P.wait("tensor", P.esem["vector"], r["v_st2"])
            r["t_st2"] = P.op("tensor", "transpose", psn[0:2, 0:128], stats[b][:, 14:16],
                              idf, sem=P.esem["tensor"])
            P.wait("scalar", P.esem["tensor"], r["t_st2"])
            if i >= 2:
                P.wait("scalar", stq[b], rec[i - 2]["st"])
            r["s_st2"] = P.op("scalar", "copy", out=st2sb[b], in_=psn[0:2, 0:128],
                              sem=P.esem["scalar"])
            P.wait("gpsimd", P.esem["vector"], r["v_ev"])
            P.wait("gpsimd", P.esem["scalar"], r["s_st2"])
            view = lambda t: t.rearrange("p (c n) -> p c n", c=DC)
            dma_split3(P, xres[:, :, i * 128:(i + 1) * 128], view(ev[b]), stq[b], 4, eng="gpsimd")
            dma_split3(P, xn32[:, :, i * 128:(i + 1) * 128], view(evn[b]), stq[b], 4, eng="gpsimd")
            dma_split3(P, xnbf[:, :, i * 128:(i + 1) * 128], view(evb[b]), stq[b], 2, eng="gpsimd")
            r["st"] = P.op("gpsimd", "dma_start", out=st2[:, i * 128:(i + 1) * 128], in_=st2sb[b], sem=stq[b])
            rec[i] = r
        P.barrier()



def emit_p1(P, ctx):
    nc = P.nc
    xn32 = P.dram("xn32").rearrange("(c p) t -> p c t", p=128)
    st2 = P.dram("st2")
    xnorm = P.dram("xnorm").rearrange("(c p) t -> p c t", p=128)
    VEC = ctx["VEC"]
    vecs = ctx["vecs"]
    onesf = ctx["ones_row_f"]
    TT = 512
    NT = S // TT
    with ExitStack() as st:
        xt = [st.enter_context(nc.sbuf_tensor(f"p1_x{i}", [128, DC * TT], F32))[:] for i in range(2)]
        mu = [st.enter_context(nc.sbuf_tensor(f"p1_m{i}", [1, TT], F32))[:] for i in range(2)]
        rs = [st.enter_context(nc.sbuf_tensor(f"p1_r{i}", [1, TT], F32))[:] for i in range(2)]
        bc = [st.enter_context(nc.sbuf_tensor(f"p1_b{i}", [128, 2 * TT], F32))[:] for i in range(2)]
        tmp8 = st.enter_context(nc.sbuf_tensor("p1_t8", [128, DC * TT], F32))[:]
        ob = [st.enter_context(nc.sbuf_tensor(f"p1_o{i}", [128, DC * TT], BF16))[:] for i in range(2)]
        psb = st.enter_context(nc.psum_tensor("p1_ps", [128, 2 * TT], F32))[:]
        ld = [P.pool_sem(), P.pool_sem()]
        stq = [P.pool_sem(), P.pool_sem()]
        rec = {}
        ldt = {}

        def issue_ld(i):
            if i >= 2:
                P.wait("sync", P.esem["vector"], rec[i - 2]["v_done"])
            dma_split3(P, xt[i % 2].rearrange("p (c n) -> p c n", c=DC),
                       xn32[:, :, i * TT:(i + 1) * TT], ld[i % 2], 8)
            P.op("sync", "dma_start", out=mu[i % 2], in_=st2[0:1, i * TT:(i + 1) * TT], sem=ld[i % 2])
            ldt[i] = P.op("sync", "dma_start", out=rs[i % 2], in_=st2[1:2, i * TT:(i + 1) * TT], sem=ld[i % 2])

        issue_ld(0)
        for i in range(NT):
            b = i % 2
            r = {}
            if i + 1 < NT:
                issue_ld(i + 1)
            r["ld"] = ldt[i]
            P.wait("tensor", ld[b], r["ld"])
            if i >= 1:
                P.wait("tensor", P.esem["vector"], rec[i - 1]["v_bc"])
            P.op("tensor", "matmul", psb[:, 0:TT], onesf, mu[b], start=True, stop=True)
            r["t_bc"] = P.op("tensor", "matmul", psb[:, TT:2 * TT], onesf, rs[b],
                             start=True, stop=True, sem=P.esem["tensor"])
            P.wait("vector", P.esem["tensor"], r["t_bc"])
            P.op("vector", "tensor_copy", out=bc[b][:, 0:TT], in_=psb[:, 0:TT])
            r["v_bc"] = P.op("vector", "tensor_copy", out=bc[b][:, TT:2 * TT], in_=psb[:, TT:2 * TT],
                             sem=P.esem["vector"])
            if i >= 2:
                P.wait("vector", stq[b], rec[i - 2]["st"])
            P.selfbar("vector")
            P.op("vector", "tensor_tensor", out=tmp8.rearrange("p (c n) -> p c n", c=DC),
                 in0=xt[b].rearrange("p (c n) -> p c n", c=DC),
                 in1=bc[b][:, 0:TT].unsqueeze(1).broadcast_to([128, DC, TT]),
                 op=ALU.subtract)
            P.selfbar("vector")
            P.op("vector", "tensor_tensor", out=tmp8.rearrange("p (c n) -> p c n", c=DC),
                 in0=tmp8.rearrange("p (c n) -> p c n", c=DC),
                 in1=bc[b][:, TT:2 * TT].unsqueeze(1).broadcast_to([128, DC, TT]),
                 op=ALU.mult)
            P.selfbar("vector")
            for c in range(DC):
                P.op("vector", "tensor_scalar", out=ob[b][:, c * TT:(c + 1) * TT],
                     in0=tmp8[:, c * TT:(c + 1) * TT],
                     scalar1=vecs[:, VEC["gamma"] + c:VEC["gamma"] + c + 1],
                     scalar2=vecs[:, VEC["beta"] + c:VEC["beta"] + c + 1],
                     op0=ALU.mult, op1=ALU.add)
            P.selfbar("vector")
            r["v_done"] = P.cnt[P.esem["vector"]]
            P.wait("gpsimd", P.esem["vector"], r["v_done"])
            r["st"] = dma_split3(P, xnorm[:, :, i * TT:(i + 1) * TT],
                                 ob[b].rearrange("p (c n) -> p c n", c=DC), stq[b], 4,
                                 eng="gpsimd")
            rec[i] = r
        P.barrier()



def emit_proj_fm(P, ctx, w_name, x_name, out_name, K, M, w_col0=0, w_row0=0,
                 scale=None, bias_base=None, resid_name=None, out_f32=False,
                 tag="pj", w_chunked=False):
    nc = P.nc
    KC_, MC = K // 128, M // 128
    x_d = P.dram(x_name).rearrange("(c p) t -> p c t", p=128)
    out_d = P.dram(out_name).rearrange("(c p) t -> p c t", p=128)
    resid_d = P.dram(resid_name).rearrange("(c p) t -> p c t", p=128) if resid_name else None
    vecs = ctx["vecs"]
    TT = 512
    NT = S // TT
    odt = F32 if (out_f32 or resid_name) else BF16
    NPS = 4
    with ExitStack() as st:
        wb = st.enter_context(nc.sbuf_tensor(f"{tag}_wb", [128, KC_ * M], BF16))[:]
        xt = [st.enter_context(nc.sbuf_tensor(f"{tag}_x{i}", [128, KC_ * TT], BF16))[:] for i in range(2)]
        ot = [st.enter_context(nc.sbuf_tensor(f"{tag}_o{i}", [128, MC * TT], odt))[:] for i in range(2)]
        rt = [st.enter_context(nc.sbuf_tensor(f"{tag}_r{i}", [128, TT], F32))[:] for i in range(NPS)] if resid_name else None
        ps = [st.enter_context(nc.psum_tensor(f"{tag}_p{i}", [128, TT], F32))[:] for i in range(NPS)]
        wld = P.pool_sem()
        ld = [P.pool_sem(), P.pool_sem()]
        stq = [P.pool_sem(), P.pool_sem()]
        rld = [P.pool_sem() for _ in range(NPS)]
        rld_t = {}
        if w_chunked:
            w_d = P.dram(w_name)
            K = KC_ * 128
            for mc in range(MC):
                dma_split2(P, wb[:, mc * K:(mc + 1) * K], w_d[w_col0 // 128 + mc], wld, 2)
            P.wait("tensor", wld, P.cnt[wld])
            wcol = lambda kc, mc: wb[:, mc * K + kc * 128: mc * K + (kc + 1) * 128]
        else:
            w_d = P.dram(w_name)
            wst = [st.enter_context(nc.sbuf_tensor(f"{tag}_ws{i}", [128, M], F32))[:] for i in range(2)]
            wld2 = [P.pool_sem(), P.pool_sem()]
            wcv = {}
            for kc in range(KC_):
                bk = kc % 2
                if kc >= 2:
                    P.wait("sync", P.esem["vector"], wcv[kc - 2])
                t = dma_split2(P, wst[bk],
                               w_d[w_row0 + kc * 128: w_row0 + (kc + 1) * 128, w_col0:w_col0 + M],
                               wld2[bk], 4)
                P.wait("vector", wld2[bk], t)
                wcv[kc] = P.op("vector", "tensor_copy", out=wb[:, kc * M:(kc + 1) * M],
                               in_=wst[bk], sem=P.esem["vector"])
            P.wait("tensor", P.esem["vector"], P.cnt[P.esem["vector"]])
            wcol = lambda kc, mc: wb[:, kc * M + mc * 128: kc * M + (mc + 1) * 128]
        rec = {}
        ev = {}
        ldt = {}

        def issue_ld(i):
            b = i % 2
            if i >= 2:
                P.wait("sync", P.esem["tensor"], rec[i - 2]["t_done"])
                if resid_name:
                    P.wait("sync", P.esem["vector"], rec[i - 2]["v_done"])
            ldt[i] = dma_split3(P, xt[b].rearrange("p (c n) -> p c n", c=KC_),
                                x_d[:, :, i * TT:(i + 1) * TT], ld[b], 4)

        issue_ld(0)
        for i in range(NT):
            b = i % 2
            r = {}
            if i + 1 < NT:
                issue_ld(i + 1)
            r["ld"] = ldt[i]
            P.wait("tensor", ld[b], r["ld"])
            first_ev = True
            for mc in range(MC):
                j = i * MC + mc
                pb = ps[j % NPS]
                if resid_name:
                    if j >= NPS:
                        P.wait("sync", P.esem["vector"], ev[j - NPS])
                    rld_t[j] = P.op("sync", "dma_start", out=rt[j % NPS],
                                    in_=resid_d[:, mc, i * TT:(i + 1) * TT], sem=rld[j % NPS])
                if j >= NPS:
                    P.wait("tensor", P.esem["vector"], ev[j - NPS])
                for kc in range(KC_):
                    P.op("tensor", "matmul", pb, wcol(kc, mc), xt[b][:, kc * TT:(kc + 1) * TT],
                         start=(kc == 0), stop=(kc == KC_ - 1))
                mtgt = P.einc("tensor")
                r["t_done"] = mtgt
                P.wait("vector", P.esem["tensor"], mtgt)
                if first_ev:
                    if i >= 2:
                        P.wait("vector", stq[b], rec[i - 2]["st"])
                    first_ev = False
                oc = ot[b][:, mc * TT:(mc + 1) * TT]
                if resid_name:
                    P.wait("vector", rld[j % NPS], rld_t[j])
                    P.op("vector", "tensor_tensor", out=oc, in0=pb,
                         in1=rt[j % NPS], op=ALU.add, sem=P.esem["vector"])
                elif bias_base is not None and scale is not None:
                    P.op("vector", "tensor_scalar", out=oc, in0=pb, scalar1=P.const(scale),
                         scalar2=vecs[:, bias_base + mc:bias_base + mc + 1],
                         op0=ALU.mult, op1=ALU.add, sem=P.esem["vector"])
                elif bias_base is not None:
                    P.op("vector", "tensor_single_scalar", out=oc, in_=pb,
                         scalar=vecs[:, bias_base + mc:bias_base + mc + 1], op=ALU.add,
                         sem=P.esem["vector"])
                elif scale is not None:
                    P.op("vector", "tensor_single_scalar", out=oc, in_=pb,
                         scalar=P.const(scale), op=ALU.mult, sem=P.esem["vector"])
                else:
                    P.op("vector", "tensor_copy", out=oc, in_=pb, sem=P.esem["vector"])
                ev[j] = P.cnt[P.esem["vector"]]
            r["v_done"] = P.cnt[P.esem["vector"]]
            P.wait("gpsimd", P.esem["vector"], r["v_done"])
            r["st"] = dma_split3(P, out_d[:, :, i * TT:(i + 1) * TT],
                                 ot[b].rearrange("p (c n) -> p c n", c=MC), stq[b], 4,
                                 eng="gpsimd")
            rec[i] = r
        P.barrier()


SC_Q0 = 1.0 / float(np.sqrt(2048.0))
SC_QMQ = 1.0 / float(np.sqrt(128.0))


def emit_p2q(P, ctx):
    emit_proj_fm(P, ctx, "L0_attn_qkv", "xnorm", "q0", 2048, 2048, w_col0=0,
                 scale=SC_Q0, tag="p2q")


def emit_p2k(P, ctx):
    emit_proj_fm(P, ctx, "L0_attn_qkv", "xnorm", "k0", 2048, 2048, w_col0=2048, tag="p2k")


def emit_proj_tm(P, ctx, w_name, x_name, out_name, K, M, w_col0=0, tag="pt"):
    nc = P.nc
    KC_, NC2 = K // 128, M // 512
    x_d = P.dram(x_name).rearrange("(c p) t -> p c t", p=128)
    out_d = P.dram(out_name)
    TT = 512
    NT = S // TT
    with ExitStack() as st:
        wb = st.enter_context(nc.sbuf_tensor(f"{tag}_wb", [128, KC_ * M], BF16))[:]
        wst = [st.enter_context(nc.sbuf_tensor(f"{tag}_ws{i}", [128, M], F32))[:] for i in range(2)]
        xt = [st.enter_context(nc.sbuf_tensor(f"{tag}_x{i}", [128, KC_ * TT], BF16))[:] for i in range(2)]
        ot = [st.enter_context(nc.sbuf_tensor(f"{tag}_o{i}", [128, 4 * M], BF16))[:] for i in range(2)]
        ps = [st.enter_context(nc.psum_tensor(f"{tag}_p{i}", [128, TT], F32))[:] for i in range(4)]
        wld2 = [P.pool_sem(), P.pool_sem()]
        ld = [P.pool_sem(), P.pool_sem()]
        stq = [P.pool_sem(), P.pool_sem()]
        w_d = P.dram(w_name)
        wcv = {}
        for kc in range(KC_):
            bk = kc % 2
            if kc >= 2:
                P.wait("sync", P.esem["vector"], wcv[kc - 2])
            t = dma_split2(P, wst[bk], w_d[kc * 128:(kc + 1) * 128, w_col0:w_col0 + M],
                           wld2[bk], 4)
            P.wait("vector", wld2[bk], t)
            wcv[kc] = P.op("vector", "tensor_copy", out=wb[:, kc * M:(kc + 1) * M],
                           in_=wst[bk], sem=P.esem["vector"])
        P.wait("tensor", P.esem["vector"], P.cnt[P.esem["vector"]])
        rec = {}
        ev = {}
        ldt = {}

        def issue_ld(i):
            if i >= 2:
                P.wait("sync", P.esem["tensor"], rec[i - 2]["t_done"])
            ldt[i] = dma_split3(P, xt[i % 2].rearrange("p (c n) -> p c n", c=KC_),
                                x_d[:, :, i * TT:(i + 1) * TT], ld[i % 2], 4)

        issue_ld(0)
        for i in range(NT):
            b = i % 2
            r = {}
            if i + 1 < NT:
                issue_ld(i + 1)
            r["ld"] = ldt[i]
            P.wait("tensor", ld[b], r["ld"])
            first_ev = True
            for tc in range(4):
                for nc2 in range(NC2):
                    j = (i * 4 + tc) * NC2 + nc2
                    pb = ps[j % 4]
                    if j >= 4:
                        P.wait("tensor", P.esem["vector"], ev[j - 4])
                    for kc in range(KC_):
                        P.op("tensor", "matmul", pb,
                             xt[b][:, kc * TT + tc * 128: kc * TT + (tc + 1) * 128],
                             wb[:, kc * M + nc2 * 512: kc * M + (nc2 + 1) * 512],
                             start=(kc == 0), stop=(kc == KC_ - 1))
                    mtgt = P.einc("tensor")
                    r["t_done"] = mtgt
                    P.wait("vector", P.esem["tensor"], mtgt)
                    if first_ev:
                        if i >= 2:
                            P.wait("vector", stq[b], rec[i - 2]["st"])
                        first_ev = False
                    P.op("vector", "tensor_copy",
                         out=ot[b][:, tc * M + nc2 * 512: tc * M + (nc2 + 1) * 512],
                         in_=pb, sem=P.esem["vector"])
                    ev[j] = P.cnt[P.esem["vector"]]
            r["v_done"] = P.cnt[P.esem["vector"]]
            P.wait("gpsimd", P.esem["vector"], r["v_done"])
            r["st"] = dma_split3(P, out_d[i * TT:(i + 1) * TT, :].rearrange("(c p) n -> p c n", p=128),
                                 ot[b].rearrange("p (c n) -> p c n", c=4), stq[b], 4,
                                 eng="gpsimd")
            rec[i] = r
        P.barrier()


def emit_p2v(P, ctx):
    emit_proj_tm(P, ctx, "L0_attn_qkv", "xnorm", "v0", 2048, 2048, w_col0=4096, tag="p2v")


def emit_p4(P, ctx):
    emit_proj_fm(P, ctx, "L0_attn_o_proj", "att0", "s0", 2048, 2048,
                 resid_name="xn32", tag="p4")



def emit_p3(P, ctx):
    nc = P.nc
    q_d = P.dram("q0").rearrange("(c p) t -> p c t", p=128)
    k_d = P.dram("k0").rearrange("(c p) t -> p c t", p=128)
    v_d = P.dram("v0")
    o_d = P.dram("att0").rearrange("(c p) t -> p c t", p=128)
    idb = ctx["idb"]
    ES_T, ES_V, ES_S = P.esem["tensor"], P.esem["vector"], P.esem["scalar"]
    NQ = S // 512
    with ExitStack() as st:
        maskA = st.enter_context(nc.sbuf_tensor("p3_maskA", [128, 4 * 512], F32))[:]
        qt_ = [st.enter_context(nc.sbuf_tensor(f"p3_q{i}", [128, DC * 512], BF16))[:] for i in range(2)]
        kt = [st.enter_context(nc.sbuf_tensor(f"p3_k{i}", [128, DC * 512], BF16))[:] for i in range(2)]
        vt = [st.enter_context(nc.sbuf_tensor(f"p3_v{i}", [128, 4 * 2048], BF16))[:] for i in range(2)]
        Osb = [st.enter_context(nc.sbuf_tensor(f"p3_O{i}", [128, 2048], F32))[:] for i in range(4)]
        Ssb = [st.enter_context(nc.sbuf_tensor(f"p3_S{i}", [128, 512], F32))[:] for i in range(2)]
        PTb = [st.enter_context(nc.sbuf_tensor(f"p3_P{i}", [128, 512], BF16))[:] for i in range(2)]
        PTs = [st.enter_context(nc.sbuf_tensor(f"p3_T{i}", [128, 512], BF16))[:] for i in range(2)]
        stat = st.enter_context(nc.sbuf_tensor("p3_st", [128, 16], F32))[:]
        onb = st.enter_context(nc.sbuf_tensor("p3_onb", [128, 2048], BF16))[:]
        osb = st.enter_context(nc.sbuf_tensor("p3_osb", [128, 2048], BF16))[:]
        psS = st.enter_context(nc.psum_tensor("p3_psS", [128, 1024], F32))[:]
        psT = st.enter_context(nc.psum_tensor("p3_psT", [128, 512], BF16))[:]
        psO = st.enter_context(nc.psum_tensor("p3_psO", [128, 2048], F32))[:]
        qld = [P.pool_sem(), P.pool_sem()]
        kld = [P.pool_sem(), P.pool_sem()]
        stq = P.pool_sem()
        P.op("gpsimd", "memset", maskA, 0.0)
        for i_ in range(4):
            P.op("gpsimd", "affine_select", out=maskA[:, i_ * 512:(i_ + 1) * 512],
                 in_=maskA[:, i_ * 512:(i_ + 1) * 512], compare_op=ALU.is_ge,
                 fill=BIG_NEG, base=128 * i_, pattern=[[-1, 512]],
                 channel_multiplier=1, sem=P.esem["gpsimd"])
        P.wait("vector", P.esem["gpsimd"], P.cnt[P.esem["gpsimd"]])
        units = []
        for qt in range(NQ):
            for kb in range(qt + 1):
                for qs in range(4):
                    units.append((qt, kb, qs, kb == qt, kb == 0))
        NU = len(units)
        s_mm, exp_done, mask_v, tr_done, cp_done, pv_done, vadd_done = {}, {}, {}, {}, {}, {}, {}
        den_v = {}
        qrec, q_loaded, kv_loaded = {}, {}, {}
        kvconsumed = {}

        def load_q(qt):
            if qt >= 2:
                P.wait("sync", ES_T, qrec[qt - 2])
            q_loaded[qt] = dma_split3(P, qt_[qt % 2].rearrange("p (c n) -> p c n", c=DC),
                                      q_d[:, :, qt * 512:(qt + 1) * 512], qld[qt % 2], 4)

        def load_kv(qt, kb):
            bk = kb % 2
            prev = kvconsumed.get(bk)
            if prev is not None:
                P.wait("sync", ES_T, prev)
            dma_split3(P, kt[bk].rearrange("p (c n) -> p c n", c=DC),
                       k_d[:, :, kb * 512:(kb + 1) * 512], kld[bk], 4)
            kv_loaded[kb] = dma_split3(P, vt[bk].rearrange("p (c n) -> p c n", c=4),
                                       v_d[kb * 512:(kb + 1) * 512, :]
                                       .rearrange("(c p) n -> p c n", p=128), kld[bk], 4)

        def emit_trcp(i):
            un = units[i]
            P.wait("tensor", ES_S, exp_done[i])
            if i >= 2:
                P.wait("tensor", ES_S, cp_done[i - 2])
            for c4 in range(4):
                P.op("tensor", "transpose", psT[:, c4 * 128:(c4 + 1) * 128],
                     PTb[i % 2][:, c4 * 128:(c4 + 1) * 128], idb)
            tr_done[i] = P.einc("tensor")
            P.wait("scalar", ES_T, tr_done[i])
            if i >= 2:
                P.wait("scalar", ES_T, pv_done.get(i - 2, 0))
            cp_done[i] = P.op("scalar", "copy", out=PTs[i % 2], in_=psT, sem=ES_S)

        def emit_pv(i):
            un = units[i]
            qt, kb, qs, diag, first = un
            bk = kb % 2
            P.wait("tensor", ES_S, cp_done[i])
            if i >= 1:
                P.wait("tensor", ES_V, vadd_done.get(i - 1, 0))
            for nc4 in range(4):
                for kc4 in range(4):
                    P.op("tensor", "matmul", psO[:, nc4 * 512:(nc4 + 1) * 512],
                         PTs[i % 2][:, kc4 * 128:(kc4 + 1) * 128],
                         vt[bk][:, kc4 * 2048 + nc4 * 512: kc4 * 2048 + (nc4 + 1) * 512],
                         start=(kc4 == 0), stop=(kc4 == 3))
            pv_done[i] = P.einc("tensor")
            if qs == 3:
                kvconsumed[bk] = pv_done[i]
            P.wait("vector", ES_T, pv_done[i])
            if first:
                P.op("vector", "tensor_copy", out=Osb[qs], in_=psO)
            else:
                P.op("vector", "tensor_tensor", out=Osb[qs], in0=Osb[qs], in1=psO,
                     op=ALU.add)
            P.wait("vector", ES_S, exp_done[i])
            if first:
                vadd_done[i] = P.op("vector", "tensor_copy", out=stat[:, 2 * qs:2 * qs + 1],
                                    in_=stat[:, 8 + i % 2:9 + i % 2], sem=ES_V)
            else:
                vadd_done[i] = P.op("vector", "tensor_tensor", out=stat[:, 2 * qs:2 * qs + 1],
                                    in0=stat[:, 2 * qs:2 * qs + 1],
                                    in1=stat[:, 8 + i % 2:9 + i % 2], op=ALU.add, sem=ES_V)
            den_v[i] = vadd_done[i]

        def finalize(qt):
            for qs in range(4):
                P.selfbar("vector")
                P.op("vector", "reciprocal", out=stat[:, 2 * qs + 1:2 * qs + 2],
                     in_=stat[:, 2 * qs:2 * qs + 1])
                P.selfbar("vector")
                nrm = P.op("vector", "tensor_single_scalar", out=onb, in_=Osb[qs],
                           scalar=stat[:, 2 * qs + 1:2 * qs + 2], op=ALU.mult, sem=ES_V)
                for r4 in range(4):
                    P.wait("tensor", ES_V, nrm)
                    P.wait("tensor", ES_S, P.cnt[ES_S])
                    for c4 in range(4):
                        P.op("tensor", "transpose", psT[:, c4 * 128:(c4 + 1) * 128],
                             onb[:, (r4 * 4 + c4) * 128:(r4 * 4 + c4 + 1) * 128], idb)
                    ttr = P.einc("tensor")
                    P.wait("scalar", ES_T, ttr)
                    P.op("scalar", "copy", out=osb[:, r4 * 512:(r4 + 1) * 512], in_=psT, sem=ES_S)
                sc = P.cnt[ES_S]
                P.wait("sync", ES_S, sc)
                std = P.op("sync", "dma_start",
                           out=o_d[:, :, qt * 512 + qs * 128: qt * 512 + (qs + 1) * 128],
                           in_=osb.rearrange("p (c n) -> p c n", c=DC), sem=stq)
                P.wait("vector", ES_S, sc)
                P.wait("scalar", stq, std)

        load_q(0)
        load_kv(0, 0)
        next_tr = 0
        next_pv = 0
        for i in range(NU):
            qt, kb, qs, diag, first = units[i]
            if qs == 0 and kb == 0 and qt + 1 < NQ:
                load_q(qt + 1)
            if i >= 2:
                P.wait("tensor", ES_S, exp_done[i - 2])
            if qs == 0 and kb == 0:
                P.wait("tensor", qld[qt % 2], q_loaded[qt])
            if qs == 0:
                P.wait("tensor", kld[kb % 2], kv_loaded[kb])
            pS = psS[:, (i % 2) * 512:(i % 2 + 1) * 512]
            for kc in range(DC):
                P.op("tensor", "matmul", pS,
                     qt_[qt % 2][:, kc * 512 + qs * 128: kc * 512 + (qs + 1) * 128],
                     kt[kb % 2][:, kc * 512:(kc + 1) * 512],
                     start=(kc == 0), stop=(kc == DC - 1))
            s_mm[i] = P.einc("tensor")
            if qs == 3 and kb == qt:
                qrec[qt] = s_mm[i]
            while next_tr <= i - 1:
                emit_trcp(next_tr)
                next_tr += 1
            while next_pv <= i - 2:
                emit_pv(next_pv)
                next_pv += 1
            if qs == 1 and kb < qt:
                load_kv(qt, kb + 1)
            if diag:
                P.wait("vector", ES_T, s_mm[i])
                if i >= 2:
                    P.wait("vector", ES_S, exp_done[i - 2])
                mask_v[i] = P.op("vector", "tensor_tensor", out=Ssb[i % 2], in0=pS,
                                 in1=maskA[:, qs * 512:(qs + 1) * 512], op=ALU.add,
                                 sem=ES_V)
                src = Ssb[i % 2]
                P.wait("scalar", ES_V, mask_v[i])
            else:
                P.wait("scalar", ES_T, s_mm[i])
                src = pS
            if i >= 2:
                P.wait("scalar", ES_T, tr_done.get(i - 2, 0))
                P.wait("scalar", ES_V, den_v.get(i - 2, 0))
            exp_done[i] = P.op("scalar", "activation", out=PTb[i % 2], in_=src, func=AF.Exp,
                               accum_out=stat[:, 8 + i % 2:9 + i % 2], sem=ES_S)
            if qs == 3 and kb == qt:
                while next_tr <= i:
                    emit_trcp(next_tr)
                    next_tr += 1
                while next_pv <= i:
                    emit_pv(next_pv)
                    next_pv += 1
                if qt + 1 < NQ:
                    load_kv(qt + 1, 0)
                finalize(qt)
        P.barrier()



def emit_p5a(P, ctx):
    nc = P.nc
    w_d = P.dram("L0_ffn_W_qkv")
    o_d = P.dram("wqkT3")
    idb = ctx["idb"]
    with ExitStack() as st:
        wst = [st.enter_context(nc.sbuf_tensor(f"p5a_w{i}", [128, 2048], F32))[:] for i in range(2)]
        wbf = [st.enter_context(nc.sbuf_tensor(f"p5a_b{i}", [128, 2048], BF16))[:] for i in range(2)]
        ev = [st.enter_context(nc.sbuf_tensor(f"p5a_e{i}", [128, 2048], BF16))[:] for i in range(2)]
        psT = [st.enter_context(nc.psum_tensor(f"p5a_p{i}", [128, 2048], BF16))[:] for i in range(2)]
        ld = [P.pool_sem(), P.pool_sem()]
        stq = [P.pool_sem(), P.pool_sem()]
        rec = {}
        ldt = {}

        def issue_ld(c):
            if c >= 2:
                P.wait("sync", P.esem["vector"], rec[c - 2]["cv"])
            ldt[c] = dma_split2(P, wst[c % 2], w_d[c * 128:(c + 1) * 128, :], ld[c % 2], 4)

        issue_ld(0)
        for c in range(48):
            b = c % 2
            r = {}
            if c + 1 < 48:
                issue_ld(c + 1)
            r["ld"] = ldt[c]
            P.wait("vector", ld[b], r["ld"])
            if c >= 2:
                P.wait("vector", P.esem["tensor"], rec[c - 2]["tr"])
            r["cv"] = P.op("vector", "tensor_copy", out=wbf[b], in_=wst[b], sem=P.esem["vector"])
            P.wait("tensor", P.esem["vector"], r["cv"])
            if c >= 2:
                P.wait("tensor", P.esem["scalar"], rec[c - 2]["ev"])
            for tc in range(DC):
                P.op("tensor", "transpose", psT[b][:, tc * 128:(tc + 1) * 128],
                     wbf[b][:, tc * 128:(tc + 1) * 128], idb)
            r["tr"] = P.einc("tensor")
            P.wait("scalar", P.esem["tensor"], r["tr"])
            if c >= 2:
                P.wait("scalar", stq[b], rec[c - 2]["st"])
            r["ev"] = P.op("scalar", "copy", out=ev[b], in_=psT[b], sem=P.esem["scalar"])
            P.wait("sync", P.esem["scalar"], r["ev"])
            r["st"] = P.op("sync", "dma_start", out=o_d[c], in_=ev[b], sem=stq[b])
            rec[c] = r
        P.barrier()



def emit_p5b(P, ctx):
    nc = P.nc
    wg_d = P.dram("L0_ffn_W_gate")
    x_d = P.dram("xn32").rearrange("(c p) t -> p c t", p=128)
    hot_d = P.dram("hot")
    g1_d = P.dram("g1v")
    idb, idf, iota16 = ctx["idb"], ctx["idf"], ctx["iota16"]
    U32 = mybir.dt.uint32
    with ExitStack() as st:
        wg = st.enter_context(nc.sbuf_tensor("p5b_wg", [16, 2048], F32))[:]
        wgT = st.enter_context(nc.sbuf_tensor("p5b_wgT", [128, DC * 16], F32))[:]
        xt = [st.enter_context(nc.sbuf_tensor(f"p5b_x{i}", [128, DC * 512], F32))[:] for i in range(2)]
        lg = st.enter_context(nc.sbuf_tensor("p5b_lg", [16, 512], F32))[:]
        lt = st.enter_context(nc.sbuf_tensor("p5b_lt", [128, 16], F32))[:]
        el = st.enter_context(nc.sbuf_tensor("p5b_el", [128, 16], F32))[:]
        mx = st.enter_context(nc.sbuf_tensor("p5b_mx", [128, 8], F32))[:]
        ix = st.enter_context(nc.sbuf_tensor("p5b_ix", [128, 8], U32))[:]
        sm = st.enter_context(nc.sbuf_tensor("p5b_sm", [128, 4], F32))[:]
        hsb = [st.enter_context(nc.sbuf_tensor(f"p5b_h{i}", [128, 32], F32))[:] for i in range(2)]
        psL = st.enter_context(nc.psum_tensor("p5b_pL", [16, 512], F32))[:]
        psX = st.enter_context(nc.psum_tensor("p5b_pX", [128, 16], F32))[:]
        ld = [P.pool_sem(), P.pool_sem()]
        stq = [P.pool_sem(), P.pool_sem()]
        ES_T, ES_V, ES_S = P.esem["tensor"], P.esem["vector"], P.esem["scalar"]
        P.op("sync", "dma_start", out=wg, in_=wg_d, sem=ld[0])
        P.wait("tensor", ld[0], P.cnt[ld[0]])
        for tc in range(DC):
            P.op("tensor", "transpose", psX, wg[0:16, tc * 128:(tc + 1) * 128], idf[0:16, 0:16])
            ttr = P.einc("tensor")
            P.wait("scalar", ES_T, ttr)
            P.op("scalar", "copy", out=wgT[:, tc * 16:(tc + 1) * 16], in_=psX, sem=ES_S)
            P.wait("tensor", ES_S, P.cnt[ES_S])
        rec = {}
        hst = {}
        ldt = {}

        def issue_ld(i):
            if i >= 2:
                P.wait("sync", ES_T, rec[i - 2]["mm"])
            ldt[i] = dma_split3(P, xt[i % 2].rearrange("p (c n) -> p c n", c=DC),
                                x_d[:, :, i * 512:(i + 1) * 512], ld[i % 2], 8)

        issue_ld(0)
        for i in range(8):
            b = i % 2
            r = {}
            if i + 1 < 8:
                issue_ld(i + 1)
            r["ld"] = ldt[i]
            P.wait("tensor", ld[b], r["ld"])
            if i >= 1:
                P.wait("tensor", ES_S, rec[i - 1]["lgc"])
            for kc in range(DC):
                P.op("tensor", "matmul", psL, wgT[:, kc * 16:(kc + 1) * 16],
                     xt[b][:, kc * 512:(kc + 1) * 512], start=(kc == 0), stop=(kc == DC - 1))
            r["mm"] = P.einc("tensor")
            P.wait("scalar", ES_T, r["mm"])
            r["lgc"] = P.op("scalar", "copy", out=lg, in_=psL, sem=ES_S)
            for s4 in range(4):
                t0 = i * 512 + s4 * 128
                P.wait("tensor", ES_S, r["lgc"])
                if i + s4 > 0:
                    P.wait("tensor", ES_V, P.cnt[ES_V])
                tt = P.op("tensor", "transpose", psX, lg[0:16, s4 * 128:(s4 + 1) * 128],
                          idf[0:16, 0:16], sem=ES_T)
                P.wait("vector", ES_T, tt)
                P.op("vector", "tensor_copy", out=lt, in_=psX)
                P.selfbar("vector")
                P.op("vector", "max", out=mx, in_=lt)
                P.selfbar("vector")
                P.op("vector", "max_index", out=ix, in_max=mx, in_values=lt)
                P.selfbar("vector")
                vsub = P.op("vector", "tensor_single_scalar", out=el, in_=lt, scalar=mx[:, 0:1],
                            op=ALU.subtract, sem=ES_V)
                P.wait("scalar", ES_V, vsub)
                ex = P.op("scalar", "activation", out=el, in_=el, func=AF.Exp, sem=ES_S)
                P.wait("vector", ES_S, ex)
                P.op("vector", "reduce_sum", out=sm[:, 0:1], in_=el, axis=AX)
                P.selfbar("vector")
                P.op("vector", "reciprocal", out=sm[:, 1:2], in_=sm[:, 0:1])
                P.op("vector", "tensor_copy", out=sm[:, 2:3], in_=ix[:, 0:1])
                P.selfbar("vector")
                bh = hsb[s4 % 2]
                uu = i * 4 + s4
                if uu >= 2:
                    P.wait("vector", stq[s4 % 2], hst[uu - 2])
                P.op("vector", "tensor_single_scalar", out=bh[:, 0:16], in_=iota16,
                     scalar=sm[:, 2:3], op=ALU.is_equal)
                P.selfbar("vector")
                vv = P.op("vector", "tensor_single_scalar", out=bh[:, 16:32], in_=bh[:, 0:16],
                          scalar=sm[:, 1:2], op=ALU.mult, sem=ES_V)
                P.wait("sync", ES_V, vv)
                P.op("sync", "dma_start", out=hot_d[t0:t0 + 128, :], in_=bh[:, 0:16], sem=stq[s4 % 2])
                hst[uu] = P.op("sync", "dma_start", out=g1_d[t0:t0 + 128, :], in_=bh[:, 16:32], sem=stq[s4 % 2])
            rec[i] = r
        P.barrier()



def emit_p5c(P, ctx):
    nc = P.nc
    w_d = P.dram("wqkT3")
    x_d = P.dram("xnbf").rearrange("(c p) t -> p c t", p=128)
    hot_d = P.dram("hot")
    g1_d = P.dram("g1v")
    kF_d = P.dram("kshF")
    vT_d = P.dram("vshT")
    idf = ctx["idf"]
    ES_T, ES_V, ES_S = P.esem["tensor"], P.esem["vector"], P.esem["scalar"]
    with ExitStack() as st:
        wk = st.enter_context(nc.sbuf_tensor("p5c_wk", [128, DC * 2048], BF16))[:]
        wv = st.enter_context(nc.sbuf_tensor("p5c_wv", [128, DC * 2048], BF16))[:]
        xt = [st.enter_context(nc.sbuf_tensor(f"p5c_x{i}", [128, DC * 512], BF16))[:] for i in range(2)]
        ktm = st.enter_context(nc.sbuf_tensor("p5c_ktm", [128, 2048], F32))[:]
        vtm = st.enter_context(nc.sbuf_tensor("p5c_vtm", [128, 2048], F32))[:]
        hsel = st.enter_context(nc.sbuf_tensor("p5c_h", [128, 32], F32))[:]
        acc = st.enter_context(nc.sbuf_tensor("p5c_acc", [128, 256], F32))[:]
        accb = st.enter_context(nc.sbuf_tensor("p5c_accb", [128, 256], BF16))[:]
        kTf = st.enter_context(nc.sbuf_tensor("p5c_kTf", [128, 128], BF16))[:]
        psK = [st.enter_context(nc.psum_tensor(f"p5c_pk{i}", [128, 512], F32))[:] for i in range(2)]
        psT = st.enter_context(nc.psum_tensor("p5c_pT", [128, 128], F32))[:]
        ld = [P.pool_sem(), P.pool_sem()]
        hld = P.pool_sem()
        stq = P.pool_sem()
        wld = P.pool_sem()
        for dst, base in ((wk, 16), (wv, 32)):
            for mc in range(16):
                P.op("sync", "dma_start",
                     out=dst.rearrange("p (kc m f) -> p kc m f", kc=DC, m=16)[:, :, mc:mc + 1, :],
                     in_=w_d[base + mc].rearrange("p (kc f) -> p kc f", kc=DC).unsqueeze(2),
                     sem=wld)
        P.wait("tensor", wld, P.cnt[wld])
        rec = {}
        ev = {}
        prev_tt = prev_sc = prev_st = None
        ldt = {}

        def issue_ld(i):
            if i >= 2:
                P.wait("sync", ES_T, rec[i - 2]["mm"])
            ldt[i] = dma_split3(P, xt[i % 2].rearrange("p (c n) -> p c n", c=DC),
                                x_d[:, :, i * 512:(i + 1) * 512], ld[i % 2], 4)

        issue_ld(0)
        u = 0
        for i in range(8):
            b = i % 2
            r = {}
            if i + 1 < 8:
                issue_ld(i + 1)
            r["ld"] = ldt[i]
            P.wait("tensor", ld[b], r["ld"])
            for tc in range(4):
                t0 = i * 512 + tc * 128
                P.wait("sync", ES_V, P.cnt[ES_V])
                hl = P.op("sync", "dma_start", out=hsel[:, 0:16], in_=hot_d[t0:t0 + 128, :], sem=hld)
                hl = P.op("sync", "dma_start", out=hsel[:, 16:32], in_=g1_d[t0:t0 + 128, :], sem=hld)
                for which, wt, dtm in ((0, wk, ktm), (1, wv, vtm)):
                    for nc4 in range(4):
                        pb = psK[u % 2]
                        if u >= 2:
                            P.wait("tensor", ES_V, ev[u - 2])
                        for kc in range(DC):
                            P.op("tensor", "matmul", pb,
                                 xt[b][:, kc * 512 + tc * 128: kc * 512 + (tc + 1) * 128],
                                 wt[:, kc * 2048 + nc4 * 512: kc * 2048 + (nc4 + 1) * 512],
                                 start=(kc == 0), stop=(kc == DC - 1))
                        mm = P.einc("tensor")
                        r["mm"] = mm
                        P.wait("vector", ES_T, mm)
                        ev[u] = P.op("vector", "tensor_copy",
                                     out=dtm[:, nc4 * 512:(nc4 + 1) * 512], in_=pb, sem=ES_V)
                        u += 1
                P.wait("vector", hld, P.cnt[hld])
                if prev_tt is not None:
                    P.wait("vector", ES_T, prev_tt)
                    P.wait("vector", stq, prev_st)
                for which, dtm, col0, a0 in ((0, ktm, 0, 0), (1, vtm, 16, 32 * 4)):
                    for h in range(16):
                        if h == 0:
                            P.op("vector", "tensor_single_scalar", out=acc[:, a0:a0 + 128],
                                 in_=dtm[:, 0:128], scalar=hsel[:, col0:col0 + 1], op=ALU.mult)
                        else:
                            P.op("vector", "scalar_tensor_tensor", out=acc[:, a0:a0 + 128],
                                 in0=dtm[:, h * 128:(h + 1) * 128],
                                 scalar=hsel[:, col0 + h:col0 + h + 1],
                                 in1=acc[:, a0:a0 + 128], op0=ALU.mult, op1=ALU.add)
                P.selfbar("vector")
                cv = P.op("vector", "tensor_copy", out=accb, in_=acc, sem=ES_V)
                P.wait("tensor", ES_V, cv)
                if prev_sc is not None:
                    P.wait("tensor", ES_S, prev_sc)
                tt = P.op("tensor", "transpose", psT, acc[:, 0:128], idf, sem=ES_T)
                prev_tt = tt
                P.wait("scalar", ES_T, tt)
                if prev_st is not None:
                    P.wait("scalar", stq, prev_st)
                sc = P.op("scalar", "copy", out=kTf, in_=psT, sem=ES_S)
                prev_sc = sc
                P.wait("sync", ES_S, sc)
                P.wait("sync", ES_V, cv)
                P.op("sync", "dma_start", out=kF_d[:, t0:t0 + 128], in_=kTf, sem=stq)
                prev_st = P.op("sync", "dma_start", out=vT_d[t0:t0 + 128, :],
                               in_=accb[:, 128:256], sem=stq)
            rec[i] = r
        P.barrier()



def emit_p5d(P, ctx):
    emit_proj_fm(P, ctx, "wqkT3", "xnbf", "q0", 2048, 2048, w_col0=0,
                 scale=SC_QMQ, tag="p5d", w_chunked=True)



def emit_p5e(P, ctx):
    nc = P.nc
    q_d = P.dram("q0").rearrange("(c p) t -> p c t", p=128)
    kF_d = P.dram("kshF")
    vT_d = P.dram("vshT").rearrange("(c p) n -> p c n", p=128)
    y_d = P.dram("y0n").rearrange("(c p) t -> p c t", p=128)
    ones_col, ones_row_f = ctx["ones_col"], ctx["ones_row_f"]
    ES_T, ES_V, ES_S = P.esem["tensor"], P.esem["vector"], P.esem["scalar"]
    with ExitStack() as st:
        maskM = st.enter_context(nc.sbuf_tensor("p5e_maskM", [128, 4 * 512], BF16))[:]
        ksh = st.enter_context(nc.sbuf_tensor("p5e_k", [128, S], BF16))[:]
        vsh = st.enter_context(nc.sbuf_tensor("p5e_v", [128, 32 * 128], BF16))[:]
        qt_ = [st.enter_context(nc.sbuf_tensor(f"p5e_q{i}", [128, DC * 512], BF16))[:] for i in range(2)]
        PT = [st.enter_context(nc.sbuf_tensor(f"p5e_P{i}", [128, 1024], BF16))[:] for i in range(2)]
        dsb = [st.enter_context(nc.sbuf_tensor(f"p5e_d{i}", [1, 512], F32))[:] for i in range(2)]
        rsb = [st.enter_context(nc.sbuf_tensor(f"p5e_r{i}", [1, 512], F32))[:] for i in range(2)]
        bcb = [st.enter_context(nc.sbuf_tensor(f"p5e_bc{i}", [128, 512], F32))[:] for i in range(2)]
        ysb = [st.enter_context(nc.sbuf_tensor(f"p5e_y{i}", [128, 512], F32))[:] for i in range(2)]
        psS = [st.enter_context(nc.psum_tensor(f"p5e_pS{i}", [128, 1024], F32))[:] for i in range(2)]
        psO = st.enter_context(nc.psum_tensor("p5e_pO", [128, 512], F32))[:]
        psD = [st.enter_context(nc.psum_tensor(f"p5e_pD{i}", [1, 512], F32))[:] for i in range(2)]
        psB = st.enter_context(nc.psum_tensor("p5e_pB", [128, 512], F32))[:]
        qld = [P.pool_sem(), P.pool_sem()]
        stq = [P.pool_sem(), P.pool_sem()]
        wld = P.pool_sem()
        P.op("gpsimd", "memset", maskM, 1.0)
        for i_ in range(4):
            P.op("gpsimd", "affine_select", out=maskM[:, i_ * 512:(i_ + 1) * 512],
                 in_=maskM[:, i_ * 512:(i_ + 1) * 512], compare_op=ALU.is_ge,
                 fill=0.0, base=-128 * i_, pattern=[[1, 512]],
                 channel_multiplier=-1, sem=P.esem["gpsimd"])
        P.wait("vector", P.esem["gpsimd"], P.cnt[P.esem["gpsimd"]])
        P.op("sync", "dma_start", out=ksh, in_=kF_d, sem=wld)
        P.op("sync", "dma_start", out=vsh.rearrange("p (c n) -> p c n", c=32), in_=vT_d, sem=wld)
        P.wait("tensor", wld, P.cnt[wld])
        units = []
        hu_ = 0
        for qt in range(8):
            for h in range(16):
                nh = 2 * (qt + 1)
                for kb in range(qt + 1):
                    for half in range(2):
                        iu = kb * 2 + half
                        units.append((qt, h, kb, half, hu_, iu == 0, iu == nh - 1,
                                      kb == qt))
                hu_ += 1
        NU = len(units)
        s_mm, exp_done, mask_done, pvden_done = {}, {}, {}, {}
        dv_done, rv_done, bc_done, bcb_done, yev, yst = {}, {}, {}, {}, {}, {}
        qrec, q_loaded = {}, {}

        def load_q(qt):
            bq = qt % 2
            if qt >= 2:
                P.wait("sync", ES_T, qrec[qt - 2])
            q_loaded[qt] = dma_split3(P, qt_[bq].rearrange("p (c n) -> p c n", c=DC),
                                      q_d[:, :, qt * 512:(qt + 1) * 512], qld[bq], 4)

        def emit_head_end(un, last_pv):
            qt, h, kb, half, hu, first, last, diag = un
            P.wait("scalar", ES_T, last_pv)
            if hu >= 2:
                P.wait("scalar", ES_V, rv_done[hu - 2])
            dv = P.op("scalar", "copy", out=dsb[hu % 2], in_=psD[hu % 2], sem=ES_S)
            dv_done[hu] = dv
            P.wait("vector", ES_S, dv)
            if hu >= 2:
                P.wait("vector", ES_T, bc_done[hu - 2])
            rv = P.op("vector", "reciprocal", out=rsb[hu % 2], in_=dsb[hu % 2], sem=ES_V)
            rv_done[hu] = rv
            P.wait("tensor", ES_V, rv)
            if hu >= 1:
                P.wait("tensor", ES_V, bcb_done[hu - 1])
            bc = P.op("tensor", "matmul", psB, ones_row_f, rsb[hu % 2], start=True,
                      stop=True, sem=ES_T)
            bc_done[hu] = bc
            P.wait("vector", ES_T, bc)
            if hu >= 2:
                P.wait("vector", ES_V, yev[hu - 2])
            cb = P.op("vector", "tensor_copy", out=bcb[hu % 2], in_=psB, sem=ES_V)
            bcb_done[hu] = cb
            P.wait("vector", ES_V, cb)
            P.wait("vector", ES_T, last_pv)
            if hu >= 2:
                P.wait("vector", stq[hu % 2], yst[hu - 2])
            yv = P.op("vector", "tensor_tensor", out=ysb[hu % 2], in0=psO,
                      in1=bcb[hu % 2], op=ALU.mult, sem=ES_V)
            yev[hu] = yv
            P.wait("gpsimd", ES_V, yv)
            yst[hu] = P.op("gpsimd", "dma_start", out=y_d[:, h, qt * 512:(qt + 1) * 512],
                           in_=ysb[hu % 2], sem=stq[hu % 2])

        def emit_pvden(i):
            un = units[i]
            qt, h, kb, half, hu, first, last, diag = un
            if diag:
                P.wait("tensor", ES_V, mask_done[i])
            else:
                P.wait("tensor", ES_S, exp_done[i])
            if first:
                if hu >= 1:
                    P.wait("tensor", ES_V, yev[hu - 1])
                if hu >= 2:
                    P.wait("tensor", ES_S, dv_done[hu - 2])
            for j in range(2):
                kc = half * 2 + j
                P.op("tensor", "matmul", psO,
                     vsh[:, (kb * 4 + kc) * 128:(kb * 4 + kc + 1) * 128],
                     PT[i % 2][:, j * 512:(j + 1) * 512],
                     start=(first and j == 0), stop=(last and j == 1))
            for j in range(2):
                P.op("tensor", "matmul", psD[hu % 2], ones_col,
                     PT[i % 2][:, j * 512:(j + 1) * 512],
                     start=(first and j == 0), stop=(last and j == 1))
            pvden_done[i] = P.einc("tensor")
            if last:
                emit_head_end(un, pvden_done[i])

        load_q(0)
        for i in range(NU):
            un = units[i]
            qt, h, kb, half, hu, first, last, diag = un
            if h == 0 and first:
                if qt + 1 < 8:
                    load_q(qt + 1)
            if i >= 2:
                P.wait("tensor", ES_S, exp_done[i - 2])
            if h == 0 and first:
                P.wait("tensor", qld[qt % 2], q_loaded[qt])
            for j in range(2):
                kc = half * 2 + j
                P.op("tensor", "matmul", psS[i % 2][:, j * 512:(j + 1) * 512],
                     ksh[:, (kb * 4 + kc) * 128:(kb * 4 + kc + 1) * 128],
                     qt_[qt % 2][:, h * 512:(h + 1) * 512], start=True, stop=True)
            s_mm[i] = P.einc("tensor")
            if h == 15 and last:
                qrec[qt] = s_mm[i]
            if i >= 2:
                emit_pvden(i - 2)
            P.wait("scalar", ES_T, s_mm[i])
            if i >= 2:
                P.wait("scalar", ES_T, pvden_done[i - 2])
            ex = P.op("scalar", "activation", out=PT[i % 2], in_=psS[i % 2], func=AF.Exp,
                      sem=ES_S)
            exp_done[i] = ex
            if diag:
                P.wait("vector", ES_S, ex)
                mask_done[i] = P.op("vector", "tensor_tensor", out=PT[i % 2],
                                    in0=PT[i % 2],
                                    in1=maskM[:, half * 1024:(half + 1) * 1024],
                                    op=ALU.mult, sem=ES_V)
        emit_pvden(NU - 2)
        emit_pvden(NU - 1)
        P.barrier()



def emit_p6(P, ctx):
    nc = P.nc
    x_d = P.dram("xres").rearrange("(c p) t -> p c t", p=128)
    s_d = P.dram("s0").rearrange("(c p) t -> p c t", p=128)
    y_d = P.dram("y0n").rearrange("(c p) t -> p c t", p=128)
    x1_d = P.dram("x1").rearrange("(c p) t -> p c t", p=128)
    xn_d = P.dram("xnbf").rearrange("(c p) t -> p c t", p=128)
    ones_col, ones_row_f = ctx["ones_col"], ctx["ones_row_f"]
    inv_d = P.const(1.0 / D)
    eps = P.const(1e-5)
    ES_T, ES_V, ES_S = P.esem["tensor"], P.esem["vector"], P.esem["scalar"]
    TT = 256
    NT = S // TT
    with ExitStack() as st:
        xt = [st.enter_context(nc.sbuf_tensor(f"p6_x{i}", [128, DC * TT], F32))[:] for i in range(2)]
        stt_ = [st.enter_context(nc.sbuf_tensor(f"p6_s{i}", [128, DC * TT], F32))[:] for i in range(2)]
        yt = [st.enter_context(nc.sbuf_tensor(f"p6_y{i}", [128, DC * TT], F32))[:] for i in range(2)]
        gt8 = st.enter_context(nc.sbuf_tensor("p6_g8", [128, DC * TT], F32))[:]
        nt8 = st.enter_context(nc.sbuf_tensor("p6_n8", [128, DC * TT], F32))[:]
        x1t = [st.enter_context(nc.sbuf_tensor(f"p6_o{i}", [128, DC * TT], F32))[:] for i in range(2)]
        x1b = st.enter_context(nc.sbuf_tensor("p6_b", [128, DC * TT], BF16))[:]
        sqb = st.enter_context(nc.sbuf_tensor("p6_q", [128, DC * TT], BF16))[:]
        row = st.enter_context(nc.sbuf_tensor("p6_r", [1, 8 * TT], F32))[:]
        bcb = [st.enter_context(nc.sbuf_tensor(f"p6_c{i}", [128, 2 * TT], F32))[:] for i in range(2)]
        onb = [st.enter_context(nc.sbuf_tensor(f"p6_n{i}", [128, DC * TT], BF16))[:] for i in range(2)]
        psS = st.enter_context(nc.psum_tensor("p6_pS", [1, 2 * TT], F32))[:]
        psB = st.enter_context(nc.psum_tensor("p6_pB", [128, 2 * TT], F32))[:]
        ld = [P.pool_sem(), P.pool_sem()]
        stq = [P.pool_sem(), P.pool_sem()]
        rec = {}
        ldt = {}

        def issue_ld(i):
            if i >= 2:
                P.wait("sync", ES_V, rec[i - 2]["v_main"])
                P.wait("sync", ES_S, rec[i - 2]["s_done"])
            dma_split3(P, xt[i % 2].rearrange("p (c n) -> p c n", c=DC),
                       x_d[:, :, i * TT:(i + 1) * TT], ld[i % 2], 4)
            dma_split3(P, stt_[i % 2].rearrange("p (c n) -> p c n", c=DC),
                       s_d[:, :, i * TT:(i + 1) * TT], ld[i % 2], 4)
            ldt[i] = dma_split3(P, yt[i % 2].rearrange("p (c n) -> p c n", c=DC),
                                y_d[:, :, i * TT:(i + 1) * TT], ld[i % 2], 4)

        def norm_tail(i):
            r = rec[i]
            P.wait("vector", ES_T, r["t_bc"])
            P.op("vector", "tensor_copy", out=bcb[i % 2][:, 0:TT], in_=psB[:, 0:TT])
            P.op("vector", "tensor_copy", out=bcb[i % 2][:, TT:2 * TT], in_=psB[:, TT:2 * TT])
            P.selfbar("vector")
            if i >= 2:
                P.wait("vector", stq[i % 2], rec[i - 2]["st"])
            P.op("vector", "tensor_tensor", out=nt8.rearrange("p (c n) -> p c n", c=DC),
                 in0=x1t[i % 2].rearrange("p (c n) -> p c n", c=DC),
                 in1=bcb[i % 2][:, 0:TT].unsqueeze(1).broadcast_to([128, DC, TT]),
                 op=ALU.subtract)
            P.selfbar("vector")
            P.op("vector", "tensor_tensor", out=onb[i % 2].rearrange("p (c n) -> p c n", c=DC),
                 in0=nt8.rearrange("p (c n) -> p c n", c=DC),
                 in1=bcb[i % 2][:, TT:2 * TT].unsqueeze(1).broadcast_to([128, DC, TT]),
                 op=ALU.mult)
            P.selfbar("vector")
            r["v_done"] = P.cnt[ES_V]
            P.wait("gpsimd", ES_V, r["v_done"])
            dma_split3(P, x1_d[:, :, i * TT:(i + 1) * TT],
                       x1t[i % 2].rearrange("p (c n) -> p c n", c=DC), stq[i % 2], 4,
                       eng="gpsimd")
            r["st"] = dma_split3(P, xn_d[:, :, i * TT:(i + 1) * TT],
                                 onb[i % 2].rearrange("p (c n) -> p c n", c=DC), stq[i % 2], 2,
                                 eng="gpsimd")

        issue_ld(0)
        for i in range(NT):
            b = i % 2
            r = {}
            r["ld"] = ldt[i]
            P.wait("scalar", ld[b], r["ld"])
            P.wait("vector", ld[b], r["ld"])
            if i >= 1:
                P.wait("vector", ES_T, rec[i - 1]["t_bc"])
                P.wait("vector", ES_S, rec[i - 1]["s_done"])
                P.wait("scalar", ES_T, rec[i - 1]["t_bc"])
                P.wait("scalar", ES_V, rec[i - 1]["v_main"])
            sg = P.op("scalar", "activation", out=gt8, in_=yt[b], func=AF.Silu, sem=ES_S)
            P.wait("vector", ES_S, sg)
            P.op("vector", "tensor_tensor", out=gt8, in0=stt_[b], in1=gt8, op=ALU.mult)
            P.selfbar("vector")
            if i >= 2:
                P.wait("vector", stq[b], rec[i - 2]["st"])
            P.op("vector", "tensor_tensor", out=x1t[b], in0=xt[b], in1=gt8, op=ALU.add)
            P.selfbar("vector")
            P.op("vector", "tensor_copy", out=x1b, in_=x1t[b], sem=ES_V)
            cv = P.cnt[ES_V]
            r["v_main"] = cv
            rec[i] = r
            if i >= 1:
                norm_tail(i - 1)
            if i + 1 < NT:
                issue_ld(i + 1)
            P.wait("scalar", ES_V, cv)
            sq = P.op("scalar", "activation", out=sqb, in_=x1b, func=AF.Square, sem=ES_S)
            r["s_done"] = sq
            P.wait("tensor", ES_V, cv)
            P.wait("tensor", ES_S, sq)
            if i >= 1:
                P.wait("tensor", ES_V, rec[i - 1]["v_row"])
            for c in range(DC):
                P.op("tensor", "matmul", psS[:, 0:TT], ones_col, x1b[:, c * TT:(c + 1) * TT],
                     start=(c == 0), stop=(c == DC - 1))
            for c in range(DC):
                P.op("tensor", "matmul", psS[:, TT:2 * TT], ones_col, sqb[:, c * TT:(c + 1) * TT],
                     start=(c == 0), stop=(c == DC - 1))
            tst = P.einc("tensor")
            P.wait("vector", ES_T, tst)
            P.op("vector", "tensor_single_scalar", out=row[:, 0:TT], in_=psS[:, 0:TT],
                 scalar=inv_d[0:1], op=ALU.mult)
            P.op("vector", "tensor_single_scalar", out=row[:, TT:2 * TT], in_=psS[:, TT:2 * TT],
                 scalar=inv_d[0:1], op=ALU.mult)
            P.selfbar("vector")
            P.op("vector", "tensor_tensor", out=row[:, 2 * TT:3 * TT], in0=row[:, 0:TT],
                 in1=row[:, 0:TT], op=ALU.mult)
            P.selfbar("vector")
            vv = P.op("vector", "tensor_tensor", out=row[:, 3 * TT:4 * TT], in0=row[:, TT:2 * TT],
                      in1=row[:, 2 * TT:3 * TT], op=ALU.subtract, sem=ES_V)
            P.wait("scalar", ES_V, vv)
            sv = P.op("scalar", "activation", out=row[:, 4 * TT:5 * TT], in_=row[:, 3 * TT:4 * TT],
                      func=AF.Sqrt, bias=eps[0:1], sem=ES_S)
            P.wait("vector", ES_S, sv)
            rv = P.op("vector", "reciprocal", out=row[:, 5 * TT:6 * TT], in_=row[:, 4 * TT:5 * TT],
                      sem=ES_V)
            r["v_row"] = rv
            P.wait("tensor", ES_V, rv)
            P.op("tensor", "matmul", psB[:, 0:TT], ones_row_f, row[:, 0:TT], start=True, stop=True)
            r["t_bc"] = P.op("tensor", "matmul", psB[:, TT:2 * TT], ones_row_f,
                             row[:, 5 * TT:6 * TT], start=True, stop=True, sem=ES_T)
            rec[i] = r
        norm_tail(NT - 1)
        P.barrier()


def emit_p7a(P, ctx):
    emit_proj_fm(P, ctx, "L1_attn_W_in", "xnbf", "xh", 2048, 2048,
                 bias_base=ctx["VEC"]["b_in"], out_f32=True, tag="p7a")



def emit_p7g(P, ctx):
    nc = P.nc
    xh_d = P.dram("xh").rearrange("(c p) t -> p c t", p=128)
    gx_d = P.dram("glx").rearrange("(c p) t -> p c t", p=128)
    ga_d = P.dram("gla").rearrange("(c p) t -> p c t", p=128)
    VEC = ctx["VEC"]
    vecs = ctx["vecs"]
    ES_T, ES_V, ES_S = P.esem["tensor"], P.esem["vector"], P.esem["scalar"]
    TT = 512
    with ExitStack() as st:
        wg = [st.enter_context(nc.sbuf_tensor(f"p7g_w{i}", [128, 16 * 128], BF16))[:] for i in range(2)]
        wst = st.enter_context(nc.sbuf_tensor("p7g_ws", [128, 16 * 128], F32))[:]
        xt = [st.enter_context(nc.sbuf_tensor(f"p7g_x{i}", [128, DC * TT], F32))[:] for i in range(2)]
        xb = [st.enter_context(nc.sbuf_tensor(f"p7g_b{i}", [128, DC * TT], BF16))[:] for i in range(2)]
        ot = [st.enter_context(nc.sbuf_tensor(f"p7g_o{i}", [128, 16 * TT], F32))[:] for i in range(2)]
        ps = [st.enter_context(nc.psum_tensor(f"p7g_p{i}", [128, TT], F32))[:] for i in range(2)]
        wld = P.pool_sem()
        ld = [P.pool_sem(), P.pool_sem()]
        stq = [P.pool_sem(), P.pool_sem()]
        for gi, nm in enumerate(("L1_attn_Wg_x", "L1_attn_Wg_a")):
            w_d = P.dram(nm)
            if gi == 1:
                P.wait("sync", ES_V, P.cnt[ES_V])
            t = P.op("sync", "dma_start", out=wst.rearrange("p (a b) -> p a b", a=16),
                     in_=w_d.rearrange("i (kc p) f -> p (i kc) f", p=128), sem=wld)
            P.wait("vector", wld, t)
            P.op("vector", "tensor_copy", out=wg[gi], in_=wst, sem=ES_V)
        P.wait("tensor", ES_V, P.cnt[ES_V])
        rec = {}
        ev = {}
        for i in range(8):
            b = i % 2
            r = {}
            if i >= 2:
                P.wait("sync", ES_T, rec[i - 2]["mm"])
            r["ld"] = dma_split3(P, xt[b].rearrange("p (c n) -> p c n", c=DC),
                                 xh_d[:, :, i * TT:(i + 1) * TT], ld[b], 8)
            P.wait("vector", ld[b], r["ld"])
            if i >= 2:
                P.wait("vector", ES_T, rec[i - 2]["mm"])
            cv = P.op("vector", "tensor_copy", out=xb[b], in_=xt[b], sem=ES_V)
            P.wait("tensor", ES_V, cv)
            for gi in range(2):
                for blk in range(8):
                    j = (i * 2 + gi) * 8 + blk
                    pb = ps[j % 2]
                    if j >= 2:
                        P.wait("tensor", ES_V, ev[j - 2])
                    for kc in range(2):
                        P.op("tensor", "matmul", pb, wg[gi][:, (blk * 2 + kc) * 128:(blk * 2 + kc + 1) * 128],
                             xb[b][:, (blk * 2 + kc) * TT:(blk * 2 + kc + 1) * TT],
                             start=(kc == 0), stop=(kc == 1))
                    mm = P.einc("tensor")
                    r["mm"] = mm
                    P.wait("vector", ES_T, mm)
                    if gi == 0 and blk == 0 and i >= 2:
                        P.wait("vector", stq[b], rec[i - 2]["st"])
                    bb = VEC["bg_x"] if gi == 0 else VEC["bg_a"]
                    ev[j] = P.op("vector", "tensor_single_scalar",
                                 out=ot[b][:, (gi * 8 + blk) * TT:(gi * 8 + blk + 1) * TT],
                                 in_=pb, scalar=vecs[:, bb + blk:bb + blk + 1], op=ALU.add,
                                 sem=ES_V)
            P.wait("sync", ES_V, ev[(i * 2 + 1) * 8 + 7])
            dma_split3(P, gx_d[:, :, i * TT:(i + 1) * TT],
                       ot[b][:, 0:8 * TT].rearrange("p (c n) -> p c n", c=8), stq[b], 4)
            r["st"] = dma_split3(P, ga_d[:, :, i * TT:(i + 1) * TT],
                                 ot[b][:, 8 * TT:16 * TT].rearrange("p (c n) -> p c n", c=8), stq[b], 4)
            rec[i] = r
        P.barrier()



def emit_p7b(P, ctx):
    nc = P.nc
    xh_d = P.dram("xh").rearrange("(c p) t -> p c t", p=128)
    gx_d = P.dram("glx").rearrange("(c p) t -> p c t", p=128)
    ga_d = P.dram("gla").rearrange("(c p) t -> p c t", p=128)
    h_d = P.dram("hsc").rearrange("(c p) t -> p c t", p=128)
    VEC = ctx["VEC"]
    vecs = ctx["vecs"]
    ES_T, ES_V, ES_S = P.esem["tensor"], P.esem["vector"], P.esem["scalar"]
    TT = 256
    NT = S // TT
    TWO_PI = float(2 * np.pi)
    INV_2PI = P.constf(1.0 / (2 * np.pi))
    NEG_2PI = P.constf(-TWO_PI)
    HALF_PI = P.const(np.pi / 2)
    with ExitStack() as st:
        xt = [st.enter_context(nc.sbuf_tensor(f"p7b_x{i}", [128, DC * TT], F32))[:] for i in range(2)]
        gxt = [st.enter_context(nc.sbuf_tensor(f"p7b_g{i}", [128, 8 * TT], F32))[:] for i in range(2)]
        gat = [st.enter_context(nc.sbuf_tensor(f"p7b_a{i}", [128, 8 * TT], F32))[:] for i in range(2)]
        hb = [st.enter_context(nc.sbuf_tensor(f"p7b_h{i}", [128, DC * TT], BF16))[:] for i in range(2)]
        la8 = st.enter_context(nc.sbuf_tensor("p7b_la8", [128, 8 * TT], F32))[:]
        phi8 = st.enter_context(nc.sbuf_tensor("p7b_ph8", [128, 8 * TT], F32))[:]
        nrm8 = st.enter_context(nc.sbuf_tensor("p7b_nr8", [128, 8 * TT], F32))[:]
        psis8 = st.enter_context(nc.sbuf_tensor("p7b_ps8", [128, 8 * TT], F32))[:]
        cosp8 = st.enter_context(nc.sbuf_tensor("p7b_co8", [128, 8 * TT], F32))[:]
        sinp8 = st.enter_context(nc.sbuf_tensor("p7b_si8", [128, 8 * TT], F32))[:]
        m8 = st.enter_context(nc.sbuf_tensor("p7b_m8", [128, 8 * TT], F32))[:]
        tw8 = st.enter_context(nc.sbuf_tensor("p7b_tw8", [128, 8 * TT], F32))[:]
        xr8 = st.enter_context(nc.sbuf_tensor("p7b_xr8", [128, 8 * TT], F32))[:]
        xi8 = st.enter_context(nc.sbuf_tensor("p7b_xi8", [128, 8 * TT], F32))[:]
        ur8 = st.enter_context(nc.sbuf_tensor("p7b_ur8", [128, 8 * TT], F32))[:]
        mi8 = st.enter_context(nc.sbuf_tensor("p7b_mi8", [128, 8 * TT], I32))[:]
        onet = st.enter_context(nc.sbuf_tensor("p7b_one", [128, TT], F32))[:]
        carry = st.enter_context(nc.sbuf_tensor("p7b_cr", [128, 24], F32))[:]
        ld = [P.pool_sem(), P.pool_sem()]
        stq = [P.pool_sem(), P.pool_sem()]
        P.op("gpsimd", "memset", onet, 1.0)
        P.op("gpsimd", "memset", carry, 0.0, sem=P.esem["gpsimd"])
        P.wait("vector", P.esem["gpsimd"], P.cnt[P.esem["gpsimd"]])
        rec = {}
        sC_prev = sE_prev = None
        ldt = {}

        def issue_ld(i):
            if i >= 2:
                P.wait("sync", ES_V, rec[i - 2]["v_done"])
                P.wait("sync", ES_S, rec[i - 2]["s_done"])
            dma_split3(P, xt[i % 2].rearrange("p (c n) -> p c n", c=DC),
                       xh_d[:, :, i * TT:(i + 1) * TT], ld[i % 2], 4)
            dma_split3(P, gxt[i % 2].rearrange("p (c n) -> p c n", c=8),
                       gx_d[:, :, i * TT:(i + 1) * TT], ld[i % 2], 2)
            ldt[i] = dma_split3(P, gat[i % 2].rearrange("p (c n) -> p c n", c=8),
                                ga_d[:, :, i * TT:(i + 1) * TT], ld[i % 2], 2)

        issue_ld(0)
        for i in range(NT):
            b = i % 2
            r = {}
            if i + 1 < NT:
                issue_ld(i + 1)
            r["ld"] = ldt[i]
            P.wait("scalar", ld[b], r["ld"])
            P.op("scalar", "activation", out=gat[b], in_=gat[b], func=AF.Sigmoid)
            sA = P.op("scalar", "activation", out=gxt[b], in_=gxt[b], func=AF.Sigmoid,
                      sem=ES_S)
            P.wait("vector", ES_S, sA)
            if sC_prev is not None:
                P.wait("vector", ES_S, sC_prev)
            for cc in range(8):
                slg = slice(cc * TT, (cc + 1) * TT)
                P.op("vector", "tensor_scalar", out=la8[:, slg], in0=gat[b][:, slg],
                     scalar1=vecs[:, VEC["sp"] + cc:VEC["sp"] + cc + 1],
                     scalar2=-8.0, op0=ALU.mult, op1=ALU.mult)
                P.op("vector", "tensor_single_scalar", out=phi8[:, slg], in_=gat[b][:, slg],
                     scalar=vecs[:, VEC["aim"] + cc:VEC["aim"] + cc + 1], op=ALU.mult)
            vB = P.einc("vector")
            P.wait("scalar", ES_V, vB)
            P.op("scalar", "activation", out=nrm8, in_=la8, func=AF.Exp, scale=2.0)
            P.selfbar("scalar")
            P.op("scalar", "activation", out=la8, in_=la8, func=AF.Exp)
            sC = P.op("scalar", "activation", out=nrm8, in_=nrm8, func=AF.Sqrt,
                      scale=-1.0, bias=1.0, sem=ES_S)
            sC_prev = sC
            if sE_prev is not None:
                P.wait("vector", ES_S, sE_prev)
            for cc in range(8):
                slg = slice(cc * TT, (cc + 1) * TT)
                P.op("vector", "tensor_tensor_scan", out=phi8[:, slg], data0=onet,
                     data1=phi8[:, slg], initial=carry[:, cc:cc + 1],
                     op0=ALU.mult, op1=ALU.add)
            P.selfbar("vector")
            for cc in range(8):
                P.op("vector", "tensor_copy", out=carry[:, cc:cc + 1],
                     in_=phi8[:, cc * TT + TT - 1:cc * TT + TT])
            P.op("vector", "tensor_single_scalar", out=m8, in_=phi8,
                 scalar=INV_2PI, op=ALU.mult)
            P.selfbar("vector")
            P.op("vector", "tensor_copy", out=mi8, in_=m8)
            P.selfbar("vector")
            P.op("vector", "tensor_copy", out=m8, in_=mi8)
            P.selfbar("vector")
            P.op("vector", "scalar_tensor_tensor", out=psis8, in0=m8,
                 scalar=NEG_2PI, in1=phi8, op0=ALU.mult, op1=ALU.add)
            P.op("vector", "tensor_scalar", out=m8, in0=phi8, scalar1=INV_2PI,
                 scalar2=0.25, op0=ALU.mult, op1=ALU.add)
            P.selfbar("vector")
            P.op("vector", "tensor_copy", out=mi8, in_=m8)
            P.selfbar("vector")
            P.op("vector", "tensor_copy", out=m8, in_=mi8)
            P.selfbar("vector")
            P.op("vector", "scalar_tensor_tensor", out=cosp8, in0=m8,
                 scalar=NEG_2PI, in1=phi8, op0=ALU.mult, op1=ALU.add)
            vD = P.einc("vector")
            P.wait("scalar", ES_V, vD)
            P.op("scalar", "activation", out=sinp8, in_=psis8, func=AF.Sin)
            sE = P.op("scalar", "activation", out=cosp8, in_=cosp8, func=AF.Sin,
                      bias=HALF_PI, sem=ES_S)
            sE_prev = sE
            P.wait("vector", ES_S, sE)
            if i >= 2:
                P.wait("vector", stq[b], rec[i - 2]["st"])
            P.op("vector", "tensor_tensor", out=tw8, in0=gxt[b], in1=nrm8, op=ALU.mult)
            P.selfbar("vector")
            P.op("vector", "tensor_tensor", out=xr8, in0=xt[b][:, 0:8 * TT],
                 in1=tw8, op=ALU.mult)
            P.op("vector", "tensor_tensor", out=xi8, in0=xt[b][:, 8 * TT:16 * TT],
                 in1=tw8, op=ALU.mult)
            P.selfbar("vector")
            P.op("vector", "tensor_tensor", out=ur8, in0=xr8, in1=cosp8, op=ALU.mult)
            P.op("vector", "tensor_tensor", out=tw8, in0=xi8, in1=sinp8, op=ALU.mult)
            P.selfbar("vector")
            P.op("vector", "tensor_tensor", out=ur8, in0=ur8, in1=tw8, op=ALU.add)
            P.op("vector", "tensor_tensor", out=tw8, in0=xi8, in1=cosp8, op=ALU.mult)
            P.selfbar("vector")
            P.op("vector", "tensor_tensor", out=xi8, in0=xr8, in1=sinp8, op=ALU.mult)
            P.selfbar("vector")
            P.op("vector", "tensor_tensor", out=xi8, in0=tw8, in1=xi8, op=ALU.subtract)
            P.selfbar("vector")
            for cc in range(8):
                slg = slice(cc * TT, (cc + 1) * TT)
                P.op("vector", "tensor_tensor_scan", out=ur8[:, slg], data0=la8[:, slg],
                     data1=ur8[:, slg], initial=carry[:, 8 + cc:9 + cc],
                     op0=ALU.mult, op1=ALU.add)
                P.op("vector", "tensor_tensor_scan", out=xi8[:, slg], data0=la8[:, slg],
                     data1=xi8[:, slg], initial=carry[:, 16 + cc:17 + cc],
                     op0=ALU.mult, op1=ALU.add)
            P.selfbar("vector")
            for cc in range(8):
                P.op("vector", "tensor_copy", out=carry[:, 8 + cc:9 + cc],
                     in_=ur8[:, cc * TT + TT - 1:cc * TT + TT])
                P.op("vector", "tensor_copy", out=carry[:, 16 + cc:17 + cc],
                     in_=xi8[:, cc * TT + TT - 1:cc * TT + TT])
            P.op("vector", "tensor_tensor", out=tw8, in0=ur8, in1=cosp8, op=ALU.mult)
            P.op("vector", "tensor_tensor", out=xr8, in0=xi8, in1=sinp8, op=ALU.mult)
            P.selfbar("vector")
            P.op("vector", "tensor_tensor", out=hb[b][:, 0:8 * TT], in0=tw8, in1=xr8,
                 op=ALU.subtract)
            P.selfbar("vector")
            P.op("vector", "tensor_tensor", out=tw8, in0=xi8, in1=cosp8, op=ALU.mult)
            P.op("vector", "tensor_tensor", out=xr8, in0=ur8, in1=sinp8, op=ALU.mult)
            P.selfbar("vector")
            P.op("vector", "tensor_tensor", out=hb[b][:, 8 * TT:16 * TT], in0=tw8, in1=xr8,
                 op=ALU.add)
            P.selfbar("vector")
            r["v_done"] = P.cnt[ES_V]
            r["s_done"] = P.cnt[ES_S]
            P.wait("gpsimd", ES_V, r["v_done"])
            r["st"] = dma_split3(P, h_d[:, :, i * TT:(i + 1) * TT],
                                 hb[b].rearrange("p (c n) -> p c n", c=DC), stq[b], 2,
                                 eng="gpsimd")
            rec[i] = r
        P.barrier()


def emit_p7c(P, ctx):
    emit_proj_fm(P, ctx, "L1_attn_W_out", "hsc", "s0", 2048, 2048,
                 bias_base=ctx["VEC"]["b_out"], out_f32=True, tag="p7c")



def emit_dequant(P, ctx, q_name, s_name, out_name, R, C, tag):
    nc = P.nc
    q_d = P.dram(q_name)
    s_d = P.dram(s_name)
    o_d = P.dram(out_name)
    idb = ctx["idb"]
    ES_T, ES_V, ES_S = P.esem["tensor"], P.esem["vector"], P.esem["scalar"]
    NCH = R // 128
    SC = C // 64
    with ExitStack() as st:
        qi = [st.enter_context(nc.sbuf_tensor(f"{tag}_q{i}", [128, C], I32))[:] for i in range(2)]
        sc_ = [st.enter_context(nc.sbuf_tensor(f"{tag}_s{i}", [128, SC], F32))[:] for i in range(2)]
        dq = [st.enter_context(nc.sbuf_tensor(f"{tag}_d{i}", [128, C], BF16))[:] for i in range(2)]
        ev = [st.enter_context(nc.sbuf_tensor(f"{tag}_e{i}", [128, C], BF16))[:] for i in range(2)]
        psT = [st.enter_context(nc.psum_tensor(f"{tag}_p{i}", [128, 2048], BF16))[:] for i in range(2)]
        ld = [P.pool_sem(), P.pool_sem()]
        stq = [P.pool_sem(), P.pool_sem()]
        rec = {}
        pT_copy = [None, None]
        ldt = {}

        def issue_ld(c):
            if c >= 2:
                P.wait("sync", ES_V, rec[c - 2]["dq"])
            dma_split2(P, qi[c % 2], q_d[c * 128:(c + 1) * 128, :], ld[c % 2], max(2, C // 1024))
            ldt[c] = P.op("sync", "dma_start", out=sc_[c % 2], in_=s_d[c * 128:(c + 1) * 128, :],
                          sem=ld[c % 2])

        issue_ld(0)
        for c in range(NCH):
            b = c % 2
            r = {}
            if c + 1 < NCH:
                issue_ld(c + 1)
            r["ld"] = ldt[c]
            P.wait("vector", ld[b], r["ld"])
            if c >= 2:
                P.wait("vector", ES_T, rec[c - 2]["tr"])
            r["dq"] = P.op("vector", "scalar_tensor_tensor",
                           out=dq[b].rearrange("p (a z) -> p a z", z=64),
                           in0=qi[b].rearrange("p (a z) -> p a z", z=64),
                           scalar=P.constf(-7.5),
                           in1=sc_[b].unsqueeze(2).broadcast_to([128, SC, 64]),
                           op0=ALU.add, op1=ALU.mult, sem=ES_V)
            P.wait("tensor", ES_V, r["dq"])
            if c >= 2:
                P.wait("tensor", ES_S, rec[c - 2]["ev"])
            nb = C // 2048
            for bb_ in range(nb):
                par = (c * nb + bb_) % 2
                pT = psT[par]
                if pT_copy[par] is not None:
                    P.wait("tensor", ES_S, pT_copy[par])
                for tc in range(16):
                    P.op("tensor", "transpose", pT[:, tc * 128:(tc + 1) * 128],
                         dq[b][:, (bb_ * 16 + tc) * 128:(bb_ * 16 + tc + 1) * 128], idb)
                r["tr"] = P.einc("tensor")
                P.wait("scalar", ES_T, r["tr"])
                if c >= 2 and bb_ == 0:
                    P.wait("scalar", stq[b], rec[c - 2]["st"])
                r["ev"] = P.op("scalar", "copy", out=ev[b][:, bb_ * 2048:(bb_ + 1) * 2048],
                               in_=pT, sem=ES_S)
                pT_copy[par] = r["ev"]
            P.wait("gpsimd", ES_S, r["ev"])
            r["st"] = dma_split2(P, o_d[c], ev[b], stq[b], max(2, C // 2048), eng="gpsimd")
            rec[c] = r
        P.barrier()


def emit_p8a(P, ctx):
    emit_dequant(P, ctx, "L1_ffn_w_up_q", "L1_ffn_w_up_scale", "wupT3", 8192, 2048, "p8a")


def emit_p8b(P, ctx):
    emit_dequant(P, ctx, "L1_ffn_w_down_q", "L1_ffn_w_down_scale", "wdnT3", 2048, 8192, "p8b")



def emit_p8c(P, ctx):
    nc = P.nc
    a_d = P.dram("L1_ffn_w_up_lora_a")
    x_d = P.dram("xnbf").rearrange("(c p) t -> p c t", p=128)
    xa_d = P.dram("xa")
    ES_T, ES_V, ES_S = P.esem["tensor"], P.esem["vector"], P.esem["scalar"]
    with ExitStack() as st:
        wa = st.enter_context(nc.sbuf_tensor("p8c_w", [128, DC * 16], BF16))[:]
        ws = st.enter_context(nc.sbuf_tensor("p8c_s", [128, DC * 16], F32))[:]
        xt = [st.enter_context(nc.sbuf_tensor(f"p8c_x{i}", [128, DC * 512], BF16))[:] for i in range(2)]
        ot = [st.enter_context(nc.sbuf_tensor(f"p8c_o{i}", [16, 512], F32))[:] for i in range(2)]
        ps = [st.enter_context(nc.psum_tensor(f"p8c_p{i}", [16, 512], F32))[:] for i in range(2)]
        wld = P.pool_sem()
        ld = [P.pool_sem(), P.pool_sem()]
        stq = [P.pool_sem(), P.pool_sem()]
        t = P.op("sync", "dma_start", out=ws.rearrange("p (c n) -> p c n", c=DC),
                 in_=a_d.rearrange("(c p) r -> p c r", p=128), sem=wld)
        P.wait("vector", wld, t)
        cv = P.op("vector", "tensor_copy", out=wa, in_=ws, sem=ES_V)
        P.wait("tensor", ES_V, cv)
        rec = {}
        ldt = {}

        def issue_ld(i):
            if i >= 2:
                P.wait("sync", ES_T, rec[i - 2]["mm"])
                P.wait("sync", stq[i % 2], rec[i - 2]["st"])
            ldt[i] = dma_split3(P, xt[i % 2].rearrange("p (c n) -> p c n", c=DC),
                                x_d[:, :, i * 512:(i + 1) * 512], ld[i % 2], 4)

        issue_ld(0)
        for i in range(8):
            b = i % 2
            r = {}
            if i + 1 < 8:
                issue_ld(i + 1)
            r["ld"] = ldt[i]
            P.wait("tensor", ld[b], r["ld"])
            if i >= 2:
                P.wait("tensor", ES_V, rec[i - 2]["ev"])
            for kc in range(DC):
                P.op("tensor", "matmul", ps[b], wa[:, kc * 16:(kc + 1) * 16],
                     xt[b][:, kc * 512:(kc + 1) * 512], start=(kc == 0), stop=(kc == DC - 1))
            r["mm"] = P.einc("tensor")
            P.wait("vector", ES_T, r["mm"])
            if i >= 2:
                P.wait("vector", stq[b], rec[i - 2]["st"])
            r["ev"] = P.op("vector", "tensor_copy", out=ot[b], in_=ps[b], sem=ES_V)
            P.wait("sync", ES_V, r["ev"])
            r["st"] = P.op("sync", "dma_start", out=xa_d[:, i * 512:(i + 1) * 512], in_=ot[b],
                           sem=stq[b])
            rec[i] = r
        P.barrier()



def emit_p9(P, ctx):
    nc = P.nc
    w_d = P.dram("wupT3")
    x_d = P.dram("xnbf").rearrange("(c p) t -> p c t", p=128)
    x2_d = P.dram("x2").rearrange("(c p) t -> p c t", p=128)
    xda_d = P.dram("xda")
    a_d = P.dram("L1_ffn_w_up_lora_a")
    lb_d = P.dram("L1_ffn_w_up_lora_b")
    wda_d = P.dram("L1_ffn_w_down_lora_a")
    VEC = ctx["VEC"]
    vecs = ctx["vecs"]
    ES_T, ES_V, ES_S = P.esem["tensor"], P.esem["vector"], P.esem["scalar"]
    with ExitStack() as st:
        xn = st.enter_context(nc.sbuf_tensor("p9_xn", [128, DC * S], BF16))[:]
        lb = st.enter_context(nc.sbuf_tensor("p9_lb", [16, 8192], BF16))[:]
        xab = st.enter_context(nc.sbuf_tensor("p9_xa", [16, S], BF16))[:]
        wda = st.enter_context(nc.sbuf_tensor("p9_wda", [128, 64 * 16], BF16))[:]
        xda = st.enter_context(nc.sbuf_tensor("p9_xda", [16, S], F32))[:]
        wda_a = st.enter_context(nc.sbuf_tensor("p9_waa", [128, DC * 16], BF16))[:]
        xab_ev = {}
        wub = [st.enter_context(nc.sbuf_tensor(f"p9_w{i}", [128, DC * 128], BF16))[:] for i in range(2)]
        x2t = [st.enter_context(nc.sbuf_tensor(f"p9_o{i}", [128, 512], BF16))[:] for i in range(4)]
        ps = [st.enter_context(nc.psum_tensor(f"p9_p{i}", [128, 512], F32))[:] for i in range(4)]
        psA = [st.enter_context(nc.psum_tensor(f"p9_a{i}", [16, 512], F32))[:] for i in range(2)]
        wld = [P.pool_sem(), P.pool_sem()]
        ld = P.pool_sem()
        stq = [P.pool_sem() for _ in range(4)]
        with ExitStack() as st2:
            stg = st2.enter_context(nc.sbuf_tensor("p9_stg", [128, 1024], F32))[:]
            for c8 in range(8):
                dma_split3(P, xn.rearrange("p (c n) -> p c n", c=DC)[:, :, c8 * 512:(c8 + 1) * 512],
                           x_d[:, :, c8 * 512:(c8 + 1) * 512], ld, 4)
            jobs = []
            for sl in range(8):
                jobs.append((lb[:, sl * 1024:(sl + 1) * 1024], lb_d[:, sl * 1024:(sl + 1) * 1024], 16))
            for dst, srcap, pp in jobs:
                t = P.op("sync", "dma_start", out=stg[0:pp, 0:dst.shape[1]], in_=srcap, sem=ld)
                P.wait("vector", ld, t)
                cv = P.op("vector", "tensor_copy", out=dst, in_=stg[0:pp, 0:dst.shape[1]], sem=ES_V)
                P.wait("sync", ES_V, cv)
            t = P.op("sync", "dma_start", out=stg.rearrange("p (c r) -> p c r", c=64),
                     in_=wda_d.rearrange("(c p) r -> p c r", p=128), sem=ld)
            P.wait("vector", ld, t)
            cv = P.op("vector", "tensor_copy", out=wda, in_=stg[:, 0:64 * 16], sem=ES_V)
            P.wait("sync", ES_V, cv)
            t = P.op("sync", "dma_start",
                     out=stg[:, 0:DC * 16].rearrange("p (c n) -> p c n", c=DC),
                     in_=a_d.rearrange("(c p) r -> p c r", p=128), sem=ld)
            P.wait("vector", ld, t)
            cv = P.op("vector", "tensor_copy", out=wda_a, in_=stg[:, 0:DC * 16], sem=ES_V)
            P.op("vector", "memset", xda, 0.0, sem=ES_V)
            vinit = P.cnt[ES_V]
            P.wait("tensor", ES_V, vinit)
            for i8 in range(8):
                pa = psA[i8 % 2]
                if i8 >= 2:
                    P.wait("tensor", ES_V, xab_ev[i8 - 2])
                for kc in range(DC):
                    P.op("tensor", "matmul", pa, wda_a[:, kc * 16:(kc + 1) * 16],
                         xn[:, kc * S + i8 * 512: kc * S + (i8 + 1) * 512],
                         start=(kc == 0), stop=(kc == DC - 1))
                tmm = P.einc("tensor")
                P.wait("vector", ES_T, tmm)
                xab_ev[i8] = P.op("vector", "tensor_copy",
                                  out=xab[:, i8 * 512:(i8 + 1) * 512], in_=pa, sem=ES_V)
            P.wait("tensor", ES_V, P.cnt[ES_V])
        rec = {}
        xda_add = {}
        xda_mm = {}
        ev_s = {}
        st_s = {}
        uinfo = {}
        u = 0
        wldt = {}

        def issue_w9(h2):
            if h2 >= 2:
                P.wait("sync", ES_T, rec[h2 - 2])
            wldt[h2] = dma_split2(P, wub[h2 % 2], w_d[h2], wld[h2 % 2], 2)

        issue_w9(0)
        for hc in range(64):
            bw = hc % 2
            if hc + 1 < 64:
                issue_w9(hc + 1)
            P.wait("tensor", wld[bw], wldt[hc])
            for tt in range(8):
                bu = u % 4
                ba = u % 2
                pb = ps[bu]
                if u >= 4:
                    P.wait("tensor", ES_S, ev_s.get(u - 4, 0))
                for kc in range(DC):
                    P.op("tensor", "matmul", pb, wub[bw][:, kc * 128:(kc + 1) * 128],
                         xn[:, kc * S + tt * 512: kc * S + tt * 512 + 512],
                         start=(kc == 0), stop=False)
                P.op("tensor", "matmul", pb, lb[:, hc * 128:(hc + 1) * 128],
                     xab[:, tt * 512:(tt + 1) * 512], start=False, stop=True)
                mm1 = P.einc("tensor")
                rec[hc] = mm1
                P.wait("scalar", ES_T, mm1)
                if u >= 4:
                    P.wait("scalar", stq[bu], st_s.get(u - 4, 0))
                    P.wait("scalar", ES_T, xda_mm.get(u - 4, 0))
                ev = P.op("scalar", "activation", out=x2t[bu], in_=pb, func=AF.Relu,
                          bias=vecs[:, VEC["b_up"] + hc:VEC["b_up"] + hc + 1], sem=ES_S)
                ev_s[u] = ev
                uinfo[u] = (hc, tt)
                if u >= 1:
                    emit_p9_xda(P, u - 1, uinfo, ev_s, xda_mm, xda_add, psA, wda, x2t, xda)
                P.wait("gpsimd", ES_S, ev)
                st_s[u] = P.op("gpsimd", "dma_start",
                               out=x2_d[:, hc, tt * 512:(tt + 1) * 512], in_=x2t[bu],
                               sem=stq[bu])
                u += 1
        emit_p9_xda(P, u - 1, uinfo, ev_s, xda_mm, xda_add, psA, wda, x2t, xda)
        P.wait("sync", ES_V, P.cnt[ES_V])
        P.op("sync", "dma_start", out=xda_d, in_=xda, sem=stq[0])
        P.barrier()


def emit_p9_xda(P, w, uinfo, ev_s, xda_mm, xda_add, psA, wda, x2t, xda):
    ES_T, ES_V, ES_S = P.esem["tensor"], P.esem["vector"], P.esem["scalar"]
    hc, tt = uinfo[w]
    P.wait("tensor", ES_S, ev_s[w])
    if w >= 2:
        P.wait("tensor", ES_V, xda_add.get(w - 2, 0))
    xmm = P.op("tensor", "matmul", psA[w % 2], wda[:, hc * 16:(hc + 1) * 16],
               x2t[w % 4], start=True, stop=True, sem=ES_T)
    xda_mm[w] = xmm
    P.wait("vector", ES_T, xmm)
    xda_add[w] = P.op("vector", "tensor_tensor", out=xda[:, tt * 512:(tt + 1) * 512],
                      in0=xda[:, tt * 512:(tt + 1) * 512], in1=psA[w % 2], op=ALU.add,
                      sem=ES_V)



def emit_p10(P, ctx):
    nc = P.nc
    w_d = P.dram("wdnT3")
    x2_d = P.dram("x2").rearrange("(c p) t -> p c t", p=128)
    x1_d = P.dram("x1").rearrange("(c p) t -> p c t", p=128)
    s2_d = P.dram("s0").rearrange("(c p) t -> p c t", p=128)
    xda_d = P.dram("xda")
    lb_d = P.dram("L1_ffn_w_down_lora_b")
    out_d = P.dram("out")
    idf = ctx["idf"]
    VEC = ctx["VEC"]
    vecs = ctx["vecs"]
    ES_T, ES_V, ES_S = P.esem["tensor"], P.esem["vector"], P.esem["scalar"]
    HC = 64
    with ExitStack() as st:
        x2t = [st.enter_context(nc.sbuf_tensor(f"p10_x{i}", [128, HC * 512], BF16))[:] for i in range(2)]
        wdb = [st.enter_context(nc.sbuf_tensor(f"p10_w{i}", [128, HC * 128], BF16))[:] for i in range(2)]
        lb = st.enter_context(nc.sbuf_tensor("p10_lb", [16, 2048], BF16))[:]
        xdab = st.enter_context(nc.sbuf_tensor("p10_xd", [16, S], BF16))[:]
        rt = [st.enter_context(nc.sbuf_tensor(f"p10_r{i}", [128, 512], F32))[:] for i in range(2)]
        qt2 = [st.enter_context(nc.sbuf_tensor(f"p10_q{i}", [128, 512], F32))[:] for i in range(2)]
        y2 = st.enter_context(nc.sbuf_tensor("p10_y2", [128, 512], F32))[:]
        g2 = st.enter_context(nc.sbuf_tensor("p10_g2", [128, 512], F32))[:]
        xo = [st.enter_context(nc.sbuf_tensor(f"p10_o{i}", [128, 512], F32))[:] for i in range(2)]
        ev = [st.enter_context(nc.sbuf_tensor(f"p10_e{i}", [128, 512], F32))[:] for i in range(2)]
        ps = [st.enter_context(nc.psum_tensor(f"p10_p{i}", [128, 512], F32))[:] for i in range(4)]
        psT = [st.enter_context(nc.psum_tensor(f"p10_t{i}", [128, 512], F32))[:] for i in range(2)]
        ld = [P.pool_sem(), P.pool_sem()]
        wld = [P.pool_sem(), P.pool_sem()]
        rld = [P.pool_sem(), P.pool_sem()]
        stq = [P.pool_sem(), P.pool_sem()]
        sld = P.pool_sem()
        with ExitStack() as st2:
            stg = st2.enter_context(nc.sbuf_tensor("p10_stg", [16, 2048], F32))[:]
            jobs = [(lb, lb_d)]
            for sl in range(2):
                jobs.append((xdab[:, sl * 2048:(sl + 1) * 2048], xda_d[:, sl * 2048:(sl + 1) * 2048]))
            for dst, srcap in jobs:
                t = P.op("sync", "dma_start", out=stg[:, 0:dst.shape[1]], in_=srcap, sem=sld)
                P.wait("vector", sld, t)
                cv = P.op("vector", "tensor_copy", out=dst, in_=stg[:, 0:dst.shape[1]], sem=ES_V)
                P.wait("sync", ES_V, cv)
            vinit = P.cnt[ES_V]
            P.wait("tensor", ES_V, vinit)
        rec = {}
        ev_v = {}
        tr_d = {}
        cp_d = {}
        st_d = {}
        wcons = {}
        y2ev = {}
        sil = {}
        uinfo = {}
        x2ld = {}
        wldt = {}

        def issue_x2(t2):
            if t2 >= 2:
                P.wait("sync", ES_T, rec[t2 - 2])
            x2ld[t2] = dma_split3(P, x2t[t2 % 2].rearrange("p (c n) -> p c n", c=HC),
                                  x2_d[:, :, t2 * 512:(t2 + 1) * 512], ld[t2 % 2], 8)

        def issue_w(u2):
            if u2 >= 2:
                P.wait("sync", ES_T, wcons.get(u2 - 2, 0))
            wldt[u2] = dma_split2(P, wdb[u2 % 2], w_d[u2 % 16], wld[u2 % 2], 8)

        issue_x2(0)
        issue_w(0)
        u = 0
        for tt in range(8):
            b = tt % 2
            if tt + 1 < 8:
                issue_x2(tt + 1)
            P.wait("tensor", ld[b], x2ld[tt])
            for dc in range(16):
                uinfo[u] = (tt, dc)
                bw = u % 2
                if u + 1 < 128:
                    issue_w(u + 1)
                if u >= 2:
                    P.wait("sync", ES_V, ev_v.get(u - 2, 0))
                rtg = P.op("sync", "dma_start", out=rt[u % 2],
                           in_=s2_d[:, dc, tt * 512:(tt + 1) * 512], sem=rld[u % 2])
                rtg = P.op("sync", "dma_start", out=qt2[u % 2],
                           in_=x1_d[:, dc, tt * 512:(tt + 1) * 512], sem=rld[u % 2])
                P.wait("tensor", wld[bw], wldt[u])
                pb = ps[u % 4]
                if u >= 4:
                    P.wait("tensor", ES_V, y2ev.get(u - 4, 0))
                for kc in range(HC):
                    P.op("tensor", "matmul", pb, wdb[bw][:, kc * 128:(kc + 1) * 128],
                         x2t[b][:, kc * 512:(kc + 1) * 512], start=(kc == 0), stop=False)
                P.op("tensor", "matmul", pb, lb[:, dc * 128:(dc + 1) * 128],
                     xdab[:, tt * 512:(tt + 1) * 512], start=False, stop=True)
                mm = P.einc("tensor")
                rec[tt] = mm
                wcons[u] = mm
                P.wait("vector", ES_T, mm)
                if u >= 1:
                    P.wait("vector", ES_S, sil.get(u - 1, 0))
                y2ev[u] = P.op("vector", "tensor_single_scalar", out=y2, in_=pb,
                               scalar=vecs[:, VEC["b_down"] + dc:VEC["b_down"] + dc + 1],
                               op=ALU.add, sem=ES_V)
                P.wait("scalar", ES_V, y2ev[u])
                sil[u] = P.op("scalar", "activation", out=g2, in_=y2, func=AF.Silu, sem=ES_S)
                P.wait("vector", ES_S, sil[u])
                P.wait("vector", rld[u % 2], P.cnt[rld[u % 2]])
                P.op("vector", "tensor_tensor", out=g2, in0=rt[u % 2], in1=g2, op=ALU.mult)
                P.selfbar("vector")
                if u >= 2:
                    P.wait("vector", ES_T, tr_d.get(u - 2, 0))
                ev_v[u] = P.op("vector", "tensor_tensor", out=xo[u % 2], in0=qt2[u % 2],
                               in1=g2, op=ALU.add, sem=ES_V)
                u += 1
                if u >= 2:
                    emit_p10_tail(P, ctx, u - 2, uinfo[u - 2], psT, xo, ev, out_d, stq,
                                  tr_d, cp_d, st_d, ev_v)
        emit_p10_tail(P, ctx, u - 1, uinfo[u - 1], psT, xo, ev, out_d, stq,
                      tr_d, cp_d, st_d, ev_v)
        P.barrier()


def emit_p10_tail(P, ctx, u, info, psT, xo, ev, out_d, stq, tr_d, cp_d, st_d, ev_v):
    idf = ctx["idf"]
    ES_T, ES_V, ES_S = P.esem["tensor"], P.esem["vector"], P.esem["scalar"]
    tt, dc = info
    P.wait("tensor", ES_V, ev_v[u])
    if u >= 2:
        P.wait("tensor", ES_S, cp_d.get(u - 2, 0))
    for c4 in range(4):
        P.op("tensor", "transpose", psT[u % 2][:, c4 * 128:(c4 + 1) * 128],
             xo[u % 2][:, c4 * 128:(c4 + 1) * 128], idf)
    tr_d[u] = P.einc("tensor")
    P.wait("scalar", ES_T, tr_d[u])
    if u >= 2:
        P.wait("scalar", stq[u % 2], st_d.get(u - 2, 0))
    cp_d[u] = P.op("scalar", "copy", out=ev[u % 2], in_=psT[u % 2], sem=ES_S)
    P.wait("gpsimd", ES_S, cp_d[u])
    st_d[u] = P.op("gpsimd", "dma_start",
                   out=out_d[tt * 512:(tt + 1) * 512, dc * 128:(dc + 1) * 128]
                   .rearrange("(c p) f -> p c f", p=128),
                   in_=ev[u % 2].rearrange("p (c f) -> p c f", c=4), sem=stq[u % 2])



PHASES = ["p0", "p1", "p2q", "p2k", "p2v", "p3", "p4",
          "p5a", "p5b", "p5c", "p5d", "p5e", "p6",
          "p7a", "p7g", "p7b", "p7c", "p8a", "p8b", "p9", "p10"]

_cache = {}


def _build():
    if "nc" not in _cache:
        P = Prog()
        ctx = emit_prep(P)
        for ph in PHASES:
            globals()["emit_" + ph](P, ctx)
        _cache["nc"] = P.build()
        _cache["P"] = P
    return _cache["P"], _cache["nc"]


def kernel(**inputs):
    P, nc = _build()
    x = np.ascontiguousarray(np.asarray(inputs["L0_attn_x"], np.float32))
    weights = {nm: np.ascontiguousarray(np.asarray(inputs[nm])) for nm in WEIGHT_SPECS
               if nm in P.drams}
    maps = []
    for c in range(NCORES):
        m = dict(weights)
        m["x_in"] = np.ascontiguousarray(x[c])
        maps.append(m)
    res = run_bass_kernel_spmd(nc, maps, core_ids=list(range(NCORES)))
    return np.stack([np.asarray(r["out"], np.float32) for r in res.results], axis=0)
